# version 04
# baseline (speedup 1.0000x reference)
import numpy as np
import ml_dtypes

import concourse.bass as bass
import concourse.mybir as mybir
from concourse.bass_utils import run_bass_kernel_spmd

F32 = mybir.dt.float32
BF16 = mybir.dt.bfloat16
AF = mybir.ActivationFunctionType
ALU = mybir.AluOpType

B, S, D = 2, 4096, 2048
NCORE = 8
TOK = (B * S) // NCORE
NT = TOK // 128
NH = 16
DK = 128
FC = D // 128
NE = 16
EDIM = 1024
LN_DKS = float(np.log(DK ** -0.5))
BIG = 1.0e9
EPS = 1e-5

ENGINES = ("sync", "gpsimd", "tensor", "vector", "scalar")
PSEM = {"sync": "ys", "gpsimd": "gs", "tensor": "ts", "vector": "vs", "scalar": "ss"}


_OP_RW = {
    "memset": ((0,), ()),
    "tensor_copy": ((0,), (1,)),
    "tensor_tensor": ((0,), (1, 2)),
    "tensor_add": ((0,), (1, 2)),
    "tensor_sub": ((0,), (1, 2)),
    "tensor_mul": ((0,), (1, 2)),
    "tensor_max": ((0,), (1, 2)),
    "tensor_scalar": ((0,), (1, 2, 3)),
    "tensor_single_scalar": ((0,), (1, 2)),
    "scalar_tensor_tensor": ((0,), (1, 2, 3)),
    "tensor_tensor_scan": ((0,), (1, 2, 3)),
    "reduce_max": ((0,), (1,)),
    "reduce_sum": ((0,), (1,)),
    "tensor_reduce": ((0,), (1,)),
    "reciprocal": ((0,), (1,)),
    "activation": ((0,), (1,)),
    "select": ((0,), (1, 2, 3)),
    "copy_predicated": ((0,), (1, 2)),
    "copy": ((0,), (1,)),
    "mul": ((0,), (1, 2)),
    "add": ((0,), (1, 2)),
}


def _ap_name(x):
    t = getattr(x, "tensor", None)
    return getattr(t, "name", None) if t is not None else None


class Prog:

    AUTO_SEM = {"vector": "var", "scalar": "sar"}

    def __init__(self, nc):
        self.nc = nc
        self.ops = {e: [] for e in ENGINES}
        self.semh = {}
        self.count = {}
        self.events = {}
        self.dma_sems = set()
        self.lastw = {e: {} for e in ENGINES}
        self.lastr = {e: {} for e in ENGINES}
        self.watermark = {e: -1 for e in ENGINES}

    def _sem(self, name):
        if name not in self.semh:
            self.semh[name] = self.nc.alloc_semaphore(name=name)
            self.count[name] = 0
        return self.semh[name]

    def op(self, eng, fn, *args, wait=(), inc=None, ev=None, dma=False, **kwargs):
        waits = [w for w in wait if w is not None]
        rec = [fn, args, kwargs, waits, inc, 16 if dma else 1, None]
        recs = self.ops[eng]
        if eng in ("vector", "scalar") and fn in _OP_RW:
            widx, ridx = _OP_RW[fn]
            reads = [_ap_name(args[i]) for i in ridx if i < len(args)]
            for kw in ("bias", "scale"):
                if kw in kwargs:
                    reads.append(_ap_name(kwargs[kw]))
            writes = [_ap_name(args[i]) for i in widx if i < len(args)]
            if kwargs.get("accum_out") is not None:
                writes.append(_ap_name(kwargs["accum_out"]))
            dep = -1
            for nm in reads:
                if nm is not None:
                    dep = max(dep, self.lastw[eng].get(nm, -1))
            for nm in writes:
                if nm is not None:
                    dep = max(dep, self.lastw[eng].get(nm, -1))
                    dep = max(dep, self.lastr[eng].get(nm, -1))
            if dep > self.watermark[eng]:
                rec[6] = dep
                self.watermark[eng] = dep
            myidx = len(recs)
            for nm in reads:
                if nm is not None:
                    self.lastr[eng][nm] = myidx
            for nm in writes:
                if nm is not None:
                    self.lastw[eng][nm] = myidx
        recs.append(rec)
        if inc is not None:
            self._sem(inc)
            if dma:
                self.dma_sems.add(inc)
            self.count[inc] += 16 if dma else 1
            if ev is not None:
                self.events[ev] = (inc, self.count[inc], eng)

    def barrier(self, name, engines=ENGINES):
        for e in engines:
            self.op(e, "sem_inc", inc=PSEM[e], ev=f"{name}.{e}")
        for e in engines:
            for o in engines:
                if o != e:
                    self.op(e, "wait_only", wait=(f"{name}.{o}",))

    def emit(self, block):
        auto_edge = {}
        for eng in ("vector", "scalar"):
            recs = self.ops[eng]
            need = {}
            for ci, rec in enumerate(recs):
                if rec[6] is not None:
                    need[rec[6]] = True
            counts = dict.fromkeys(self.count, 0)
            post = {}
            asem = self.AUTO_SEM[eng]
            self._sem(asem)
            counts[asem] = 0
            for i, rec in enumerate(recs):
                if i in need and rec[4] is None and rec[0] not in ("sem_inc", "wait_only"):
                    rec[4] = asem
                    rec[5] = 1
                if rec[4] is not None:
                    counts[rec[4]] = counts.get(rec[4], 0) + rec[5]
                post[i] = (rec[4], counts.get(rec[4], 0)) if rec[4] else None
            for ci, rec in enumerate(recs):
                if rec[6] is not None:
                    p = post[rec[6]]
                    assert p is not None, (eng, rec[6], recs[rec[6]][0])
                    auto_edge[(eng, ci)] = p

        for eng in ENGINES:
            recs = self.ops[eng]
            if not recs:
                continue

            def body(handle, recs=recs, eng=eng):
                waited = {}
                last_inst = None
                last_has_update = False
                for idx, (fn, args, kwargs, waits, inc, amt, dep) in enumerate(recs):
                    for evname in waits:
                        force = evname.startswith("!")
                        if force:
                            evname = evname[1:]
                        sem, val, prod = self.events[evname]
                        if prod == eng and sem not in self.dma_sems and not force:
                            continue
                        if waited.get(sem, 0) >= val:
                            continue
                        handle.wait_ge(self.semh[sem], val)
                        waited[sem] = val
                    ae = auto_edge.get((eng, idx))
                    if ae is not None and waited.get(ae[0], 0) < ae[1]:
                        handle.wait_ge(self.semh[ae[0]], ae[1])
                        waited[ae[0]] = ae[1]
                    if fn == "wait_only":
                        continue
                    if fn == "sem_inc":
                        if (eng in ("tensor", "vector", "scalar")
                                and last_inst is not None and not last_has_update):
                            last_inst.then_inc(self.semh[inc], amt)
                            last_has_update = True
                        elif eng in ("tensor", "vector", "scalar"):
                            handle.drain()
                            handle.sem_inc(self.semh[inc], amt)
                        else:
                            handle.sem_inc(self.semh[inc], amt)
                        continue
                    inst = getattr(handle, fn)(*args, **kwargs)
                    last_inst = inst
                    last_has_update = inc is not None
                    if inc is not None:
                        inst.then_inc(self.semh[inc], amt)

            getattr(block, eng)(body)


def build_program(stop=None, debug=False):
    nc = bass.Bass("TRN2", target_bir_lowering=False, detect_race_conditions=True)
    P = Prog(nc)

    d_x = nc.dram_tensor("x_tok", [TOK, D], F32, kind="ExternalInput").ap()
    d_xT = nc.dram_tensor("x_feat", [D, TOK], F32, kind="ExternalInput").ap()
    d_wq = nc.dram_tensor("wq", [D, D], F32, kind="ExternalInput").ap()
    d_wk = nc.dram_tensor("wk", [D, D], F32, kind="ExternalInput").ap()
    d_wv = nc.dram_tensor("wv", [D, D], F32, kind="ExternalInput").ap()
    d_wg = nc.dram_tensor("wgt", [D, D], F32, kind="ExternalInput").ap()
    d_wo = nc.dram_tensor("wo", [D, D], F32, kind="ExternalInput").ap()
    d_rwt = nc.dram_tensor("rwt", [128, FC * NE], F32, kind="ExternalInput").ap()
    d_bias = nc.dram_tensor("rbias", [128, NE], F32, kind="ExternalInput").ap()
    d_wu = nc.dram_tensor("wu", [NE, D, EDIM], F32, kind="ExternalInput").ap()
    d_wgf = nc.dram_tensor("wgf", [NE, D, EDIM], F32, kind="ExternalInput").ap()
    d_wd = nc.dram_tensor("wd", [NE, EDIM, D], F32, kind="ExternalInput").ap()
    d_su = nc.dram_tensor("su", [D, D], F32, kind="ExternalInput").ap()
    d_sg = nc.dram_tensor("sg", [D, D], F32, kind="ExternalInput").ap()
    d_sd = nc.dram_tensor("sd", [D, D], F32, kind="ExternalInput").ap()
    d_idb = nc.dram_tensor("identb", [128, 128], BF16, kind="ExternalInput").ap()
    d_idf = nc.dram_tensor("identf", [128, 128], F32, kind="ExternalInput").ap()
    d_um = nc.dram_tensor("utri", [128, 128], BF16, kind="ExternalInput").ap()
    d_fl = nc.dram_tensor("flags", [128, 4], F32, kind="ExternalInput").ap()
    d_fl1 = nc.dram_tensor("flags1m", [128, 4], F32, kind="ExternalInput").ap()
    d_out = nc.dram_tensor("out", [TOK, D], F32, kind="ExternalOutput").ap()
    if debug:
        d_dbg = {k: nc.dram_tensor(f"dbg_{k}", [128, D], F32, kind="ExternalOutput").ap()
                 for k in ("qt", "kh", "v", "oT", "S", "sin", "x2T", "cmb", "acc", "lgt")}
    d_x2 = nc.dram_tensor("x2_scr", [TOK, D], F32, kind="Internal").ap()
    d_ccin = nc.dram_tensor("cc_in", [128, NH * 128 + NH], BF16, kind="Internal").ap()
    d_ccout = nc.dram_tensor("cc_out", [4, 128, NH * 128 + NH], BF16, kind="Internal").ap()

    base = (nc.sbuf_base + 31) & ~31
    limit = nc.sbuf_top

    class Arena:
        def __init__(self, start):
            self.off = start
            self.hi = start

        def al(self, name, shape, dt):
            nbytes = int(np.prod(shape[1:])) * mybir.dt.size(dt)
            off = (self.off + 31) & ~31
            h = nc.alloc_sbuf_tensor_at(name, list(shape), dt, offset=base + off)
            self.off = off + nbytes
            self.hi = max(self.hi, self.off)
            assert base + self.off <= limit, (name, self.off, limit - base)
            return h

    CONST0 = 0
    OV1 = 2048
    OV2 = 34816
    SGRP = 100352
    OT0 = 114304
    OV3 = 147072
    ATMP = 179840
    DTMP = 100864

    ar = Arena(CONST0)
    c_idb = ar.al("c_idb", [128, 128], BF16)
    c_idf = ar.al("c_idf", [128, 128], F32)
    c_um = ar.al("c_um", [128, 128], BF16)
    c_fl = ar.al("c_fl", [128, 4], F32)
    c_fl1 = ar.al("c_fl1", [128, 4], F32)
    c_bias = ar.al("c_bias", [128, NE], F32)
    c_lnb = ar.al("c_lnb", [128, 1], F32)
    c_eps = ar.al("c_eps", [128, 1], F32)
    assert ar.hi <= OV1

    ar = Arena(OV1)
    s_qt = ar.al("s_qt", [128, NH * TOK], BF16)
    assert ar.hi <= OV2
    ar = Arena(OV1)
    s_x2T = ar.al("s_x2T", [128, FC * TOK], BF16)
    assert ar.hi <= OV2

    ar = Arena(OV2)
    s_kh = ar.al("s_kh", [128, NH * TOK], BF16)
    s_v = ar.al("s_v", [128, NT * D], BF16)
    assert ar.hi <= SGRP
    ar = Arena(OV2)
    s_wo = ar.al("s_wo", [128, FC * D], BF16)
    assert ar.hi <= SGRP + 512
    ar = Arena(OV2)
    s_acc = ar.al("s_acc", [128, NT * D], F32)
    assert ar.hi <= DTMP

    ar = Arena(SGRP)
    s_cmb = ar.al("s_cmb", [128, NT * NE], F32)
    CMB_END = ar.hi
    ar = Arena(SGRP + 512)
    s_Sb = ar.al("s_Sb", [128, NH * 128], BF16)
    s_ea = ar.al("s_ea", [128, NH * NT], F32)
    s_pfr = ar.al("s_pfr", [128, NH], F32)
    s_pfa = ar.al("s_pfa", [128, NH * NT], F32)
    assert ar.hi <= OT0

    ar = Arena(OT0)
    s_oT = ar.al("s_oT", [128, NH * TOK], BF16)
    assert ar.hi <= OV3

    ar = Arena(OV3)
    s_xT = ar.al("s_xT", [128, FC * TOK], BF16)
    assert ar.hi <= ATMP

    ar = Arena(OV3)
    s_t1b = ar.al("s_t1b", [128, 128], F32)
    s_pfb = ar.al("s_pfb", [128, NH], BF16)
    s_sin = ar.al("s_sin", [128, NH * 128], F32)
    s_sinb = ar.al("s_sinb", [128, NH * 128], BF16)
    s_rb = [ar.al(f"s_rb{i}", [128, NH * 128 + NH], BF16) for i in range(2)]
    s_t1 = ar.al("s_t1", [128, 128], F32)
    s_dp = ar.al("s_dp", [128, 1], F32)
    s_ptm = [ar.al(f"s_ptm{i}", [128, 128], BF16) for i in range(2)]
    s_khtm = [ar.al(f"s_khtm{i}", [128, 128], BF16) for i in range(2)]
    s_qp = [ar.al(f"s_qp{i}", [128, 128], BF16) for i in range(2)]
    assert ar.hi <= ATMP

    ar = Arena(OV3)
    s_xr = [ar.al(f"s_xr{i}", [128, D], F32) for i in range(2)]
    s_r = [ar.al(f"s_r{i}", [128, D], F32) for i in range(2)]
    s_x2 = [ar.al(f"s_x2{i}", [128, D], F32) for i in range(2)]
    s_sq = ar.al("s_sq", [128, D], F32)
    s_st4 = ar.al("s_st4", [128, 16], F32)
    s_rst = ar.al("s_rst", [128, 8], F32)
    s_rwt = ar.al("s_rwt", [128, FC * NE], BF16)
    s_lgt = ar.al("s_lgt", [128, NT * NE], F32)
    s_i1 = ar.al("s_i1", [128, NE], F32)
    s_lb = ar.al("s_lb", [128, NE], F32)
    s_ex = ar.al("s_ex", [128, NE], F32)
    assert base + ar.hi <= limit

    ar = Arena(ATMP)
    s_wr = [ar.al(f"s_wr{i}", [128, FC * 128], BF16) for i in range(2)]
    s_lq = [ar.al(f"s_lq{i}", [128, TOK], BF16) for i in range(2)]
    s_lk = [ar.al(f"s_lk{i}", [128, TOK], BF16) for i in range(2)]
    s_wv2 = [ar.al(f"s_wv2{i}", [128, FC * 256], BF16) for i in range(2)]
    assert base + ar.hi <= limit, ar.hi

    ar = Arena(DTMP)
    s_H = ar.al("s_H", [128, 8 * TOK], BF16)
    s_wring = [ar.al(f"s_wring{i}", [128, FC * 512], BF16) for i in range(3)]
    s_wd2 = [ar.al(f"s_wd2{i}", [128, 4 * D], BF16) for i in range(2)]
    s_su2 = [ar.al(f"s_su2{i}", [128, 512], F32) for i in range(2)]
    assert base + ar.hi <= limit, ar.hi

    ar = Arena(DTMP)
    s_x2r = [ar.al(f"s_x2r{i}", [128, D], F32) for i in range(2)]
    s_r2 = [ar.al(f"s_r2{i}", [128, D], F32) for i in range(2)]
    s_ob = [ar.al(f"s_ob{i}", [128, D], F32) for i in range(2)]
    s_sq2 = ar.al("s_sq2", [128, D], F32)
    s_st2 = ar.al("s_st2", [128, 16], F32)
    assert base + ar.hi <= limit

    pstack = []

    def pal(name, shape, dt):
        t = nc.psum_tensor(name, shape, dt)
        h = t.__enter__()
        pstack.append(t)
        return h

    def pfree(n):
        for _ in range(n):
            pstack.pop().__exit__(None, None, None)

    def _finish_early(dep_ev):
        P.op("sync", "dma_start", out=d_out[0:128, :], in_=s_sq[:] if False else s_xT[:, 0:D],
             wait=(dep_ev,), inc="out_st", dma=True, ev="early_st")
        P.op("sync", "wait_only", wait=("early_st",))
        with nc.Block() as blk:
            P.emit(blk)
        pfree(len(pstack))
        return nc

    P.op("sync", "dma_start", out=c_idb[:], in_=d_idb, inc="c_ld", dma=True)
    P.op("sync", "dma_start", out=c_idf[:], in_=d_idf, inc="c_ld", dma=True)
    P.op("sync", "dma_start", out=c_um[:], in_=d_um, inc="c_ld", dma=True)
    P.op("sync", "dma_start", out=c_fl[:], in_=d_fl, inc="c_ld", dma=True)
    P.op("sync", "dma_start", out=c_fl1[:], in_=d_fl1, inc="c_ld", dma=True)
    P.op("sync", "dma_start", out=c_bias[:], in_=d_bias, inc="c_ld", dma=True, ev="ld_const")
    P.op("vector", "memset", c_lnb[:], LN_DKS)
    P.op("vector", "memset", c_eps[:], EPS)
    P.op("vector", "memset", s_Sb[:], 0.0)
    P.op("vector", "memset", s_pfr[:], 1.0, inc="vs", ev="init_v")
    P.op("gpsimd", "dma_start",
         out=s_xT[:].rearrange("p (c t) -> p c t", c=FC),
         in_=d_xT.rearrange("(c p) t -> p c t", p=128),
         inc="xT_ld", dma=True, ev="ld_xT")

    p_q = pal("p_q", [128, TOK], F32)
    p_k = pal("p_k", [128, TOK], F32)
    p_z = pal("p_z", [128, TOK], F32)
    p_cu = pal("p_cu", [128, TOK], F32)

    wsrc = {0: d_wg, 1: d_wq, 2: d_wk}
    consume_ev = {0: "a_z", 1: "a_q", 2: "a_k"}

    for h in range(NH):
        for t3 in range(3):
            j = 3 * h + t3
            sl = j % 2
            w = []
            if j >= 2:
                pj = j - 2
                w.append(f"{consume_ev[pj % 3]}{pj // 3}")
            P.op("gpsimd", "dma_start",
                 out=s_wr[sl][:].rearrange("p (c d) -> p c d", c=FC),
                 in_=wsrc[t3][:, h * 128:(h + 1) * 128].rearrange("(c p) d -> p c d", p=128),
                 wait=tuple(w), inc=f"wr{j % 2}_ld", dma=True, ev=f"a_wl{j}")
        for (t3, pp, evn, freev) in (
            (0, p_z, f"a_z{h}", f"a_cu{h-1}"),
            (1, p_q, f"a_q{h}", f"a_qg{h-1}"),
            (2, p_k, f"a_k{h}", f"a_kg{h-1}"),
        ):
            j = 3 * h + t3
            sl = j % 2
            w = [f"a_wl{j}", "ld_xT"]
            if h >= 1:
                w.append(freev)
            for st in range(2):
                for fc in range(FC):
                    P.op("tensor", "matmul",
                         pp[:, st * 512:(st + 1) * 512],
                         s_wr[sl][:, fc * 128:(fc + 1) * 128],
                         s_xT[:, fc * TOK + st * 512: fc * TOK + (st + 1) * 512],
                         start=(fc == 0), stop=(fc == FC - 1),
                         wait=tuple(w) if (st == 0 and fc == 0) else ())
            P.op("tensor", "sem_inc", inc="ts", ev=evn)
        if stop == ("h", h, "mm"):
            return _finish_early(f"a_k{h}")
        P.op("scalar", "activation", p_cu[:], p_z[:], AF.Sigmoid,
             wait=(f"a_z{h}", "init_v") + ((f"a_ea{h-1}",) if h >= 1 else ()))
        P.op("scalar", "activation", p_z[:], p_cu[:], AF.Ln, inc="ss", ev=f"a_ln{h}")
        if stop == ("h", h, "ln"):
            return _finish_early(f"a_ln{h}")
        for c in range(NT):
            csl = slice(c * 128, (c + 1) * 128)
            P.op("vector", "tensor_tensor_scan", p_cu[:, csl], p_z[:, csl],
                 c_um[:], 0.0, ALU.add, ALU.bypass,
                 wait=((f"a_ln{h}",) + (("ld_const",) if h == 0 else ())) if c == 0 else ())
        P.op("vector", "sem_inc", inc="vs", ev=f"a_cu{h}")
        if stop == ("h", h, "cu"):
            return _finish_early(f"a_cu{h}")
        sl = h % 2
        P.op("scalar", "activation", s_lq[sl][:], p_cu[:], AF.Exp,
             bias=c_lnb[:, 0:1], scale=1.0 / 16.0,
             wait=(f"a_cu{h}",) + ((f"a_qg{h-2}",) if h >= 2 else ()),
             inc="ss", ev=f"a_lq{h}")
        P.op("scalar", "activation", s_lk[sl][:], p_cu[:], AF.Exp, scale=-1.0 / 16.0,
             wait=(f"a_kg{h-2}",) if h >= 2 else (),
             inc="ss", ev=f"a_lk{h}")
        for c in range(NT):
            P.op("scalar", "activation", s_ea[:, h * NT + c: h * NT + c + 1],
                 p_cu[:, (c + 1) * 128 - 1:(c + 1) * 128], AF.Exp, scale=1.0 / 16.0)
        P.op("scalar", "sem_inc", inc="ss", ev=f"a_ea{h}")
        if stop == ("h", h, "ea"):
            return _finish_early(f"a_ea{h}")
        P.op("vector", "tensor_mul", s_qt[:, h * TOK:(h + 1) * TOK], p_q[:], s_lq[sl][:],
             wait=(f"a_lq{h}", f"a_q{h}"), inc="vs", ev=f"a_qg{h}")
        P.op("vector", "tensor_mul", s_kh[:, h * TOK:(h + 1) * TOK], p_k[:], s_lk[sl][:],
             wait=(f"a_lk{h}", f"a_k{h}"), inc="vs", ev=f"a_kg{h}")
        if stop == ("h", h):
            return _finish_early(f"a_kg{h}")

    pfree(2)
    p_v = [pal(f"p_v{i}", [128, 256], F32) for i in range(2)]
    for dc in range(8):
        sl = dc % 2
        w = [f"a_cu{NH-1}", f"a_ea{NH-1}"]
        if dc >= 2:
            w.append(f"a_v{dc-2}_{NT-1}")
        P.op("gpsimd", "dma_start",
             out=s_wv2[sl][:].rearrange("p (c d) -> p c d", c=FC),
             in_=d_wv[:, dc * 256:(dc + 1) * 256].rearrange("(c p) d -> p c d", p=128),
             wait=tuple(w), inc=f"wv{sl}_ld", dma=True, ev=f"a_wvl{dc}")
        for tt in range(NT):
            pi = tt % 2
            j = dc * NT + tt
            w = [f"a_wvl{dc}"]
            if j >= 2:
                w.append(f"a_ve{(j-2)//NT}_{(j-2)%NT}")
            for fc in range(FC):
                P.op("tensor", "matmul", p_v[pi][:],
                     s_xT[:, fc * TOK + tt * 128: fc * TOK + (tt + 1) * 128],
                     s_wv2[sl][:, fc * 256:(fc + 1) * 256],
                     start=(fc == 0), stop=(fc == FC - 1),
                     wait=tuple(w) if fc == 0 else ())
            P.op("tensor", "sem_inc", inc="ts", ev=f"a_v{dc}_{tt}")
            P.op("vector", "tensor_copy",
                 s_v[:, tt * D + dc * 256: tt * D + (dc + 1) * 256],
                 p_v[pi][:], wait=(f"a_v{dc}_{tt}",), inc="vs", ev=f"a_ve{dc}_{tt}")

    if debug:
        P.op("gpsimd", "dma_start", out=d_dbg["qt"], in_=s_qt[:, 0:D], wait=("a_qg1",), inc="dbg_ld", dma=True)
        P.op("gpsimd", "dma_start", out=d_dbg["kh"], in_=s_kh[:, 0:D], wait=("a_kg1",), inc="dbg_ld", dma=True)
        P.op("gpsimd", "dma_start", out=d_dbg["v"], in_=s_v[:, 0:D], wait=(f"a_ve7_{NT-1}",), inc="dbg_ld", dma=True)
    P.barrier("endA")
    pfree(4)
    if stop == "A":
        return _finish_early("endA.vector")

    p_pt = [pal(f"p_pt{i}", [128, 128], F32) for i in range(2)]
    p_tr = [pal(f"p_tr{i}", [128, 128], BF16) for i in range(2)]
    p_ot = [pal(f"p_ot{i}", [128, 128], F32) for i in range(2)]
    p_kv = [pal(f"p_kv{i}", [128, 128], F32) for i in range(2)]

    for c in range(NT):
        for h in range(NH):
            i = c * NH + h
            p = i % 2
            qsl = s_qt[:, h * TOK + c * 128: h * TOK + (c + 1) * 128]
            ksl = s_kh[:, h * TOK + c * 128: h * TOK + (c + 1) * 128]
            vsl = s_v[:, c * D + h * 128: c * D + (h + 1) * 128]
            Sbsl = s_Sb[:, h * 128:(h + 1) * 128]
            osl = s_oT[:, h * TOK + c * 128: h * TOK + (c + 1) * 128]
            ea = s_ea[:, h * NT + c: h * NT + c + 1]
            w = ["endA.vector"] if i == 0 else ([f"b_ptm{i-2}"] if i >= 2 else [])
            P.op("tensor", "matmul", p_pt[p][:], ksl, qsl, start=True, stop=True,
                 wait=tuple(w), inc="ts", ev=f"b_pt{i}")
            w = ([f"b_khtm{i-2}"] if i >= 2 else []) + (["ld_const"] if i == 0 else [])
            P.op("tensor", "transpose", p_tr[p][:], ksl, c_idb[:],
                 wait=tuple(w), inc="ts", ev=f"b_tr{i}")
            w = [f"b_ptm{i}"]
            if i >= 2:
                w.append(f"b_oev{i-2}")
            if i >= NH:
                w.append(f"b_st{i-NH}")
            P.op("tensor", "matmul", p_ot[p][:], vsl, s_ptm[p][:], start=True, stop=False,
                 wait=tuple(w))
            P.op("tensor", "matmul", p_ot[p][:], Sbsl, qsl, start=False, stop=True,
                 inc="ts", ev=f"b_ot{i}")
            w = [f"b_khtm{i}"]
            if i >= 2:
                w.append(f"b_st{i-2}")
            P.op("tensor", "matmul", p_kv[p][:], s_khtm[p][:], vsl, start=True, stop=True,
                 wait=tuple(w), inc="ts", ev=f"b_kv{i}")
            P.op("vector", "tensor_mul", s_ptm[p][:], p_pt[p][:], c_um[:],
                 wait=(f"b_pt{i}",) + (("ld_const",) if i == 0 else ()),
                 inc="vs", ev=f"b_ptm{i}")
            P.op("scalar", "copy", s_khtm[p][:], p_tr[p][:],
                 wait=(f"b_tr{i}",), inc="ss", ev=f"b_khtm{i}")
            P.op("scalar", "copy", osl, p_ot[p][:],
                 wait=(f"b_ot{i}",), inc="ss", ev=f"b_oev{i}")
            P.op("vector", "tensor_add", s_t1b[:], Sbsl, p_kv[p][:], wait=(f"b_kv{i}",))
            P.op("scalar", "copy", s_pfa[:, h * NT + c: h * NT + c + 1],
                 s_pfr[:, h:h + 1])
            P.op("scalar", "mul", s_pfr[:, h:h + 1], s_pfr[:, h:h + 1], ea,
                 inc="ss", ev=f"b_pf{i}")
            P.op("vector", "tensor_scalar", Sbsl, s_t1b[:], ea, None, ALU.mult,
                 inc="vs", ev=f"b_st{i}")

    if stop == "B":
        return _finish_early(f"b_st{NT * NH - 1}")
    P.op("vector", "tensor_copy", s_pfb[:], s_pfr[:], wait=(f"b_pf{NT * NH - 1}",),
         inc="vs", ev="x_pfb")
    P.op("sync", "dma_start", out=d_ccin[:, :NH * 128], in_=s_Sb[:],
         wait=(f"b_st{NT * NH - 1}",), inc="ccin_ld", dma=True)
    P.op("sync", "dma_start", out=d_ccin[:, NH * 128:], in_=s_pfb[:],
         wait=("x_pfb",), inc="ccin_ld", dma=True, ev="x_ccin")
    P.op("gpsimd", "collective_compute", "AllGather", ALU.bypass,
         ins=[d_ccin], outs=[d_ccout],
         replica_groups=[[0, 1, 2, 3], [4, 5, 6, 7]],
         wait=("x_ccin",), inc="ccs", ev="x_cc")
    P.op("vector", "memset", s_sin[:], 0.0, wait=("endA.tensor",), inc="vs", ev="x_sin0")
    for r in range(4):
        w = ["x_cc"]
        if r >= 2:
            w.append(f"x_comb{r-2}")
        P.op("sync", "dma_start", out=s_rb[r % 2][:], in_=d_ccout[r],
             wait=tuple(w), inc=f"rb{r % 2}_ld", dma=True, ev=f"x_rb{r}")
        for h in range(NH):
            w = (f"x_rb{r}", "x_sin0", "ld_const") if h == 0 else ()
            sinh = s_sin[:, h * 128:(h + 1) * 128]
            P.op("vector", "tensor_scalar", s_dp[:],
                 s_rb[r % 2][:, NH * 128 + h: NH * 128 + h + 1],
                 c_fl[:, r:r + 1], c_fl1[:, r:r + 1], ALU.mult, ALU.add, wait=w)
            P.op("vector", "tensor_scalar", s_t1[:], sinh, s_dp[:, 0:1], None, ALU.mult)
            P.op("vector", "scalar_tensor_tensor", sinh,
                 s_rb[r % 2][:, h * 128:(h + 1) * 128], c_fl[:, r:r + 1], s_t1[:],
                 ALU.mult, ALU.add)
        P.op("vector", "sem_inc", inc="vs", ev=f"x_comb{r}")
    P.op("vector", "tensor_copy", s_sinb[:], s_sin[:], inc="vs", ev="x_sinb")

    if stop == "X":
        return _finish_early("x_sinb")
    pfree(8)
    p_om = [pal(f"p_om{i}", [128, 128], F32) for i in range(2)]
    j = 0
    for h in range(NH):
        for c in range(NT):
            p = j % 2
            qsl = s_qt[:, h * TOK + c * 128: h * TOK + (c + 1) * 128]
            osl = s_oT[:, h * TOK + c * 128: h * TOK + (c + 1) * 128]
            w = []
            if j == 0:
                w.append("x_sinb")
                w.append(f"b_pf{NT * NH - 1}")
            if j >= 2:
                w.append(f"a3_om{j-2}")
            P.op("vector", "tensor_scalar", s_qp[p][:], qsl,
                 s_pfa[:, h * NT + c: h * NT + c + 1], None, ALU.mult,
                 wait=tuple(w), inc="vs", ev=f"a3_qp{j}")
            w = [f"a3_qp{j}", "x_sinb"]
            if j >= 2:
                w.append(f"a3_oe{j-2}")
            P.op("tensor", "matmul", p_om[p][:], s_sinb[:, h * 128:(h + 1) * 128],
                 s_qp[p][:], start=True, stop=True, wait=tuple(w), inc="ts",
                 ev=f"a3_om{j}")
            P.op("vector", "tensor_add", osl, osl, p_om[p][:],
                 wait=(f"a3_om{j}",) + ((f"b_oev{NT * NH - 1}",) if j == 0 else ()),
                 inc="vs", ev=f"a3_oe{j}")
            j += 1

    if debug:
        P.op("gpsimd", "dma_start", out=d_dbg["S"][:, 0:NH * 128], in_=s_Sb[:], wait=(f"b_st{NT*NH-1}",), inc="dbg_ld", dma=True)
        P.op("sync", "dma_start", out=d_dbg["sin"][:, 0:NH * 128], in_=s_sin[:], wait=("x_sinb",), inc="dbg_ld", dma=True)
        P.op("gpsimd", "dma_start", out=d_dbg["oT"], in_=s_oT[:, 0:D], wait=(f"a3_oe{NH*NT-1}",), inc="dbg_ld", dma=True)
    P.barrier("endB")
    pfree(2)
    if stop == "A3":
        return _finish_early("endB.vector")

    p_w = [pal(f"p_w{i}", [128, 512], F32) for i in range(2)]
    p_t2 = [pal(f"p_t2{i}", [128, 128], F32) for i in range(2)]
    p_rt = [pal(f"p_rt{i}", [128, NE], F32) for i in range(2)]

    P.op("gpsimd", "dma_start",
         out=s_wo[:].rearrange("p (c d) -> p c d", c=FC),
         in_=d_wo.rearrange("(c p) d -> p c d", p=128),
         wait=("endB.tensor", "endB.vector", "endB.scalar", "endB.sync"),
         inc="wo_ld", dma=True, ev="c_wo_l")
    P.op("gpsimd", "dma_start", out=s_rwt[:], in_=d_rwt, inc="rwt_ld", dma=True, ev="c_rw_l")

    for tt in range(NT):
        bi = tt % 2
        sb4 = 8 * (tt % 2)
        w = ["endB.vector"]
        if tt >= 2:
            w.append(f"c_x2{tt-2}")
        P.op("sync", "dma_start", out=s_xr[bi][:], in_=d_x[tt * 128:(tt + 1) * 128, :],
             wait=tuple(w), inc=f"xr{bi}_ld", dma=True, ev=f"c_x{tt}")
        for fc in range(4):
            pj = fc % 2
            w = ["c_wo_l"]
            if tt * 4 + fc >= 2:
                pidx = tt * 4 + fc - 2
                w.append(f"c_ev{pidx//4}_{pidx%4}")
            for ec in range(NH):
                P.op("tensor", "matmul", p_w[pj][:],
                     s_oT[:, ec * TOK + tt * 128: ec * TOK + (tt + 1) * 128],
                     s_wo[:, ec * D + fc * 512: ec * D + (fc + 1) * 512],
                     start=(ec == 0), stop=(ec == NH - 1),
                     wait=tuple(w) if ec == 0 else ())
            P.op("tensor", "sem_inc", inc="ts", ev=f"c_mm{tt}_{fc}")
            wv = [f"c_mm{tt}_{fc}", f"c_x{tt}"]
            if tt >= 2:
                wv.append(f"c_std{tt-2}")
            P.op("vector", "scalar_tensor_tensor",
                 s_r[bi][:, fc * 512:(fc + 1) * 512], p_w[pj][:], 1.0,
                 s_xr[bi][:, fc * 512:(fc + 1) * 512], ALU.mult, ALU.add,
                 accum_out=s_st4[:, sb4 + fc: sb4 + fc + 1],
                 wait=tuple(wv), inc="vs", ev=f"c_ev{tt}_{fc}")
        P.op("vector", "reduce_sum", s_st4[:, sb4 + 4: sb4 + 5], s_st4[:, sb4: sb4 + 4],
             axis=mybir.AxisListType.X, wait=(f"!c_ev{tt}_3",), inc="vs", ev=f"c_rs{tt}")
        P.op("vector", "tensor_scalar", s_st4[:, sb4 + 5: sb4 + 6],
             s_st4[:, sb4 + 4: sb4 + 5], -1.0 / D, None,
             ALU.mult, wait=(f"!c_rs{tt}",), inc="vs", ev=f"c_nm{tt}")
        P.op("scalar", "activation", s_sq[:], s_r[bi][:], AF.Square,
             bias=s_st4[:, sb4 + 5: sb4 + 6], accum_out=s_st4[:, sb4 + 6: sb4 + 7],
             wait=(f"c_nm{tt}",), inc="ss", ev=f"c_vsm{tt}")
        P.op("scalar", "activation", s_st4[:, sb4 + 7: sb4 + 8],
             s_st4[:, sb4 + 6: sb4 + 7], AF.Sqrt,
             bias=c_eps[:, 0:1], scale=1.0 / D, wait=(f"!c_vsm{tt}",),
             inc="ss", ev=f"c_std{tt}")
        P.op("vector", "reciprocal", s_st4[:, sb4 + 6: sb4 + 7],
             s_st4[:, sb4 + 7: sb4 + 8], wait=(f"c_std{tt}",), inc="vs", ev=f"c_rc{tt}")
        P.op("vector", "tensor_scalar", s_st4[:, sb4 + 4: sb4 + 5],
             s_st4[:, sb4 + 5: sb4 + 6], -1.0, None, ALU.mult)
        wv = [f"!c_rc{tt}"]
        if tt >= 2:
            wv.append(f"c_sp{tt-2}")
            wv.append(f"c_te{tt-2}_{FC-1}")
        P.op("vector", "tensor_scalar", s_x2[bi][:], s_r[bi][:],
             s_st4[:, sb4 + 4: sb4 + 5],
             s_st4[:, sb4 + 6: sb4 + 7], ALU.subtract, ALU.mult, wait=tuple(wv),
             inc="vs", ev=f"c_x2{tt}")
        P.op("sync", "dma_start", out=d_x2[tt * 128:(tt + 1) * 128, :], in_=s_x2[bi][:],
             wait=(f"c_x2{tt}",), inc=f"sp{bi}_ld", dma=True, ev=f"c_sp{tt}")
        for fc in range(FC):
            pj = fc % 2
            w = [f"c_x2{tt}"]
            if tt == 0 and fc == 0:
                w.append("ld_const")
            if tt * FC + fc >= 2:
                pidx = tt * FC + fc - 2
                w.append(f"c_te{pidx//FC}_{pidx%FC}")
            P.op("tensor", "transpose", p_t2[pj][:],
                 s_x2[bi][:, fc * 128:(fc + 1) * 128], c_idf[:],
                 wait=tuple(w), inc="ts", ev=f"c_tr{tt}_{fc}")
            P.op("vector", "tensor_copy",
                 s_x2T[:, fc * TOK + tt * 128: fc * TOK + (tt + 1) * 128],
                 p_t2[pj][:], wait=(f"c_tr{tt}_{fc}",), inc="vs", ev=f"c_te{tt}_{fc}")

    for tt in range(NT):
        pj = tt % 2
        w = ["c_rw_l", f"c_te{tt}_{FC-1}"]
        if tt >= 2:
            w.append(f"c_lg{tt-2}")
        for fc in range(FC):
            P.op("tensor", "matmul", p_rt[pj][:],
                 s_x2T[:, fc * TOK + tt * 128: fc * TOK + (tt + 1) * 128],
                 s_rwt[:, fc * NE:(fc + 1) * NE],
                 start=(fc == 0), stop=(fc == FC - 1),
                 wait=tuple(w) if fc == 0 else ())
        P.op("tensor", "sem_inc", inc="ts", ev=f"c_rt{tt}")
        lg = s_lgt[:, tt * NE:(tt + 1) * NE]
        P.op("vector", "tensor_copy", lg, p_rt[pj][:], wait=(f"c_rt{tt}",),
             inc="vs", ev=f"c_lg{tt}")
        rb4 = 4 * (tt % 2)
        P.op("vector", "tensor_add", s_lb[:], lg, c_bias[:],
             wait=("ld_const",) if tt == 0 else ())
        P.op("vector", "reduce_max", s_rst[:, rb4: rb4 + 1], s_lb[:],
             axis=mybir.AxisListType.X, inc="vs", ev=f"c_m1{tt}")
        P.op("vector", "tensor_scalar", s_i1[:], s_lb[:], s_rst[:, rb4: rb4 + 1], None,
             ALU.is_ge, wait=(f"!c_m1{tt}",))
        P.op("vector", "scalar_tensor_tensor", s_i1[:], s_i1[:], -BIG, s_lb[:],
             ALU.mult, ALU.add)
        P.op("vector", "reduce_max", s_rst[:, rb4 + 1: rb4 + 2], s_i1[:],
             axis=mybir.AxisListType.X, inc="vs", ev=f"c_m2{tt}")
        P.op("vector", "tensor_scalar", s_i1[:], s_lb[:], s_rst[:, rb4 + 1: rb4 + 2], None,
             ALU.is_ge, wait=(f"!c_m2{tt}",))
        P.op("vector", "reduce_max", s_rst[:, rb4 + 2: rb4 + 3], lg, axis=mybir.AxisListType.X,
             negate=True, inc="vs", ev=f"c_pm{tt}")
        P.op("scalar", "activation", s_ex[:], lg, AF.Exp, bias=s_rst[:, rb4 + 2: rb4 + 3],
             wait=(f"c_pm{tt}",), inc="ss", ev=f"c_ex{tt}")
        P.op("vector", "scalar_tensor_tensor", s_ex[:], s_ex[:], 1.0, s_i1[:],
             ALU.bypass, ALU.mult, accum_out=s_rst[:, rb4 + 3: rb4 + 4],
             wait=(f"c_ex{tt}",), inc="vs", ev=f"c_cs{tt}")
        P.op("vector", "reciprocal", s_rst[:, rb4 + 3: rb4 + 4], s_rst[:, rb4 + 3: rb4 + 4],
             wait=(f"!c_cs{tt}",), inc="vs", ev=f"c_rc2{tt}")
        P.op("vector", "tensor_scalar", s_cmb[:, tt * NE:(tt + 1) * NE], s_ex[:],
             s_rst[:, rb4 + 3: rb4 + 4], None, ALU.mult, wait=(f"!c_rc2{tt}",),
             inc="vs", ev=f"c_cb{tt}")

    if debug:
        P.op("gpsimd", "dma_start", out=d_dbg["x2T"], in_=s_x2T[:, 0:D], wait=(f"c_te7_{FC-1}", f"c_te{NT-1}_{FC-1}"), inc="dbg_ld", dma=True)
        P.op("sync", "dma_start", out=d_dbg["cmb"][:, 0:NT * NE], in_=s_cmb[:], wait=(f"c_cb{NT-1}",), inc="dbg_ld", dma=True)
        P.op("sync", "dma_start", out=d_dbg["lgt"][:, 0:NT * NE], in_=s_lgt[:], wait=(f"c_cb{NT-1}",), inc="dbg_ld", dma=True)
    P.barrier("endC")
    pfree(6)
    if stop == "C":
        return _finish_early("endC.vector")

    p_u = [pal(f"p_u{i}", [128, 512], F32) for i in range(2)]
    p_g = [pal(f"p_g{i}", [128, 512], F32) for i in range(2)]
    p_d = [pal(f"p_d{i}", [128, 512], F32) for i in range(2)]

    P.op("vector", "memset", s_acc[:], 0.0, wait=("endC.tensor", "endC.scalar"),
         inc="vs", ev="d_acc0")

    NEX = NE + 2

    def exp_srcs(e):
        if e < NE:
            return d_wu[e], d_wgf[e], d_wd[e]
        half = e - NE
        return (d_su[:, half * EDIM:(half + 1) * EDIM],
                d_sg[:, half * EDIM:(half + 1) * EDIM],
                d_sd[half * EDIM:(half + 1) * EDIM, :])

    for e in range(NEX):
        wu_src, wg_src, wd_src = exp_srcs(e)
        for jh in range(2):
            for gg, src in ((0, wu_src), (1, wg_src)):
                k = e * 4 + jh * 2 + gg
                sl = k % 3
                w = []
                if k >= 3:
                    pk = k - 3
                    pe, rem = divmod(pk, 4)
                    pjh, pgg = divmod(rem, 2)
                    w.append(f"d_{'u' if pgg == 0 else 'g'}{pe}_{pjh}_1_3")
                elif e == 0 and k == 0:
                    w += ["endC.tensor", "endC.vector", "endC.scalar", "endC.sync",
                          f"c_sp{NT-2}", f"c_sp{NT-1}"]
                P.op("gpsimd", "dma_start",
                     out=s_wring[sl][:].rearrange("p (c j) -> p c j", c=FC),
                     in_=src[:, jh * 512:(jh + 1) * 512].rearrange("(c p) j -> p c j", p=128),
                     wait=tuple(w), inc=f"wring{sl}_ld", dma=True, ev=f"d_wl{k}")
        for dh in range(2):
            w = []
            if e >= 1:
                w.append(f"d_de{e-1}_{dh}_{31}")
            P.op("gpsimd", "dma_start",
                 out=s_wd2[dh][:].rearrange("p (c f) -> p c f", c=4),
                 in_=wd_src[dh * 512:(dh + 1) * 512, :].rearrange("(c p) f -> p c f", p=128),
                 wait=tuple(w), inc=f"wd{dh}_ld", dma=True, ev=f"d_wd{e}_{dh}")
        for jh in range(2):
            for st in range(2):
                for jl in range(4):
                    pj = jl % 2
                    idx = (jh * 2 + st) * 4 + jl
                    ku = e * 4 + jh * 2
                    kg = ku + 1
                    w = [f"d_wl{ku}", "d_acc0"]
                    if idx >= 2:
                        w.append(f"d_si{e}_{idx-2}")
                    elif e >= 1:
                        w.append(f"d_si{e-1}_{14 + idx}")
                    for fc in range(FC):
                        P.op("tensor", "matmul", p_u[pj][:],
                             s_wring[ku % 3][:, fc * 512 + jl * 128: fc * 512 + (jl + 1) * 128],
                             s_x2T[:, fc * TOK + st * 512: fc * TOK + (st + 1) * 512],
                             start=(fc == 0), stop=(fc == FC - 1),
                             wait=tuple(w) if fc == 0 else ())
                    P.op("tensor", "sem_inc", inc="ts", ev=f"d_u{e}_{jh}_{st}_{jl}")
                    w = [f"d_wl{kg}"]
                    if idx >= 2:
                        w.append(f"d_h{e}_{idx-2}")
                    elif e >= 1:
                        w.append(f"d_h{e-1}_{14 + idx}")
                    for fc in range(FC):
                        P.op("tensor", "matmul", p_g[pj][:],
                             s_wring[kg % 3][:, fc * 512 + jl * 128: fc * 512 + (jl + 1) * 128],
                             s_x2T[:, fc * TOK + st * 512: fc * TOK + (st + 1) * 512],
                             start=(fc == 0), stop=(fc == FC - 1),
                             wait=tuple(w) if fc == 0 else ())
                    P.op("tensor", "sem_inc", inc="ts", ev=f"d_g{e}_{jh}_{st}_{jl}")
                    w = [f"d_u{e}_{jh}_{st}_{jl}"]
                    if idx >= 2:
                        w.append(f"d_h{e}_{idx-2}")
                    elif e >= 1:
                        w.append(f"d_h{e-1}_{14 + idx}")
                    P.op("scalar", "activation", s_su2[pj][:], p_u[pj][:], AF.Silu,
                         wait=tuple(w), inc="ss", ev=f"d_si{e}_{idx}")
                    jc = jh * 4 + jl
                    P.op("vector", "tensor_mul",
                         s_H[:, jc * TOK + st * 512: jc * TOK + (st + 1) * 512],
                         s_su2[pj][:], p_g[pj][:],
                         wait=(f"d_si{e}_{idx}", f"d_g{e}_{jh}_{st}_{jl}"),
                         inc="vs", ev=f"d_h{e}_{idx}")
        for dh in range(2):
            for tsub in range(NT):
                for fch in range(4):
                    idx = tsub * 4 + fch
                    pj = idx % 2
                    w = [f"d_h{e}_15", f"d_wd{e}_{dh}"]
                    if idx >= 2:
                        w.append(f"d_dev{e}_{dh}_{idx-2}")
                    elif dh == 1:
                        w.append(f"d_dev{e}_{0}_{30 + idx}")
                    elif e >= 1:
                        w.append(f"d_dev{e-1}_{1}_{30 + idx}")
                    for jl in range(4):
                        jc = dh * 4 + jl
                        P.op("tensor", "matmul", p_d[pj][:],
                             s_H[:, jc * TOK + tsub * 128: jc * TOK + (tsub + 1) * 128],
                             s_wd2[dh][:, jl * D + fch * 512: jl * D + (fch + 1) * 512],
                             start=(jl == 0), stop=(jl == 3),
                             wait=tuple(w) if jl == 0 else ())
                    P.op("tensor", "sem_inc", inc="ts", ev=f"d_de{e}_{dh}_{idx}")
                    acc = s_acc[:, tsub * D + fch * 512: tsub * D + (fch + 1) * 512]
                    scal = s_cmb[:, tsub * NE + e: tsub * NE + e + 1] if e < NE else 1.0
                    wv = [f"d_de{e}_{dh}_{idx}"]
                    if e == 0 and dh == 0 and idx < 2:
                        wv.append("d_acc0")
                    P.op("vector", "scalar_tensor_tensor", acc, p_d[pj][:], scal, acc,
                         ALU.mult, ALU.add, wait=tuple(wv),
                         inc="vs", ev=f"d_dev{e}_{dh}_{idx}")

    if debug:
        P.op("sync", "dma_start", out=d_dbg["acc"], in_=s_acc[:, 0:D], wait=(f"d_dev{NE+1}_1_31",), inc="dbg_ld", dma=True)
    P.barrier("endD", engines=("sync", "tensor", "vector", "scalar"))
    pfree(6)
    if stop == "D":
        return _finish_early("endD.vector")

    for tt in range(NT):
        bi = tt % 2
        eb = 8 * (tt % 2)
        w = ["endD.vector", "endD.tensor", f"c_sp{tt}"]
        if tt >= 2:
            w.append(f"e_r2{tt-2}")
        P.op("sync", "dma_start", out=s_x2r[bi][:], in_=d_x2[tt * 128:(tt + 1) * 128, :],
             wait=tuple(w), inc=f"x2r{bi}_ld", dma=True, ev=f"e_x2{tt}")
        wv = [f"e_x2{tt}"]
        if tt >= 2:
            wv.append(f"e_ob{tt-2}")
        P.op("vector", "scalar_tensor_tensor", s_r2[bi][:],
             s_acc[:, tt * D:(tt + 1) * D], 1.0, s_x2r[bi][:], ALU.mult, ALU.add,
             accum_out=s_st2[:, eb: eb + 1], wait=tuple(wv), inc="vs", ev=f"e_r2{tt}")
        P.op("vector", "tensor_scalar", s_st2[:, eb + 1: eb + 2], s_st2[:, eb: eb + 1],
             -1.0 / D, None, ALU.mult, wait=(f"!e_r2{tt}",), inc="vs", ev=f"e_nm{tt}")
        P.op("scalar", "activation", s_sq2[:], s_r2[bi][:], AF.Square,
             bias=s_st2[:, eb + 1: eb + 2], accum_out=s_st2[:, eb + 2: eb + 3],
             wait=(f"e_nm{tt}", "endD.vector"), inc="ss", ev=f"e_vsm{tt}")
        P.op("scalar", "activation", s_st2[:, eb + 3: eb + 4], s_st2[:, eb + 2: eb + 3],
             AF.Sqrt, bias=c_eps[:, 0:1], scale=1.0 / D, wait=(f"!e_vsm{tt}",),
             inc="ss", ev=f"e_std{tt}")
        P.op("vector", "reciprocal", s_st2[:, eb + 2: eb + 3], s_st2[:, eb + 3: eb + 4],
             wait=(f"e_std{tt}",), inc="vs", ev=f"e_rc{tt}")
        P.op("vector", "tensor_scalar", s_st2[:, eb + 4: eb + 5], s_st2[:, eb + 1: eb + 2],
             -1.0, None, ALU.mult)
        P.op("vector", "tensor_scalar", s_ob[bi][:], s_r2[bi][:], s_st2[:, eb + 4: eb + 5],
             s_st2[:, eb + 2: eb + 3], ALU.subtract, ALU.mult,
             wait=(f"!e_rc{tt}",) + ((f"e_st{tt-2}",) if tt >= 2 else ()),
             inc="vs", ev=f"e_ob{tt}")
        P.op("sync", "dma_start", out=d_out[tt * 128:(tt + 1) * 128, :], in_=s_ob[bi][:],
             wait=(f"e_ob{tt}",), inc=f"st{bi}_st", dma=True, ev=f"e_st{tt}")

    P.op("sync", "wait_only", wait=(f"e_st{NT-2}", f"e_st{NT-1}"))

    with nc.Block() as block:
        P.emit(block)
    pfree(len(pstack))
    return nc


_NC_CACHE = None


def _get_nc():
    global _NC_CACHE
    if _NC_CACHE is None:
        _NC_CACHE = build_program()
    return _NC_CACHE


def make_in_maps(L0_attn_x, L0_attn_Wq, L0_attn_Wk, L0_attn_Wv, L0_attn_Wg, L0_attn_Wo,
                 L0_ffn_router_w, L0_ffn_router_bias, L0_ffn_Wu, L0_ffn_Wg, L0_ffn_Wd,
                 L0_ffn_Su, L0_ffn_Sg, L0_ffn_Sd):
    x = np.ascontiguousarray(np.asarray(L0_attn_x, dtype=np.float32)).reshape(B * S, D)
    wq = np.ascontiguousarray(np.asarray(L0_attn_Wq, dtype=np.float32))
    wk = np.ascontiguousarray(np.asarray(L0_attn_Wk, dtype=np.float32))
    wv = np.ascontiguousarray(np.asarray(L0_attn_Wv, dtype=np.float32))
    wg = np.ascontiguousarray(np.asarray(L0_attn_Wg, dtype=np.float32))
    wo = np.ascontiguousarray(np.asarray(L0_attn_Wo, dtype=np.float32))
    rw = np.ascontiguousarray(np.asarray(L0_ffn_router_w, dtype=np.float32))
    rb = np.ascontiguousarray(np.asarray(L0_ffn_router_bias, dtype=np.float32))
    wu = np.ascontiguousarray(np.asarray(L0_ffn_Wu, dtype=np.float32))
    wgf = np.ascontiguousarray(np.asarray(L0_ffn_Wg, dtype=np.float32))
    wd = np.ascontiguousarray(np.asarray(L0_ffn_Wd, dtype=np.float32))
    su = np.ascontiguousarray(np.asarray(L0_ffn_Su, dtype=np.float32).reshape(D, 2 * EDIM))
    sg = np.ascontiguousarray(np.asarray(L0_ffn_Sg, dtype=np.float32).reshape(D, 2 * EDIM))
    sd = np.ascontiguousarray(np.asarray(L0_ffn_Sd, dtype=np.float32).reshape(2 * EDIM, D))

    rwt = np.ascontiguousarray(
        np.transpose(rw.reshape(NE, FC, 128), (2, 1, 0)).reshape(128, FC * NE))
    biasb = np.ascontiguousarray(np.broadcast_to(rb, (128, NE)))
    identb = np.eye(128).astype(ml_dtypes.bfloat16)
    identf = np.eye(128, dtype=np.float32)
    utri = np.triu(np.ones((128, 128))).astype(ml_dtypes.bfloat16)

    in_maps = []
    for core in range(NCORE):
        xs = np.ascontiguousarray(x[core * TOK:(core + 1) * TOK])
        flags = np.zeros((128, 4), dtype=np.float32)
        flags[:, :core % 4] = 1.0
        in_maps.append({
            "x_tok": xs,
            "x_feat": np.ascontiguousarray(xs.T),
            "wq": wq, "wk": wk, "wv": wv, "wgt": wg, "wo": wo,
            "rwt": rwt, "rbias": biasb,
            "wu": wu, "wgf": wgf, "wd": wd,
            "su": su, "sg": sg, "sd": sd,
            "identb": identb, "identf": identf, "utri": utri,
            "flags": flags,
            "flags1m": np.ascontiguousarray(1.0 - flags),
        })
    return in_maps


def kernel(**inputs):
    in_maps = make_in_maps(**inputs)
    nc = _get_nc()
    res = run_bass_kernel_spmd(nc, in_maps, core_ids=list(range(NCORE)))
    out = np.concatenate([np.asarray(r["out"]) for r in res.results], axis=0)
    return out.reshape(B, S, D)


# version 06
# speedup vs baseline: 1.4156x; 1.4156x over previous
import numpy as np
import ml_dtypes

import concourse.bass as bass
import concourse.mybir as mybir
from concourse.bass_utils import run_bass_kernel_spmd

F32 = mybir.dt.float32
BF16 = mybir.dt.bfloat16
AF = mybir.ActivationFunctionType
ALU = mybir.AluOpType

B, S, D = 2, 4096, 2048
NCORE = 8
TOK = (B * S) // NCORE
NT = TOK // 128
NH = 16
DK = 128
FC = D // 128
NE = 16
EDIM = 1024
CAP = 256
LN_DKS = float(np.log(DK ** -0.5))
BIG = 1.0e9
EPS = 1e-5

ENGINES = ("sync", "gpsimd", "tensor", "vector", "scalar")
PSEM = {"sync": "ys", "gpsimd": "gs", "tensor": "ts", "vector": "vs", "scalar": "ss"}


_OP_RW = {
    "memset": ((0,), ()),
    "tensor_copy": ((0,), (1,)),
    "tensor_tensor": ((0,), (1, 2)),
    "tensor_add": ((0,), (1, 2)),
    "tensor_sub": ((0,), (1, 2)),
    "tensor_mul": ((0,), (1, 2)),
    "tensor_max": ((0,), (1, 2)),
    "tensor_scalar": ((0,), (1, 2, 3)),
    "tensor_single_scalar": ((0,), (1, 2)),
    "scalar_tensor_tensor": ((0,), (1, 2, 3)),
    "tensor_tensor_scan": ((0,), (1, 2, 3)),
    "reduce_max": ((0,), (1,)),
    "reduce_sum": ((0,), (1,)),
    "tensor_reduce": ((0,), (1,)),
    "reciprocal": ((0,), (1,)),
    "activation": ((0,), (1,)),
    "select": ((0,), (1, 2, 3)),
    "copy_predicated": ((0,), (1, 2)),
    "copy": ((0,), (1,)),
    "mul": ((0,), (1, 2)),
    "add": ((0,), (1, 2)),
}


def _ap_name(x):
    t = getattr(x, "tensor", None)
    return getattr(t, "name", None) if t is not None else None


class Prog:

    AUTO_SEM = {"vector": "var", "scalar": "sar"}

    def __init__(self, nc):
        self.nc = nc
        self.ops = {e: [] for e in ENGINES}
        self.semh = {}
        self.count = {}
        self.events = {}
        self.dma_sems = set()
        self.lastw = {e: {} for e in ENGINES}
        self.lastr = {e: {} for e in ENGINES}
        self.watermark = {e: -1 for e in ENGINES}

    def _sem(self, name):
        if name not in self.semh:
            self.semh[name] = self.nc.alloc_semaphore(name=name)
            self.count[name] = 0
        return self.semh[name]

    def op(self, eng, fn, *args, wait=(), inc=None, ev=None, dma=False, **kwargs):
        waits = [w for w in wait if w is not None]
        rec = [fn, args, kwargs, waits, inc, 16 if dma else 1, None]
        recs = self.ops[eng]
        if eng in ("vector", "scalar") and fn in _OP_RW:
            widx, ridx = _OP_RW[fn]
            reads = [_ap_name(args[i]) for i in ridx if i < len(args)]
            for kw in ("bias", "scale"):
                if kw in kwargs:
                    reads.append(_ap_name(kwargs[kw]))
            writes = [_ap_name(args[i]) for i in widx if i < len(args)]
            if kwargs.get("accum_out") is not None:
                writes.append(_ap_name(kwargs["accum_out"]))
            dep = -1
            for nm in reads:
                if nm is not None:
                    dep = max(dep, self.lastw[eng].get(nm, -1))
            for nm in writes:
                if nm is not None:
                    dep = max(dep, self.lastw[eng].get(nm, -1))
                    dep = max(dep, self.lastr[eng].get(nm, -1))
            if dep > self.watermark[eng]:
                rec[6] = dep
                self.watermark[eng] = dep
            myidx = len(recs)
            for nm in reads:
                if nm is not None:
                    self.lastr[eng][nm] = myidx
            for nm in writes:
                if nm is not None:
                    self.lastw[eng][nm] = myidx
        recs.append(rec)
        if inc is not None:
            self._sem(inc)
            if dma:
                self.dma_sems.add(inc)
            self.count[inc] += 16 if dma else 1
            if ev is not None:
                self.events[ev] = (inc, self.count[inc], eng)

    def barrier(self, name, engines=ENGINES):
        for e in engines:
            self.op(e, "sem_inc", inc=PSEM[e], ev=f"{name}.{e}")
        for e in engines:
            for o in engines:
                if o != e:
                    self.op(e, "wait_only", wait=(f"{name}.{o}",))

    def emit(self, block):
        auto_edge = {}
        for eng in ("vector", "scalar"):
            recs = self.ops[eng]
            need = {}
            for ci, rec in enumerate(recs):
                if rec[6] is not None:
                    need[rec[6]] = True
            counts = dict.fromkeys(self.count, 0)
            post = {}
            asem = self.AUTO_SEM[eng]
            self._sem(asem)
            counts[asem] = 0
            for i, rec in enumerate(recs):
                if i in need and rec[4] is None and rec[0] not in ("sem_inc", "wait_only"):
                    rec[4] = asem
                    rec[5] = 1
                if rec[4] is not None:
                    counts[rec[4]] = counts.get(rec[4], 0) + rec[5]
                post[i] = (rec[4], counts.get(rec[4], 0)) if rec[4] else None
            for ci, rec in enumerate(recs):
                if rec[6] is not None:
                    p = post[rec[6]]
                    assert p is not None, (eng, rec[6], recs[rec[6]][0])
                    auto_edge[(eng, ci)] = p

        for eng in ENGINES:
            recs = self.ops[eng]
            if not recs:
                continue

            def body(handle, recs=recs, eng=eng):
                waited = {}
                last_inst = None
                last_has_update = False
                for idx, (fn, args, kwargs, waits, inc, amt, dep) in enumerate(recs):
                    for evname in waits:
                        force = evname.startswith("!")
                        if force:
                            evname = evname[1:]
                        sem, val, prod = self.events[evname]
                        if prod == eng and sem not in self.dma_sems and not force:
                            continue
                        if waited.get(sem, 0) >= val:
                            continue
                        handle.wait_ge(self.semh[sem], val)
                        waited[sem] = val
                    ae = auto_edge.get((eng, idx))
                    if ae is not None and waited.get(ae[0], 0) < ae[1]:
                        handle.wait_ge(self.semh[ae[0]], ae[1])
                        waited[ae[0]] = ae[1]
                    if fn == "wait_only":
                        continue
                    if fn == "sem_inc":
                        if (eng in ("tensor", "vector", "scalar")
                                and last_inst is not None and not last_has_update):
                            last_inst.then_inc(self.semh[inc], amt)
                            last_has_update = True
                        elif eng in ("tensor", "vector", "scalar"):
                            handle.drain()
                            handle.sem_inc(self.semh[inc], amt)
                        else:
                            handle.sem_inc(self.semh[inc], amt)
                        continue
                    inst = getattr(handle, fn)(*args, **kwargs)
                    last_inst = inst
                    last_has_update = inc is not None
                    if inc is not None:
                        inst.then_inc(self.semh[inc], amt)

            getattr(block, eng)(body)


def build_program(stop=None, debug=False):
    nc = bass.Bass("TRN2", target_bir_lowering=False, detect_race_conditions=True)
    P = Prog(nc)

    d_x = nc.dram_tensor("x_tok", [TOK, D], F32, kind="ExternalInput").ap()
    d_xT = nc.dram_tensor("x_feat", [D, TOK], F32, kind="ExternalInput").ap()
    d_wq = nc.dram_tensor("wq", [D, D], F32, kind="ExternalInput").ap()
    d_wk = nc.dram_tensor("wk", [D, D], F32, kind="ExternalInput").ap()
    d_wv = nc.dram_tensor("wv", [D, D], F32, kind="ExternalInput").ap()
    d_wg = nc.dram_tensor("wgt", [D, D], F32, kind="ExternalInput").ap()
    d_wo = nc.dram_tensor("wo", [D, D], F32, kind="ExternalInput").ap()
    d_rwt = nc.dram_tensor("rwt", [128, FC * NE], F32, kind="ExternalInput").ap()
    d_bias = nc.dram_tensor("rbias", [128, NE], F32, kind="ExternalInput").ap()
    d_wu = nc.dram_tensor("wu", [NE, D, EDIM], F32, kind="ExternalInput").ap()
    d_wgf = nc.dram_tensor("wgf", [NE, D, EDIM], F32, kind="ExternalInput").ap()
    d_wd = nc.dram_tensor("wd", [NE, EDIM, D], F32, kind="ExternalInput").ap()
    d_su = nc.dram_tensor("su", [D, D], F32, kind="ExternalInput").ap()
    d_sg = nc.dram_tensor("sg", [D, D], F32, kind="ExternalInput").ap()
    d_sd = nc.dram_tensor("sd", [D, D], F32, kind="ExternalInput").ap()
    d_idb = nc.dram_tensor("identb", [128, 128], BF16, kind="ExternalInput").ap()
    d_iot = nc.dram_tensor("iotac", [128, CAP], BF16, kind="ExternalInput").ap()
    d_um = nc.dram_tensor("utri", [128, 128], BF16, kind="ExternalInput").ap()
    d_fl = nc.dram_tensor("flags", [128, 4], F32, kind="ExternalInput").ap()
    d_fl1 = nc.dram_tensor("flags1m", [128, 4], F32, kind="ExternalInput").ap()
    d_out = nc.dram_tensor("out", [TOK, D], F32, kind="ExternalOutput").ap()
    if debug:
        d_dbg = {k: nc.dram_tensor(f"dbg_{k}", [128, D], F32, kind="ExternalOutput").ap()
                 for k in ("qt", "kh", "v", "oT", "S", "sin", "x2T", "cmb", "acc", "lgt")}
    d_x2 = nc.dram_tensor("x2_scr", [TOK, D], BF16, kind="Internal").ap()
    d_ccin = nc.dram_tensor("cc_in", [128, NH * 128 + NH], BF16, kind="Internal").ap()
    d_ccout = nc.dram_tensor("cc_out", [4, 128, NH * 128 + NH], BF16, kind="Internal").ap()

    base = (nc.sbuf_base + 31) & ~31
    limit = nc.sbuf_top

    class Arena:
        def __init__(self, start):
            self.off = start
            self.hi = start

        def al(self, name, shape, dt):
            nbytes = int(np.prod(shape[1:])) * mybir.dt.size(dt)
            off = (self.off + 31) & ~31
            h = nc.alloc_sbuf_tensor_at(name, list(shape), dt, offset=base + off)
            self.off = off + nbytes
            self.hi = max(self.hi, self.off)
            assert base + self.off <= limit, (name, self.off, limit - base)
            return h

    CONST0 = 0
    OV1 = 2048
    OV2 = 34816
    SGRP = 100352
    OT0 = 114304
    OV3 = 147072
    ATMP = 179840
    DTMP = 100864

    ar = Arena(CONST0)
    c_idb = ar.al("c_idb", [128, 128], BF16)
    c_iot = ar.al("c_iot", [128, CAP], BF16)
    c_one = ar.al("c_one", [128, 128], BF16)
    c_um = ar.al("c_um", [128, 128], BF16)
    c_fl = ar.al("c_fl", [128, 4], F32)
    c_fl1 = ar.al("c_fl1", [128, 4], F32)
    c_bias = ar.al("c_bias", [128, NE], F32)
    c_lnb = ar.al("c_lnb", [128, 1], F32)
    c_eps = ar.al("c_eps", [128, 1], F32)
    assert ar.hi <= OV1

    ar = Arena(OV1)
    s_qt = ar.al("s_qt", [128, NH * TOK], BF16)
    assert ar.hi <= OV2
    ar = Arena(OV1)
    s_x2T = ar.al("s_x2T", [128, FC * TOK], BF16)
    assert ar.hi <= OV2

    ar = Arena(OV2)
    s_kh = ar.al("s_kh", [128, NH * TOK], BF16)
    s_v = ar.al("s_v", [128, NT * D], BF16)
    assert ar.hi <= SGRP
    ar = Arena(OV2)
    s_wo = ar.al("s_wo", [128, FC * D], BF16)
    assert ar.hi <= SGRP + 512
    ar = Arena(OV2)
    s_acc = ar.al("s_acc", [128, NT * D], BF16)
    s_x2tok = ar.al("s_x2tok", [128, NT * D], BF16)
    assert ar.hi <= SGRP

    ar = Arena(SGRP)
    s_cmb = ar.al("s_cmb", [128, NT * NE], F32)
    CMB_END = ar.hi
    ar = Arena(SGRP + 512)
    s_Sb = ar.al("s_Sb", [128, NH * 128], BF16)
    s_ea = ar.al("s_ea", [128, NH * NT], F32)
    s_pfr = ar.al("s_pfr", [128, NH], F32)
    s_pfa = ar.al("s_pfa", [128, NH * NT], F32)
    assert ar.hi <= OT0

    ar = Arena(OT0)
    s_oT = ar.al("s_oT", [128, NH * TOK], BF16)
    assert ar.hi <= OV3

    ar = Arena(OV3)
    s_xT = ar.al("s_xT", [128, FC * TOK], BF16)
    assert ar.hi <= ATMP

    ar = Arena(OV3)
    s_t1b = ar.al("s_t1b", [128, 128], F32)
    s_pfb = ar.al("s_pfb", [128, NH], BF16)
    s_sin = ar.al("s_sin", [128, NH * 128], F32)
    s_sinb = ar.al("s_sinb", [128, NH * 128], BF16)
    s_rb = [ar.al(f"s_rb{i}", [128, NH * 128 + NH], BF16) for i in range(2)]
    s_t1 = ar.al("s_t1", [128, 128], F32)
    s_dp = ar.al("s_dp", [128, 1], F32)
    s_ptm = [ar.al(f"s_ptm{i}", [128, 128], BF16) for i in range(2)]
    s_khtm = [ar.al(f"s_khtm{i}", [128, 128], BF16) for i in range(2)]
    s_qp = [ar.al(f"s_qp{i}", [128, 128], BF16) for i in range(2)]
    assert ar.hi <= ATMP

    ar = Arena(OV3)
    s_xr = [ar.al(f"s_xr{i}", [128, D], F32) for i in range(2)]
    s_r = [ar.al(f"s_r{i}", [128, D], F32) for i in range(2)]
    s_x2 = [ar.al(f"s_x2{i}", [128, D], BF16) for i in range(2)]
    s_sq = ar.al("s_sq", [128, D], F32)
    s_st4 = ar.al("s_st4", [128, 16], F32)
    s_rst = ar.al("s_rst", [128, 8], F32)
    s_rwt = ar.al("s_rwt", [128, FC * NE], BF16)
    s_lgt = ar.al("s_lgt", [128, NT * NE], F32)
    s_i1 = ar.al("s_i1", [128, NE], F32)
    s_lb = ar.al("s_lb", [128, NE], F32)
    s_ex = ar.al("s_ex", [128, NE], F32)
    assert base + ar.hi <= limit

    ar = Arena(ATMP)
    s_wr = [ar.al(f"s_wr{i}", [128, FC * 128], BF16) for i in range(2)]
    s_lq = [ar.al(f"s_lq{i}", [128, TOK], BF16) for i in range(2)]
    s_lk = [ar.al(f"s_lk{i}", [128, TOK], BF16) for i in range(2)]
    s_wv2 = [ar.al(f"s_wv2{i}", [128, FC * 256], BF16) for i in range(2)]
    assert base + ar.hi <= limit, ar.hi

    ar = Arena(DTMP)
    s_wring = [ar.al(f"s_wring{i}", [128, FC * 512], BF16) for i in range(3)]
    s_wd2 = [ar.al(f"s_wd2{i}", [128, 4 * D], BF16) for i in range(2)]
    SHOV = ar.hi
    ar = Arena(SHOV)
    s_Hsh = ar.al("s_Hsh", [128, 8 * TOK], BF16)
    SMALLS = ar.hi
    ar = Arena(SHOV)
    s_sel = ar.al("s_sel", [128, NT * CAP], BF16)
    s_selT = ar.al("s_selT", [128, 2 * TOK], BF16)
    s_xg = ar.al("s_xg", [128, FC * CAP], BF16)
    assert ar.hi <= SMALLS
    ar = Arena(SMALLS)
    s_su2 = [ar.al(f"s_su2{i}", [128, 512], F32) for i in range(2)]
    ar = Arena(SMALLS)
    s_deg2 = [ar.al(f"s_deg2{i}", [128, 2 * 512], BF16) for i in range(2)]
    s_hh = ar.al("s_hh", [128, 8 * CAP], BF16)
    s_su2s = [ar.al(f"s_su2s{i}", [128, CAP], F32) for i in range(2)]
    s_selc2 = [ar.al(f"s_selc2{i}", [128, CAP], BF16) for i in range(2)]
    s_pg = ar.al("s_pg", [128, NT * NE], F32)
    s_indf = ar.al("s_indf", [128, NT * NE], F32)
    s_indb = ar.al("s_indb", [128, NT * NE], BF16)
    assert base + ar.hi <= limit, ar.hi

    ar = Arena(DTMP)
    s_x2r = [ar.al(f"s_x2r{i}", [128, D], BF16) for i in range(2)]
    s_r2 = [ar.al(f"s_r2{i}", [128, D], F32) for i in range(2)]
    s_ob = [ar.al(f"s_ob{i}", [128, D], F32) for i in range(2)]
    s_sq2 = ar.al("s_sq2", [128, D], F32)
    s_st2 = ar.al("s_st2", [128, 16], F32)
    assert base + ar.hi <= limit

    pstack = []

    def pal(name, shape, dt):
        t = nc.psum_tensor(name, shape, dt)
        h = t.__enter__()
        pstack.append(t)
        return h

    def pfree(n):
        for _ in range(n):
            pstack.pop().__exit__(None, None, None)

    def _finish_early(dep_ev):
        P.op("sync", "dma_start", out=d_out[0:128, :], in_=s_sq[:] if False else s_xT[:, 0:D],
             wait=(dep_ev,), inc="out_st", dma=True, ev="early_st")
        P.op("sync", "wait_only", wait=("early_st",))
        with nc.Block() as blk:
            P.emit(blk)
        pfree(len(pstack))
        return nc

    P.op("sync", "dma_start", out=c_idb[:], in_=d_idb, inc="c_ld", dma=True)
    P.op("sync", "dma_start", out=c_iot[:], in_=d_iot, inc="c_ld", dma=True)
    P.op("sync", "dma_start", out=c_um[:], in_=d_um, inc="c_ld", dma=True)
    P.op("sync", "dma_start", out=c_fl[:], in_=d_fl, inc="c_ld", dma=True)
    P.op("sync", "dma_start", out=c_fl1[:], in_=d_fl1, inc="c_ld", dma=True)
    P.op("sync", "dma_start", out=c_bias[:], in_=d_bias, inc="c_ld", dma=True, ev="ld_const")
    P.op("vector", "memset", c_lnb[:], LN_DKS)
    P.op("vector", "memset", c_eps[:], EPS)
    P.op("vector", "memset", c_one[:], 1.0)
    P.op("vector", "memset", s_Sb[:], 0.0)
    P.op("vector", "memset", s_pfr[:], 1.0, inc="vs", ev="init_v")
    P.op("gpsimd", "dma_start",
         out=s_xT[:].rearrange("p (c t) -> p c t", c=FC),
         in_=d_xT.rearrange("(c p) t -> p c t", p=128),
         inc="xT_ld", dma=True, ev="ld_xT")

    p_q = pal("p_q", [128, TOK], F32)
    p_k = pal("p_k", [128, TOK], F32)
    p_z = pal("p_z", [128, TOK], F32)
    p_cu = pal("p_cu", [128, TOK], F32)

    wsrc = {0: d_wg, 1: d_wq, 2: d_wk}
    consume_ev = {0: "a_z", 1: "a_q", 2: "a_k"}

    for h in range(NH):
        for t3 in range(3):
            j = 3 * h + t3
            sl = j % 2
            w = []
            if j >= 2:
                pj = j - 2
                w.append(f"{consume_ev[pj % 3]}{pj // 3}")
            P.op("gpsimd", "dma_start",
                 out=s_wr[sl][:].rearrange("p (c d) -> p c d", c=FC),
                 in_=wsrc[t3][:, h * 128:(h + 1) * 128].rearrange("(c p) d -> p c d", p=128),
                 wait=tuple(w), inc=f"wr{j % 2}_ld", dma=True, ev=f"a_wl{j}")
        for (t3, pp, evn, freev) in (
            (0, p_z, f"a_z{h}", f"a_cu{h-1}"),
            (1, p_q, f"a_q{h}", f"a_qg{h-1}"),
            (2, p_k, f"a_k{h}", f"a_kg{h-1}"),
        ):
            j = 3 * h + t3
            sl = j % 2
            w = [f"a_wl{j}", "ld_xT"]
            if h >= 1:
                w.append(freev)
            for st in range(2):
                for fc in range(FC):
                    P.op("tensor", "matmul",
                         pp[:, st * 512:(st + 1) * 512],
                         s_wr[sl][:, fc * 128:(fc + 1) * 128],
                         s_xT[:, fc * TOK + st * 512: fc * TOK + (st + 1) * 512],
                         start=(fc == 0), stop=(fc == FC - 1),
                         wait=tuple(w) if (st == 0 and fc == 0) else ())
            P.op("tensor", "sem_inc", inc="ts", ev=evn)
        if stop == ("h", h, "mm"):
            return _finish_early(f"a_k{h}")
        P.op("scalar", "activation", p_cu[:], p_z[:], AF.Sigmoid,
             wait=(f"a_z{h}", "init_v") + ((f"a_ea{h-1}",) if h >= 1 else ()))
        P.op("scalar", "activation", p_z[:], p_cu[:], AF.Ln, inc="ss", ev=f"a_ln{h}")
        if stop == ("h", h, "ln"):
            return _finish_early(f"a_ln{h}")
        for c in range(NT):
            csl = slice(c * 128, (c + 1) * 128)
            P.op("vector", "tensor_tensor_scan", p_cu[:, csl], p_z[:, csl],
                 c_um[:], 0.0, ALU.add, ALU.bypass,
                 wait=((f"a_ln{h}",) + (("ld_const",) if h == 0 else ())) if c == 0 else ())
        P.op("vector", "sem_inc", inc="vs", ev=f"a_cu{h}")
        if stop == ("h", h, "cu"):
            return _finish_early(f"a_cu{h}")
        sl = h % 2
        P.op("scalar", "activation", s_lq[sl][:], p_cu[:], AF.Exp,
             bias=c_lnb[:, 0:1], scale=1.0 / 16.0,
             wait=(f"a_cu{h}",) + ((f"a_qg{h-2}",) if h >= 2 else ()),
             inc="ss", ev=f"a_lq{h}")
        P.op("scalar", "activation", s_lk[sl][:], p_cu[:], AF.Exp, scale=-1.0 / 16.0,
             wait=(f"a_kg{h-2}",) if h >= 2 else (),
             inc="ss", ev=f"a_lk{h}")
        for c in range(NT):
            P.op("scalar", "activation", s_ea[:, h * NT + c: h * NT + c + 1],
                 p_cu[:, (c + 1) * 128 - 1:(c + 1) * 128], AF.Exp, scale=1.0 / 16.0)
        P.op("scalar", "sem_inc", inc="ss", ev=f"a_ea{h}")
        if stop == ("h", h, "ea"):
            return _finish_early(f"a_ea{h}")
        P.op("vector", "tensor_mul", s_qt[:, h * TOK:(h + 1) * TOK], p_q[:], s_lq[sl][:],
             wait=(f"a_lq{h}", f"a_q{h}"), inc="vs", ev=f"a_qg{h}")
        P.op("vector", "tensor_mul", s_kh[:, h * TOK:(h + 1) * TOK], p_k[:], s_lk[sl][:],
             wait=(f"a_lk{h}", f"a_k{h}"), inc="vs", ev=f"a_kg{h}")
        if stop == ("h", h):
            return _finish_early(f"a_kg{h}")

    pfree(2)
    p_v = [pal(f"p_v{i}", [128, 256], F32) for i in range(2)]
    for dc in range(8):
        sl = dc % 2
        w = [f"a_cu{NH-1}", f"a_ea{NH-1}"]
        if dc >= 2:
            w.append(f"a_v{dc-2}_{NT-1}")
        P.op("gpsimd", "dma_start",
             out=s_wv2[sl][:].rearrange("p (c d) -> p c d", c=FC),
             in_=d_wv[:, dc * 256:(dc + 1) * 256].rearrange("(c p) d -> p c d", p=128),
             wait=tuple(w), inc=f"wv{sl}_ld", dma=True, ev=f"a_wvl{dc}")
        for tt in range(NT):
            pi = tt % 2
            j = dc * NT + tt
            w = [f"a_wvl{dc}"]
            if j >= 2:
                w.append(f"a_ve{(j-2)//NT}_{(j-2)%NT}")
            for fc in range(FC):
                P.op("tensor", "matmul", p_v[pi][:],
                     s_xT[:, fc * TOK + tt * 128: fc * TOK + (tt + 1) * 128],
                     s_wv2[sl][:, fc * 256:(fc + 1) * 256],
                     start=(fc == 0), stop=(fc == FC - 1),
                     wait=tuple(w) if fc == 0 else ())
            P.op("tensor", "sem_inc", inc="ts", ev=f"a_v{dc}_{tt}")
            P.op("vector", "tensor_copy",
                 s_v[:, tt * D + dc * 256: tt * D + (dc + 1) * 256],
                 p_v[pi][:], wait=(f"a_v{dc}_{tt}",), inc="vs", ev=f"a_ve{dc}_{tt}")

    if debug:
        P.op("gpsimd", "dma_start", out=d_dbg["qt"], in_=s_qt[:, 0:D], wait=("a_qg1",), inc="dbg_ld", dma=True)
        P.op("gpsimd", "dma_start", out=d_dbg["kh"], in_=s_kh[:, 0:D], wait=("a_kg1",), inc="dbg_ld", dma=True)
        P.op("gpsimd", "dma_start", out=d_dbg["v"], in_=s_v[:, 0:D], wait=(f"a_ve7_{NT-1}",), inc="dbg_ld", dma=True)
    P.barrier("endA")
    pfree(4)
    if stop == "A":
        return _finish_early("endA.vector")

    p_pt = [pal(f"p_pt{i}", [128, 128], F32) for i in range(2)]
    p_tr = [pal(f"p_tr{i}", [128, 128], BF16) for i in range(2)]
    p_ot = [pal(f"p_ot{i}", [128, 128], F32) for i in range(2)]
    p_kv = [pal(f"p_kv{i}", [128, 128], F32) for i in range(2)]

    for c in range(NT):
        for h in range(NH):
            i = c * NH + h
            p = i % 2
            qsl = s_qt[:, h * TOK + c * 128: h * TOK + (c + 1) * 128]
            ksl = s_kh[:, h * TOK + c * 128: h * TOK + (c + 1) * 128]
            vsl = s_v[:, c * D + h * 128: c * D + (h + 1) * 128]
            Sbsl = s_Sb[:, h * 128:(h + 1) * 128]
            osl = s_oT[:, h * TOK + c * 128: h * TOK + (c + 1) * 128]
            ea = s_ea[:, h * NT + c: h * NT + c + 1]
            w = ["endA.vector"] if i == 0 else ([f"b_ptm{i-2}"] if i >= 2 else [])
            P.op("tensor", "matmul", p_pt[p][:], ksl, qsl, start=True, stop=True,
                 wait=tuple(w), inc="ts", ev=f"b_pt{i}")
            w = ([f"b_khtm{i-2}"] if i >= 2 else []) + (["ld_const"] if i == 0 else [])
            P.op("tensor", "transpose", p_tr[p][:], ksl, c_idb[:],
                 wait=tuple(w), inc="ts", ev=f"b_tr{i}")
            w = [f"b_ptm{i}"]
            if i >= 2:
                w.append(f"b_oev{i-2}")
            if i >= NH:
                w.append(f"b_st{i-NH}")
            P.op("tensor", "matmul", p_ot[p][:], vsl, s_ptm[p][:], start=True, stop=False,
                 wait=tuple(w))
            P.op("tensor", "matmul", p_ot[p][:], Sbsl, qsl, start=False, stop=True,
                 inc="ts", ev=f"b_ot{i}")
            w = [f"b_khtm{i}"]
            if i >= 2:
                w.append(f"b_st{i-2}")
            P.op("tensor", "matmul", p_kv[p][:], s_khtm[p][:], vsl, start=True, stop=True,
                 wait=tuple(w), inc="ts", ev=f"b_kv{i}")
            P.op("vector", "tensor_mul", s_ptm[p][:], p_pt[p][:], c_um[:],
                 wait=(f"b_pt{i}",) + (("ld_const",) if i == 0 else ()),
                 inc="vs", ev=f"b_ptm{i}")
            P.op("scalar", "copy", s_khtm[p][:], p_tr[p][:],
                 wait=(f"b_tr{i}",), inc="ss", ev=f"b_khtm{i}")
            P.op("scalar", "copy", osl, p_ot[p][:],
                 wait=(f"b_ot{i}",), inc="ss", ev=f"b_oev{i}")
            P.op("vector", "tensor_add", s_t1b[:], Sbsl, p_kv[p][:], wait=(f"b_kv{i}",))
            P.op("scalar", "copy", s_pfa[:, h * NT + c: h * NT + c + 1],
                 s_pfr[:, h:h + 1])
            P.op("scalar", "mul", s_pfr[:, h:h + 1], s_pfr[:, h:h + 1], ea,
                 inc="ss", ev=f"b_pf{i}")
            P.op("vector", "tensor_scalar", Sbsl, s_t1b[:], ea, None, ALU.mult,
                 inc="vs", ev=f"b_st{i}")

    if stop == "B":
        return _finish_early(f"b_st{NT * NH - 1}")
    P.op("vector", "tensor_copy", s_pfb[:], s_pfr[:], wait=(f"b_pf{NT * NH - 1}",),
         inc="vs", ev="x_pfb")
    P.op("sync", "dma_start", out=d_ccin[:, :NH * 128], in_=s_Sb[:],
         wait=(f"b_st{NT * NH - 1}",), inc="ccin_ld", dma=True)
    P.op("sync", "dma_start", out=d_ccin[:, NH * 128:], in_=s_pfb[:],
         wait=("x_pfb",), inc="ccin_ld", dma=True, ev="x_ccin")
    P.op("gpsimd", "collective_compute", "AllGather", ALU.bypass,
         ins=[d_ccin], outs=[d_ccout],
         replica_groups=[[0, 1, 2, 3], [4, 5, 6, 7]],
         wait=("x_ccin",), inc="ccs", ev="x_cc")
    P.op("vector", "memset", s_sin[:], 0.0, wait=("endA.tensor",), inc="vs", ev="x_sin0")
    for r in range(4):
        w = ["x_cc"]
        if r >= 2:
            w.append(f"x_comb{r-2}")
        P.op("sync", "dma_start", out=s_rb[r % 2][:], in_=d_ccout[r],
             wait=tuple(w), inc=f"rb{r % 2}_ld", dma=True, ev=f"x_rb{r}")
        for h in range(NH):
            w = (f"x_rb{r}", "x_sin0", "ld_const") if h == 0 else ()
            sinh = s_sin[:, h * 128:(h + 1) * 128]
            P.op("vector", "tensor_scalar", s_dp[:],
                 s_rb[r % 2][:, NH * 128 + h: NH * 128 + h + 1],
                 c_fl[:, r:r + 1], c_fl1[:, r:r + 1], ALU.mult, ALU.add, wait=w)
            P.op("vector", "tensor_scalar", s_t1[:], sinh, s_dp[:, 0:1], None, ALU.mult)
            P.op("vector", "scalar_tensor_tensor", sinh,
                 s_rb[r % 2][:, h * 128:(h + 1) * 128], c_fl[:, r:r + 1], s_t1[:],
                 ALU.mult, ALU.add)
        P.op("vector", "sem_inc", inc="vs", ev=f"x_comb{r}")
    P.op("vector", "tensor_copy", s_sinb[:], s_sin[:], inc="vs", ev="x_sinb")

    if stop == "X":
        return _finish_early("x_sinb")
    pfree(8)
    p_om = [pal(f"p_om{i}", [128, 128], F32) for i in range(2)]
    j = 0
    for h in range(NH):
        for c in range(NT):
            p = j % 2
            qsl = s_qt[:, h * TOK + c * 128: h * TOK + (c + 1) * 128]
            osl = s_oT[:, h * TOK + c * 128: h * TOK + (c + 1) * 128]
            w = []
            if j == 0:
                w.append("x_sinb")
                w.append(f"b_pf{NT * NH - 1}")
            if j >= 2:
                w.append(f"a3_om{j-2}")
            P.op("vector", "tensor_scalar", s_qp[p][:], qsl,
                 s_pfa[:, h * NT + c: h * NT + c + 1], None, ALU.mult,
                 wait=tuple(w), inc="vs", ev=f"a3_qp{j}")
            w = [f"a3_qp{j}", "x_sinb"]
            if j >= 2:
                w.append(f"a3_oe{j-2}")
            P.op("tensor", "matmul", p_om[p][:], s_sinb[:, h * 128:(h + 1) * 128],
                 s_qp[p][:], start=True, stop=True, wait=tuple(w), inc="ts",
                 ev=f"a3_om{j}")
            P.op("vector", "tensor_add", osl, osl, p_om[p][:],
                 wait=(f"a3_om{j}",) + ((f"b_oev{NT * NH - 1}",) if j == 0 else ()),
                 inc="vs", ev=f"a3_oe{j}")
            j += 1

    if debug:
        P.op("gpsimd", "dma_start", out=d_dbg["S"][:, 0:NH * 128], in_=s_Sb[:], wait=(f"b_st{NT*NH-1}",), inc="dbg_ld", dma=True)
        P.op("sync", "dma_start", out=d_dbg["sin"][:, 0:NH * 128], in_=s_sin[:], wait=("x_sinb",), inc="dbg_ld", dma=True)
        P.op("gpsimd", "dma_start", out=d_dbg["oT"], in_=s_oT[:, 0:D], wait=(f"a3_oe{NH*NT-1}",), inc="dbg_ld", dma=True)
    P.barrier("endB")
    pfree(2)
    if stop == "A3":
        return _finish_early("endB.vector")

    p_w = [pal(f"p_w{i}", [128, 512], F32) for i in range(2)]
    p_t2 = [pal(f"p_t2{i}", [128, 128], BF16) for i in range(2)]
    p_rt = [pal(f"p_rt{i}", [128, NE], F32) for i in range(2)]
    p_pg = [pal(f"p_pg{i}", [128, NE], F32) for i in range(2)]

    P.op("gpsimd", "dma_start",
         out=s_wo[:].rearrange("p (c d) -> p c d", c=FC),
         in_=d_wo.rearrange("(c p) d -> p c d", p=128),
         wait=("endB.tensor", "endB.vector", "endB.scalar", "endB.sync"),
         inc="wo_ld", dma=True, ev="c_wo_l")
    P.op("gpsimd", "dma_start", out=s_rwt[:], in_=d_rwt, inc="rwt_ld", dma=True, ev="c_rw_l")

    for tt in range(NT):
        bi = tt % 2
        sb4 = 8 * (tt % 2)
        w = ["endB.vector"]
        if tt >= 2:
            w.append(f"c_x2{tt-2}")
        P.op("sync", "dma_start", out=s_xr[bi][:], in_=d_x[tt * 128:(tt + 1) * 128, :],
             wait=tuple(w), inc=f"xr{bi}_ld", dma=True, ev=f"c_x{tt}")
        for fc in range(4):
            pj = fc % 2
            w = ["c_wo_l"]
            if tt * 4 + fc >= 2:
                pidx = tt * 4 + fc - 2
                w.append(f"c_ev{pidx//4}_{pidx%4}")
            for ec in range(NH):
                P.op("tensor", "matmul", p_w[pj][:],
                     s_oT[:, ec * TOK + tt * 128: ec * TOK + (tt + 1) * 128],
                     s_wo[:, ec * D + fc * 512: ec * D + (fc + 1) * 512],
                     start=(ec == 0), stop=(ec == NH - 1),
                     wait=tuple(w) if ec == 0 else ())
            P.op("tensor", "sem_inc", inc="ts", ev=f"c_mm{tt}_{fc}")
            wv = [f"c_mm{tt}_{fc}", f"c_x{tt}"]
            if tt >= 2:
                wv.append(f"c_std{tt-2}")
            P.op("vector", "scalar_tensor_tensor",
                 s_r[bi][:, fc * 512:(fc + 1) * 512], p_w[pj][:], 1.0,
                 s_xr[bi][:, fc * 512:(fc + 1) * 512], ALU.mult, ALU.add,
                 accum_out=s_st4[:, sb4 + fc: sb4 + fc + 1],
                 wait=tuple(wv), inc="vs", ev=f"c_ev{tt}_{fc}")
        P.op("vector", "reduce_sum", s_st4[:, sb4 + 4: sb4 + 5], s_st4[:, sb4: sb4 + 4],
             axis=mybir.AxisListType.X, wait=(f"!c_ev{tt}_3",), inc="vs", ev=f"c_rs{tt}")
        P.op("vector", "tensor_scalar", s_st4[:, sb4 + 5: sb4 + 6],
             s_st4[:, sb4 + 4: sb4 + 5], -1.0 / D, None,
             ALU.mult, wait=(f"!c_rs{tt}",), inc="vs", ev=f"c_nm{tt}")
        P.op("scalar", "activation", s_sq[:], s_r[bi][:], AF.Square,
             bias=s_st4[:, sb4 + 5: sb4 + 6], accum_out=s_st4[:, sb4 + 6: sb4 + 7],
             wait=(f"c_nm{tt}",), inc="ss", ev=f"c_vsm{tt}")
        P.op("scalar", "activation", s_st4[:, sb4 + 7: sb4 + 8],
             s_st4[:, sb4 + 6: sb4 + 7], AF.Sqrt,
             bias=c_eps[:, 0:1], scale=1.0 / D, wait=(f"!c_vsm{tt}",),
             inc="ss", ev=f"c_std{tt}")
        P.op("vector", "reciprocal", s_st4[:, sb4 + 6: sb4 + 7],
             s_st4[:, sb4 + 7: sb4 + 8], wait=(f"c_std{tt}",), inc="vs", ev=f"c_rc{tt}")
        P.op("vector", "tensor_scalar", s_st4[:, sb4 + 4: sb4 + 5],
             s_st4[:, sb4 + 5: sb4 + 6], -1.0, None, ALU.mult)
        wv = [f"!c_rc{tt}"]
        if tt >= 2:
            wv.append(f"c_sp{tt-2}")
            wv.append(f"c_te{tt-2}_{FC-1}")
        P.op("vector", "tensor_scalar", s_x2[bi][:], s_r[bi][:],
             s_st4[:, sb4 + 4: sb4 + 5],
             s_st4[:, sb4 + 6: sb4 + 7], ALU.subtract, ALU.mult, wait=tuple(wv),
             inc="vs", ev=f"c_x2{tt}")
        P.op("sync", "dma_start", out=d_x2[tt * 128:(tt + 1) * 128, :], in_=s_x2[bi][:],
             wait=(f"c_x2{tt}",), inc=f"sp{bi}_ld", dma=True, ev=f"c_sp{tt}")
        for fc in range(FC):
            pj = fc % 2
            w = [f"c_x2{tt}"]
            if tt == 0 and fc == 0:
                w.append("ld_const")
            if tt * FC + fc >= 2:
                pidx = tt * FC + fc - 2
                w.append(f"c_te{pidx//FC}_{pidx%FC}")
            P.op("tensor", "transpose", p_t2[pj][:],
                 s_x2[bi][:, fc * 128:(fc + 1) * 128], c_idb[:],
                 wait=tuple(w), inc="ts", ev=f"c_tr{tt}_{fc}")
            P.op("vector", "tensor_copy",
                 s_x2T[:, fc * TOK + tt * 128: fc * TOK + (tt + 1) * 128],
                 p_t2[pj][:], wait=(f"c_tr{tt}_{fc}",), inc="vs", ev=f"c_te{tt}_{fc}")

    for tt in range(NT):
        pj = tt % 2
        w = ["c_rw_l", f"c_te{tt}_{FC-1}"]
        if tt >= 2:
            w.append(f"c_lg{tt-2}")
        for fc in range(FC):
            P.op("tensor", "matmul", p_rt[pj][:],
                 s_x2T[:, fc * TOK + tt * 128: fc * TOK + (tt + 1) * 128],
                 s_rwt[:, fc * NE:(fc + 1) * NE],
                 start=(fc == 0), stop=(fc == FC - 1),
                 wait=tuple(w) if fc == 0 else ())
        P.op("tensor", "sem_inc", inc="ts", ev=f"c_rt{tt}")
        lg = s_lgt[:, tt * NE:(tt + 1) * NE]
        P.op("vector", "tensor_copy", lg, p_rt[pj][:], wait=(f"c_rt{tt}",),
             inc="vs", ev=f"c_lg{tt}")
        rb4 = 4 * (tt % 2)
        P.op("vector", "tensor_add", s_lb[:], lg, c_bias[:],
             wait=("ld_const",) if tt == 0 else ())
        P.op("vector", "reduce_max", s_rst[:, rb4: rb4 + 1], s_lb[:],
             axis=mybir.AxisListType.X, inc="vs", ev=f"c_m1{tt}")
        P.op("vector", "tensor_scalar", s_i1[:], s_lb[:], s_rst[:, rb4: rb4 + 1], None,
             ALU.is_ge, wait=(f"!c_m1{tt}",))
        P.op("vector", "scalar_tensor_tensor", s_i1[:], s_i1[:], -BIG, s_lb[:],
             ALU.mult, ALU.add)
        P.op("vector", "reduce_max", s_rst[:, rb4 + 1: rb4 + 2], s_i1[:],
             axis=mybir.AxisListType.X, inc="vs", ev=f"c_m2{tt}")
        P.op("vector", "tensor_scalar", s_i1[:], s_lb[:], s_rst[:, rb4 + 1: rb4 + 2], None,
             ALU.is_ge, wait=(f"!c_m2{tt}",))
        P.op("vector", "reduce_max", s_rst[:, rb4 + 2: rb4 + 3], lg, axis=mybir.AxisListType.X,
             negate=True, inc="vs", ev=f"c_pm{tt}")
        P.op("scalar", "activation", s_ex[:], lg, AF.Exp, bias=s_rst[:, rb4 + 2: rb4 + 3],
             wait=(f"c_pm{tt}",), inc="ss", ev=f"c_ex{tt}")
        P.op("vector", "scalar_tensor_tensor", s_ex[:], s_ex[:], 1.0, s_i1[:],
             ALU.bypass, ALU.mult, accum_out=s_rst[:, rb4 + 3: rb4 + 4],
             wait=(f"c_ex{tt}",), inc="vs", ev=f"c_cs{tt}")
        P.op("vector", "reciprocal", s_rst[:, rb4 + 3: rb4 + 4], s_rst[:, rb4 + 3: rb4 + 4],
             wait=(f"!c_cs{tt}",), inc="vs", ev=f"c_rc2{tt}")
        P.op("vector", "tensor_scalar", s_cmb[:, tt * NE:(tt + 1) * NE], s_ex[:],
             s_rst[:, rb4 + 3: rb4 + 4], None, ALU.mult, wait=(f"!c_rc2{tt}",),
             inc="vs", ev=f"c_cb{tt}")
        P.op("vector", "tensor_scalar", s_indf[:, tt * NE:(tt + 1) * NE],
             s_cmb[:, tt * NE:(tt + 1) * NE], 0.0, None, ALU.is_gt,
             wait=(f"!c_cb{tt}",))
        P.op("vector", "tensor_copy", s_indb[:, tt * NE:(tt + 1) * NE],
             s_indf[:, tt * NE:(tt + 1) * NE], inc="vs", ev=f"c_ind{tt}")
        w = [f"c_ind{tt}"]
        if tt == 0:
            w.append("init_v")
        if tt >= 2:
            w.append(f"c_pge{tt-2}")
        for tp in range(tt + 1):
            P.op("tensor", "matmul", p_pg[tt % 2][:],
                 c_um[:] if tp == tt else c_one[:],
                 s_indb[:, tp * NE:(tp + 1) * NE],
                 start=(tp == 0), stop=(tp == tt),
                 wait=tuple(w) if tp == 0 else ())
        P.op("tensor", "sem_inc", inc="ts", ev=f"c_pgm{tt}")
        P.op("vector", "tensor_copy", s_pg[:, tt * NE:(tt + 1) * NE], p_pg[tt % 2][:],
             wait=(f"c_pgm{tt}",), inc="vs", ev=f"c_pge{tt}")

    if debug:
        P.op("gpsimd", "dma_start", out=d_dbg["x2T"], in_=s_x2T[:, 0:D], wait=(f"c_te7_{FC-1}", f"c_te{NT-1}_{FC-1}"), inc="dbg_ld", dma=True)
        P.op("sync", "dma_start", out=d_dbg["cmb"][:, 0:NT * NE], in_=s_cmb[:], wait=(f"c_cb{NT-1}",), inc="dbg_ld", dma=True)
        P.op("sync", "dma_start", out=d_dbg["lgt"][:, 0:NT * NE], in_=s_lgt[:], wait=(f"c_cb{NT-1}",), inc="dbg_ld", dma=True)
    P.barrier("endC")
    pfree(8)
    if stop == "C":
        return _finish_early("endC.vector")

    p_u = [pal(f"p_u{i}", [128, 512], F32) for i in range(2)]
    p_g = [pal(f"p_g{i}", [128, 512], F32) for i in range(2)]
    p_d = [pal(f"p_d{i}", [128, 512], F32) for i in range(2)]

    NPOS = NE + 2
    last_de = {}
    gate0 = ("endC.tensor", "endC.vector", "endC.scalar", "endC.sync",
             f"c_sp{NT-2}", f"c_sp{NT-1}")

    def emit_loads(pos):
        if pos < 2:
            half = pos
            wu_src = d_su[:, half * EDIM:(half + 1) * EDIM]
            wg_src = d_sg[:, half * EDIM:(half + 1) * EDIM]
            wd_src = d_sd[half * EDIM:(half + 1) * EDIM, :]
        else:
            e = pos - 2
            wu_src, wg_src, wd_src = d_wu[e], d_wgf[e], d_wd[e]
        for jh in range(2):
            for gg, src in ((0, wu_src), (1, wg_src)):
                k = pos * 4 + jh * 2 + gg
                sl = k % 3
                w = []
                if k >= 3:
                    ppos, rem = divmod(k - 3, 4)
                    pjh, pgg = divmod(rem, 2)
                    w.append(f"d_{'uh' if pgg == 0 else 'gh'}{ppos}_{pjh}")
                elif k == 0:
                    w += list(gate0)
                P.op("gpsimd", "dma_start",
                     out=s_wring[sl][:].rearrange("p (c j) -> p c j", c=FC),
                     in_=src[:, jh * 512:(jh + 1) * 512].rearrange("(c p) j -> p c j", p=128),
                     wait=tuple(w), inc=f"wring{sl}_ld", dma=True, ev=f"d_wl{k}")
        for dh in range(2):
            w = []
            if pos >= 1:
                w.append(last_de[pos - 1])
            P.op("gpsimd", "dma_start",
                 out=s_wd2[dh][:].rearrange("p (c f) -> p c f", c=4),
                 in_=wd_src[dh * 512:(dh + 1) * 512, :].rearrange("(c p) f -> p c f", p=128),
                 wait=tuple(w), inc=f"wd{dh}_ld", dma=True, ev=f"d_wd{pos}_{dh}")
        if pos == 0:
            P.op("gpsimd", "dma_start",
                 out=s_x2tok[:].rearrange("p (t d) -> p t d", t=NT),
                 in_=d_x2.rearrange("(t p) d -> p t d", p=128),
                 inc="x2tok_ld", dma=True, ev="d_x2tok")

    def shared_compute(pos):
        for jh in range(2):
            ku = pos * 4 + jh * 2
            kg = ku + 1
            for st in range(2):
                for jl in range(4):
                    pj = jl % 2
                    idx = (jh * 2 + st) * 4 + jl
                    w = [f"d_wl{ku}"]
                    if idx >= 2:
                        w.append(f"d_si{pos}_{idx-2}")
                    elif pos >= 1:
                        w.append(f"d_si{pos-1}_{14 + idx}")
                    for fc in range(FC):
                        P.op("tensor", "matmul", p_u[pj][:],
                             s_wring[ku % 3][:, fc * 512 + jl * 128: fc * 512 + (jl + 1) * 128],
                             s_x2T[:, fc * TOK + st * 512: fc * TOK + (st + 1) * 512],
                             start=(fc == 0), stop=(fc == FC - 1),
                             wait=tuple(w) if fc == 0 else ())
                    uev = f"d_uh{pos}_{jh}" if (st, jl) == (1, 3) else f"d_u{pos}_{jh}_{st}_{jl}"
                    P.op("tensor", "sem_inc", inc="ts", ev=uev)
                    w = [f"d_wl{kg}"]
                    if idx >= 2:
                        w.append(f"d_h{pos}_{idx-2}")
                    elif pos >= 1:
                        w.append(f"d_h{pos-1}_{14 + idx}")
                    for fc in range(FC):
                        P.op("tensor", "matmul", p_g[pj][:],
                             s_wring[kg % 3][:, fc * 512 + jl * 128: fc * 512 + (jl + 1) * 128],
                             s_x2T[:, fc * TOK + st * 512: fc * TOK + (st + 1) * 512],
                             start=(fc == 0), stop=(fc == FC - 1),
                             wait=tuple(w) if fc == 0 else ())
                    gev = f"d_gh{pos}_{jh}" if (st, jl) == (1, 3) else f"d_g{pos}_{jh}_{st}_{jl}"
                    P.op("tensor", "sem_inc", inc="ts", ev=gev)
                    w = [uev]
                    if idx >= 2:
                        w.append(f"d_h{pos}_{idx-2}")
                    elif pos >= 1:
                        w.append(f"d_h{pos-1}_{14 + idx}")
                    P.op("scalar", "activation", s_su2[pj][:], p_u[pj][:], AF.Silu,
                         wait=tuple(w), inc="ss", ev=f"d_si{pos}_{idx}")
                    jc = jh * 4 + jl
                    P.op("vector", "tensor_mul",
                         s_Hsh[:, jc * TOK + st * 512: jc * TOK + (st + 1) * 512],
                         s_su2[pj][:], p_g[pj][:],
                         wait=(f"d_si{pos}_{idx}", gev),
                         inc="vs", ev=f"d_h{pos}_{idx}")
        for dh in range(2):
            for tsub in range(NT):
                for fch in range(4):
                    idx = tsub * 4 + fch
                    pj = idx % 2
                    w = [f"d_h{pos}_15", f"d_wd{pos}_{dh}"]
                    if idx >= 2:
                        w.append(f"d_dev{pos}_{dh}_{idx-2}")
                    elif dh == 1:
                        w.append(f"d_dev{pos}_0_{30 + idx}")
                    elif pos >= 1:
                        w.append(f"d_dev{pos-1}_1_{30 + idx}")
                    for jl in range(4):
                        jc = dh * 4 + jl
                        P.op("tensor", "matmul", p_d[pj][:],
                             s_Hsh[:, jc * TOK + tsub * 128: jc * TOK + (tsub + 1) * 128],
                             s_wd2[dh][:, jl * D + fch * 512: jl * D + (fch + 1) * 512],
                             start=(jl == 0), stop=(jl == 3),
                             wait=tuple(w) if jl == 0 else ())
                    ev = f"d_de{pos}_{dh}_{idx}"
                    P.op("tensor", "sem_inc", inc="ts", ev=ev)
                    if dh == 1 and idx == 31:
                        last_de[pos] = ev
                    acc = s_acc[:, tsub * D + fch * 512: tsub * D + (fch + 1) * 512]
                    if pos == 0 and dh == 0:
                        P.op("vector", "tensor_scalar", acc, p_d[pj][:], 1.0, None,
                             ALU.mult, wait=(ev,), inc="vs", ev=f"d_dev{pos}_{dh}_{idx}")
                    else:
                        P.op("vector", "scalar_tensor_tensor", acc, p_d[pj][:], 1.0, acc,
                             ALU.mult, ALU.add, wait=(ev,),
                             inc="vs", ev=f"d_dev{pos}_{dh}_{idx}")

    def sparse_compute(pos):
        e = pos - 2
        for tt in range(NT):
            w = []
            if tt == 0:
                w.append(last_de[1] if pos == 2 else f"d_gmm{pos-1}_{15}")
            P.op("vector", "tensor_scalar", s_sel[:, tt * CAP:(tt + 1) * CAP],
                 c_iot[:], s_pg[:, tt * NE + e: tt * NE + e + 1],
                 s_indf[:, tt * NE + e: tt * NE + e + 1], ALU.is_equal, ALU.mult,
                 wait=tuple(w))
            P.op("vector", "tensor_scalar", s_selc2[tt % 2][:],
                 c_iot[:], s_pg[:, tt * NE + e: tt * NE + e + 1],
                 s_cmb[:, tt * NE + e: tt * NE + e + 1], ALU.is_equal, ALU.mult,
                 wait=(f"d_stm{pos}_{(tt - 2) * 2 + 1}",) if tt >= 2 else (),
                 inc="vs", ev=f"d_slc{pos}_{tt}")
        for tt in range(NT):
            for cs in range(2):
                ti = tt * 2 + cs
                w = [f"d_slc{pos}_{tt}"]
                if ti == 0:
                    w.append(f"d_dev1_1_31" if pos == 2 else f"d_ste{pos-1}")
                if ti >= 2:
                    w.append(f"d_stv{pos}_{ti-2}")
                P.op("tensor", "transpose", p_stb[ti % 2][:],
                     s_selc2[tt % 2][:, cs * 128:(cs + 1) * 128], c_idb[:],
                     wait=tuple(w), inc="ts", ev=f"d_stm{pos}_{ti}")
                P.op("scalar", "copy",
                     s_selT[:, cs * TOK + tt * 128: cs * TOK + (tt + 1) * 128],
                     p_stb[ti % 2][:],
                     wait=(f"d_stm{pos}_{ti}",), inc="ss",
                     ev=f"d_ste{pos}" if ti == 15 else f"d_stv{pos}_{ti}")
        for fc in range(FC):
            pgx = p_su if fc % 2 == 0 else p_sg
            w = []
            if fc == 0:
                w += [f"d_slc{pos}_{NT-1}", "d_x2tok"]
                if pos > 2:
                    w += [f"d_gev{pos-1}_{15}", f"d_si{pos-1}_{7}", f"d_h{pos-1}_{7}"]
            if fc >= 2:
                w.append(f"d_gev{pos}_{fc-2}")
            for tt in range(NT):
                P.op("tensor", "matmul", pgx[:],
                     s_x2tok[:, tt * D + fc * 128: tt * D + (fc + 1) * 128],
                     s_sel[:, tt * CAP:(tt + 1) * CAP],
                     start=(tt == 0), stop=(tt == NT - 1),
                     wait=tuple(w) if tt == 0 else ())
            P.op("tensor", "sem_inc", inc="ts", ev=f"d_gmm{pos}_{fc}")
            P.op("scalar", "copy", s_xg[:, fc * CAP:(fc + 1) * CAP], pgx[:],
                 wait=(f"d_gmm{pos}_{fc}",), inc="ss", ev=f"d_gev{pos}_{fc}")
        for jh in range(2):
            ku = pos * 4 + jh * 2
            kg = ku + 1
            for jl in range(4):
                jc = jh * 4 + jl
                pj = jc % 2
                w = [f"d_wl{ku}"]
                w.append(f"d_gev{pos}_{FC-1}" if jc == 0 else f"d_si{pos}_{jc-1}")
                for fc in range(FC):
                    P.op("tensor", "matmul", p_su[:],
                         s_wring[ku % 3][:, fc * 512 + jl * 128: fc * 512 + (jl + 1) * 128],
                         s_xg[:, fc * CAP:(fc + 1) * CAP],
                         start=(fc == 0), stop=(fc == FC - 1),
                         wait=tuple(w) if fc == 0 else ())
                uev = f"d_uh{pos}_{jh}" if jl == 3 else f"d_u{pos}_{jh}_{jl}"
                P.op("tensor", "sem_inc", inc="ts", ev=uev)
                w = [f"d_wl{kg}"]
                w.append(f"d_gev{pos}_{FC-1}" if jc == 0 else f"d_h{pos}_{jc-1}")
                for fc in range(FC):
                    P.op("tensor", "matmul", p_sg[:],
                         s_wring[kg % 3][:, fc * 512 + jl * 128: fc * 512 + (jl + 1) * 128],
                         s_xg[:, fc * CAP:(fc + 1) * CAP],
                         start=(fc == 0), stop=(fc == FC - 1),
                         wait=tuple(w) if fc == 0 else ())
                gev = f"d_gh{pos}_{jh}" if jl == 3 else f"d_g{pos}_{jh}_{jl}"
                P.op("tensor", "sem_inc", inc="ts", ev=gev)
                w = [uev]
                if jc >= 2:
                    w.append(f"d_h{pos}_{jc-2}")
                elif pos > 2:
                    w.append(f"d_h{pos-1}_{6 + jc}")
                P.op("scalar", "activation", s_su2s[pj][:], p_su[:], AF.Silu,
                     wait=tuple(w), inc="ss", ev=f"d_si{pos}_{jc}")
                P.op("vector", "tensor_mul", s_hh[:, jc * CAP:(jc + 1) * CAP],
                     s_su2s[pj][:], p_sg[:],
                     wait=(f"d_si{pos}_{jc}", gev), inc="vs", ev=f"d_h{pos}_{jc}")

        def de_mm(fch, cs):
            idx = fch * 2 + cs
            pj = idx % 2
            w = []
            if idx == 0:
                w += [f"d_h{pos}_{7}", f"d_wd{pos}_0", f"d_wd{pos}_1"]
            if idx >= 2:
                w.append(f"d_dv{pos}_{idx-2}")
            for dh in range(2):
                for jl in range(4):
                    jc = dh * 4 + jl
                    P.op("tensor", "matmul", p_sd[pj][:],
                         s_hh[:, jc * CAP + cs * 128: jc * CAP + cs * 128 + 128],
                         s_wd2[dh][:, jl * D + fch * 512: jl * D + (fch + 1) * 512],
                         start=(dh == 0 and jl == 0), stop=(dh == 1 and jl == 3),
                         wait=tuple(w) if (dh == 0 and jl == 0) else ())
            ev = f"d_dm{pos}_{idx}"
            P.op("tensor", "sem_inc", inc="ts", ev=ev)
            if idx == 7:
                last_de[pos] = ev
            w = [ev]
            if fch >= 2:
                w.append(f"d_scm{pos}_{fch-2}_{NT-1}")
            elif pos > 2:
                w.append(f"d_scm{pos-1}_{fch+2}_{NT-1}")
            P.op("scalar", "copy", s_deg2[fch % 2][:, cs * 512:(cs + 1) * 512],
                 p_sd[pj][:], wait=tuple(w), inc="ss", ev=f"d_dv{pos}_{idx}")

        def sc_mm(fch):
            for tt in range(NT):
                g = fch * NT + tt
                pj = g % 2
                w = []
                if tt == 0:
                    w.append(f"d_dv{pos}_{fch * 2 + 1}")
                    if g == 0:
                        w.append(f"d_ste{pos}")
                        if pos > 2:
                            w.append(f"d_sv{pos-1}_3_{NT-1}")
                if g >= 2:
                    fp, tp = divmod(g - 2, NT)
                    w.append(f"d_sv{pos}_{fp}_{tp}")
                for cs in range(2):
                    P.op("tensor", "matmul", p_ss[pj][:],
                         s_selT[:, cs * TOK + tt * 128: cs * TOK + (tt + 1) * 128],
                         s_deg2[fch % 2][:, cs * 512:(cs + 1) * 512],
                         start=(cs == 0), stop=(cs == 1),
                         wait=tuple(w) if cs == 0 else ())
                P.op("tensor", "sem_inc", inc="ts", ev=f"d_scm{pos}_{fch}_{tt}")
                acc = s_acc[:, tt * D + fch * 512: tt * D + (fch + 1) * 512]
                P.op("vector", "scalar_tensor_tensor", acc, p_ss[pj][:], 1.0, acc,
                     ALU.mult, ALU.add, wait=(f"d_scm{pos}_{fch}_{tt}",),
                     inc="vs", ev=f"d_sv{pos}_{fch}_{tt}")

        de_mm(0, 0); de_mm(0, 1); de_mm(1, 0); de_mm(1, 1)
        sc_mm(0)
        de_mm(2, 0); de_mm(2, 1)
        sc_mm(1)
        de_mm(3, 0); de_mm(3, 1)
        sc_mm(2)
        sc_mm(3)

    for pos in range(2):
        emit_loads(pos)
        shared_compute(pos)

    pfree(6)
    p_su = pal("p_su", [128, CAP], F32)
    p_sg = pal("p_sg", [128, CAP], F32)
    p_sd = [pal(f"p_sd{i}", [128, 512], F32) for i in range(2)]
    p_ss = [pal(f"p_ss{i}", [128, 512], F32) for i in range(2)]
    p_stb = [pal(f"p_stb{i}", [128, 128], BF16) for i in range(2)]

    for pos in range(2, NPOS):
        emit_loads(pos)
        sparse_compute(pos)

    P.barrier("endD", engines=("sync", "tensor", "vector", "scalar"))
    pfree(8)
    if stop == "D":
        return _finish_early("endD.vector")

    for tt in range(NT):
        bi = tt % 2
        eb = 8 * (tt % 2)
        w = ["endD.vector", "endD.tensor", f"c_sp{tt}"]
        if tt >= 2:
            w.append(f"e_r2{tt-2}")
        P.op("sync", "dma_start", out=s_x2r[bi][:], in_=d_x2[tt * 128:(tt + 1) * 128, :],
             wait=tuple(w), inc=f"x2r{bi}_ld", dma=True, ev=f"e_x2{tt}")
        wv = [f"e_x2{tt}"]
        if tt >= 2:
            wv.append(f"e_ob{tt-2}")
        P.op("vector", "scalar_tensor_tensor", s_r2[bi][:],
             s_acc[:, tt * D:(tt + 1) * D], 1.0, s_x2r[bi][:], ALU.mult, ALU.add,
             accum_out=s_st2[:, eb: eb + 1], wait=tuple(wv), inc="vs", ev=f"e_r2{tt}")
        P.op("vector", "tensor_scalar", s_st2[:, eb + 1: eb + 2], s_st2[:, eb: eb + 1],
             -1.0 / D, None, ALU.mult, wait=(f"!e_r2{tt}",), inc="vs", ev=f"e_nm{tt}")
        P.op("scalar", "activation", s_sq2[:], s_r2[bi][:], AF.Square,
             bias=s_st2[:, eb + 1: eb + 2], accum_out=s_st2[:, eb + 2: eb + 3],
             wait=(f"e_nm{tt}", "endD.vector"), inc="ss", ev=f"e_vsm{tt}")
        P.op("scalar", "activation", s_st2[:, eb + 3: eb + 4], s_st2[:, eb + 2: eb + 3],
             AF.Sqrt, bias=c_eps[:, 0:1], scale=1.0 / D, wait=(f"!e_vsm{tt}",),
             inc="ss", ev=f"e_std{tt}")
        P.op("vector", "reciprocal", s_st2[:, eb + 2: eb + 3], s_st2[:, eb + 3: eb + 4],
             wait=(f"e_std{tt}",), inc="vs", ev=f"e_rc{tt}")
        P.op("vector", "tensor_scalar", s_st2[:, eb + 4: eb + 5], s_st2[:, eb + 1: eb + 2],
             -1.0, None, ALU.mult)
        P.op("vector", "tensor_scalar", s_ob[bi][:], s_r2[bi][:], s_st2[:, eb + 4: eb + 5],
             s_st2[:, eb + 2: eb + 3], ALU.subtract, ALU.mult,
             wait=(f"!e_rc{tt}",) + ((f"e_st{tt-2}",) if tt >= 2 else ()),
             inc="vs", ev=f"e_ob{tt}")
        P.op("sync", "dma_start", out=d_out[tt * 128:(tt + 1) * 128, :], in_=s_ob[bi][:],
             wait=(f"e_ob{tt}",), inc=f"st{bi}_st", dma=True, ev=f"e_st{tt}")

    P.op("sync", "wait_only", wait=(f"e_st{NT-2}", f"e_st{NT-1}"))

    with nc.Block() as block:
        P.emit(block)
    pfree(len(pstack))
    return nc


_NC_CACHE = None


def _get_nc():
    global _NC_CACHE
    if _NC_CACHE is None:
        _NC_CACHE = build_program()
    return _NC_CACHE


def make_in_maps(L0_attn_x, L0_attn_Wq, L0_attn_Wk, L0_attn_Wv, L0_attn_Wg, L0_attn_Wo,
                 L0_ffn_router_w, L0_ffn_router_bias, L0_ffn_Wu, L0_ffn_Wg, L0_ffn_Wd,
                 L0_ffn_Su, L0_ffn_Sg, L0_ffn_Sd):
    x = np.ascontiguousarray(np.asarray(L0_attn_x, dtype=np.float32)).reshape(B * S, D)
    wq = np.ascontiguousarray(np.asarray(L0_attn_Wq, dtype=np.float32))
    wk = np.ascontiguousarray(np.asarray(L0_attn_Wk, dtype=np.float32))
    wv = np.ascontiguousarray(np.asarray(L0_attn_Wv, dtype=np.float32))
    wg = np.ascontiguousarray(np.asarray(L0_attn_Wg, dtype=np.float32))
    wo = np.ascontiguousarray(np.asarray(L0_attn_Wo, dtype=np.float32))
    rw = np.ascontiguousarray(np.asarray(L0_ffn_router_w, dtype=np.float32))
    rb = np.ascontiguousarray(np.asarray(L0_ffn_router_bias, dtype=np.float32))
    wu = np.ascontiguousarray(np.asarray(L0_ffn_Wu, dtype=np.float32))
    wgf = np.ascontiguousarray(np.asarray(L0_ffn_Wg, dtype=np.float32))
    wd = np.ascontiguousarray(np.asarray(L0_ffn_Wd, dtype=np.float32))
    su = np.ascontiguousarray(np.asarray(L0_ffn_Su, dtype=np.float32).reshape(D, 2 * EDIM))
    sg = np.ascontiguousarray(np.asarray(L0_ffn_Sg, dtype=np.float32).reshape(D, 2 * EDIM))
    sd = np.ascontiguousarray(np.asarray(L0_ffn_Sd, dtype=np.float32).reshape(2 * EDIM, D))

    rwt = np.ascontiguousarray(
        np.transpose(rw.reshape(NE, FC, 128), (2, 1, 0)).reshape(128, FC * NE))
    biasb = np.ascontiguousarray(np.broadcast_to(rb, (128, NE)))
    identb = np.eye(128).astype(ml_dtypes.bfloat16)
    utri = np.triu(np.ones((128, 128))).astype(ml_dtypes.bfloat16)
    iotac = np.ascontiguousarray(np.broadcast_to(
        np.arange(1, CAP + 1, dtype=np.float32), (128, CAP))).astype(ml_dtypes.bfloat16)

    in_maps = []
    for core in range(NCORE):
        xs = np.ascontiguousarray(x[core * TOK:(core + 1) * TOK])
        flags = np.zeros((128, 4), dtype=np.float32)
        flags[:, :core % 4] = 1.0
        in_maps.append({
            "x_tok": xs,
            "x_feat": np.ascontiguousarray(xs.T),
            "wq": wq, "wk": wk, "wv": wv, "wgt": wg, "wo": wo,
            "rwt": rwt, "rbias": biasb,
            "wu": wu, "wgf": wgf, "wd": wd,
            "su": su, "sg": sg, "sd": sd,
            "identb": identb, "iotac": iotac, "utri": utri,
            "flags": flags,
            "flags1m": np.ascontiguousarray(1.0 - flags),
        })
    return in_maps


def kernel(**inputs):
    in_maps = make_in_maps(**inputs)
    nc = _get_nc()
    res = run_bass_kernel_spmd(nc, in_maps, core_ids=list(range(NCORE)))
    out = np.concatenate([np.asarray(r["out"]) for r in res.results], axis=0)
    return out.reshape(B, S, D)


# version 08
# speedup vs baseline: 1.4517x; 1.0255x over previous
import numpy as np
import ml_dtypes

import concourse.bass as bass
import concourse.mybir as mybir
from concourse.bass_utils import run_bass_kernel_spmd

F32 = mybir.dt.float32
BF16 = mybir.dt.bfloat16
AF = mybir.ActivationFunctionType
ALU = mybir.AluOpType

B, S, D = 2, 4096, 2048
NCORE = 8
TOK = (B * S) // NCORE
NT = TOK // 128
NH = 16
DK = 128
FC = D // 128
NE = 16
EDIM = 1024
CAP = 256
LN_DKS = float(np.log(DK ** -0.5))
BIG = 1.0e9
EPS = 1e-5

ENGINES = ("sync", "gpsimd", "tensor", "vector", "scalar")
PSEM = {"sync": "ys", "gpsimd": "gs", "tensor": "ts", "vector": "vs", "scalar": "ss"}


_OP_RW = {
    "memset": ((0,), ()),
    "tensor_copy": ((0,), (1,)),
    "tensor_tensor": ((0,), (1, 2)),
    "tensor_add": ((0,), (1, 2)),
    "tensor_sub": ((0,), (1, 2)),
    "tensor_mul": ((0,), (1, 2)),
    "tensor_max": ((0,), (1, 2)),
    "tensor_scalar": ((0,), (1, 2, 3)),
    "tensor_single_scalar": ((0,), (1, 2)),
    "scalar_tensor_tensor": ((0,), (1, 2, 3)),
    "tensor_tensor_scan": ((0,), (1, 2, 3)),
    "reduce_max": ((0,), (1,)),
    "reduce_sum": ((0,), (1,)),
    "tensor_reduce": ((0,), (1,)),
    "reciprocal": ((0,), (1,)),
    "activation": ((0,), (1,)),
    "select": ((0,), (1, 2, 3)),
    "copy_predicated": ((0,), (1, 2)),
    "copy": ((0,), (1,)),
    "mul": ((0,), (1, 2)),
    "add": ((0,), (1, 2)),
}


def _ap_name(x):
    t = getattr(x, "tensor", None)
    return getattr(t, "name", None) if t is not None else None


class Prog:

    AUTO_SEM = {"vector": "var", "scalar": "sar"}

    def __init__(self, nc):
        self.nc = nc
        self.ops = {e: [] for e in ENGINES}
        self.semh = {}
        self.count = {}
        self.events = {}
        self.dma_sems = set()
        self.lastw = {e: {} for e in ENGINES}
        self.lastr = {e: {} for e in ENGINES}
        self.watermark = {e: -1 for e in ENGINES}

    def _sem(self, name):
        if name not in self.semh:
            self.semh[name] = self.nc.alloc_semaphore(name=name)
            self.count[name] = 0
        return self.semh[name]

    def op(self, eng, fn, *args, wait=(), inc=None, ev=None, dma=False, **kwargs):
        waits = [w for w in wait if w is not None]
        rec = [fn, args, kwargs, waits, inc, 16 if dma else 1, None]
        recs = self.ops[eng]
        if eng in ("vector", "scalar") and fn in _OP_RW:
            widx, ridx = _OP_RW[fn]
            reads = [_ap_name(args[i]) for i in ridx if i < len(args)]
            for kw in ("bias", "scale"):
                if kw in kwargs:
                    reads.append(_ap_name(kwargs[kw]))
            writes = [_ap_name(args[i]) for i in widx if i < len(args)]
            if kwargs.get("accum_out") is not None:
                writes.append(_ap_name(kwargs["accum_out"]))
            dep = -1
            for nm in reads:
                if nm is not None:
                    dep = max(dep, self.lastw[eng].get(nm, -1))
            for nm in writes:
                if nm is not None:
                    dep = max(dep, self.lastw[eng].get(nm, -1))
                    dep = max(dep, self.lastr[eng].get(nm, -1))
            if dep > self.watermark[eng]:
                rec[6] = dep
                self.watermark[eng] = dep
            myidx = len(recs)
            for nm in reads:
                if nm is not None:
                    self.lastr[eng][nm] = myidx
            for nm in writes:
                if nm is not None:
                    self.lastw[eng][nm] = myidx
        recs.append(rec)
        if inc is not None:
            self._sem(inc)
            if dma:
                self.dma_sems.add(inc)
            self.count[inc] += 16 if dma else 1
            if ev is not None:
                self.events[ev] = (inc, self.count[inc], eng)

    def barrier(self, name, engines=ENGINES):
        for e in engines:
            self.op(e, "sem_inc", inc=PSEM[e], ev=f"{name}.{e}")
        for e in engines:
            for o in engines:
                if o != e:
                    self.op(e, "wait_only", wait=(f"{name}.{o}",))

    def emit(self, block):
        auto_edge = {}
        for eng in ("vector", "scalar"):
            recs = self.ops[eng]
            need = {}
            for ci, rec in enumerate(recs):
                if rec[6] is not None:
                    need[rec[6]] = True
            counts = dict.fromkeys(self.count, 0)
            post = {}
            asem = self.AUTO_SEM[eng]
            self._sem(asem)
            counts[asem] = 0
            for i, rec in enumerate(recs):
                if i in need and rec[4] is None and rec[0] not in ("sem_inc", "wait_only"):
                    rec[4] = asem
                    rec[5] = 1
                if rec[4] is not None:
                    counts[rec[4]] = counts.get(rec[4], 0) + rec[5]
                post[i] = (rec[4], counts.get(rec[4], 0)) if rec[4] else None
            for ci, rec in enumerate(recs):
                if rec[6] is not None:
                    p = post[rec[6]]
                    assert p is not None, (eng, rec[6], recs[rec[6]][0])
                    auto_edge[(eng, ci)] = p

        for eng in ENGINES:
            recs = self.ops[eng]
            if not recs:
                continue

            def body(handle, recs=recs, eng=eng):
                waited = {}
                last_inst = None
                last_has_update = False
                for idx, (fn, args, kwargs, waits, inc, amt, dep) in enumerate(recs):
                    for evname in waits:
                        force = evname.startswith("!")
                        if force:
                            evname = evname[1:]
                        sem, val, prod = self.events[evname]
                        if prod == eng and sem not in self.dma_sems and not force:
                            continue
                        if waited.get(sem, 0) >= val:
                            continue
                        handle.wait_ge(self.semh[sem], val)
                        waited[sem] = val
                    ae = auto_edge.get((eng, idx))
                    if ae is not None and waited.get(ae[0], 0) < ae[1]:
                        handle.wait_ge(self.semh[ae[0]], ae[1])
                        waited[ae[0]] = ae[1]
                    if fn == "wait_only":
                        continue
                    if fn == "sem_inc":
                        if (eng in ("tensor", "vector", "scalar")
                                and last_inst is not None and not last_has_update):
                            last_inst.then_inc(self.semh[inc], amt)
                            last_has_update = True
                        elif eng in ("tensor", "vector", "scalar"):
                            handle.drain()
                            handle.sem_inc(self.semh[inc], amt)
                        else:
                            handle.sem_inc(self.semh[inc], amt)
                        continue
                    inst = getattr(handle, fn)(*args, **kwargs)
                    last_inst = inst
                    last_has_update = inc is not None
                    if inc is not None:
                        inst.then_inc(self.semh[inc], amt)

            getattr(block, eng)(body)


def build_program(stop=None, debug=False):
    nc = bass.Bass("TRN2", target_bir_lowering=False, detect_race_conditions=True)
    P = Prog(nc)

    d_x = nc.dram_tensor("x_tok", [TOK, D], F32, kind="ExternalInput").ap()
    d_xT = nc.dram_tensor("x_feat", [D, TOK], F32, kind="ExternalInput").ap()
    d_wq = nc.dram_tensor("wq", [D, D], F32, kind="ExternalInput").ap()
    d_wk = nc.dram_tensor("wk", [D, D], F32, kind="ExternalInput").ap()
    d_wv = nc.dram_tensor("wv", [D, D], F32, kind="ExternalInput").ap()
    d_wg = nc.dram_tensor("wgt", [D, D], F32, kind="ExternalInput").ap()
    d_wo = nc.dram_tensor("wo", [D, D], F32, kind="ExternalInput").ap()
    d_rwt = nc.dram_tensor("rwt", [128, FC * NE], F32, kind="ExternalInput").ap()
    d_bias = nc.dram_tensor("rbias", [128, NE], F32, kind="ExternalInput").ap()
    d_wu = nc.dram_tensor("wu", [NE, D, EDIM], F32, kind="ExternalInput").ap()
    d_wgf = nc.dram_tensor("wgf", [NE, D, EDIM], F32, kind="ExternalInput").ap()
    d_wd = nc.dram_tensor("wd", [NE, EDIM, D], F32, kind="ExternalInput").ap()
    d_su = nc.dram_tensor("su", [D, D], F32, kind="ExternalInput").ap()
    d_sg = nc.dram_tensor("sg", [D, D], F32, kind="ExternalInput").ap()
    d_sd = nc.dram_tensor("sd", [D, D], F32, kind="ExternalInput").ap()
    d_idb = nc.dram_tensor("identb", [128, 128], BF16, kind="ExternalInput").ap()
    d_iot = nc.dram_tensor("iotac", [128, CAP], BF16, kind="ExternalInput").ap()
    d_um = nc.dram_tensor("utri", [128, 128], BF16, kind="ExternalInput").ap()
    d_fl = nc.dram_tensor("flags", [128, 4], F32, kind="ExternalInput").ap()
    d_fl1 = nc.dram_tensor("flags1m", [128, 4], F32, kind="ExternalInput").ap()
    d_out = nc.dram_tensor("out", [TOK, D], F32, kind="ExternalOutput").ap()
    if debug:
        d_dbg = {k: nc.dram_tensor(f"dbg_{k}", [128, D], F32, kind="ExternalOutput").ap()
                 for k in ("qt", "kh", "v", "oT", "S", "sin", "x2T", "cmb", "acc", "lgt")}
    d_x2 = nc.dram_tensor("x2_scr", [TOK, D], BF16, kind="Internal").ap()
    d_ccin = nc.dram_tensor("cc_in", [128, NH * 128 + NH], BF16, kind="Internal").ap()
    d_ccout = nc.dram_tensor("cc_out", [4, 128, NH * 128 + NH], BF16, kind="Internal").ap()

    base = (nc.sbuf_base + 31) & ~31
    limit = nc.sbuf_top

    class Arena:
        def __init__(self, start):
            self.off = start
            self.hi = start

        def al(self, name, shape, dt):
            nbytes = int(np.prod(shape[1:])) * mybir.dt.size(dt)
            off = (self.off + 31) & ~31
            h = nc.alloc_sbuf_tensor_at(name, list(shape), dt, offset=base + off)
            self.off = off + nbytes
            self.hi = max(self.hi, self.off)
            assert base + self.off <= limit, (name, self.off, limit - base)
            return h

    CONST0 = 0
    OV1 = 2048
    OV2 = 34816
    SGRP = 100352
    OT0 = 114304
    OV3 = 147072
    ATMP = 179840
    DTMP = 100864

    ar = Arena(CONST0)
    c_idb = ar.al("c_idb", [128, 128], BF16)
    c_iot = ar.al("c_iot", [128, CAP], BF16)
    c_one = ar.al("c_one", [128, 128], BF16)
    c_um = ar.al("c_um", [128, 128], BF16)
    c_fl = ar.al("c_fl", [128, 4], F32)
    c_fl1 = ar.al("c_fl1", [128, 4], F32)
    c_bias = ar.al("c_bias", [128, NE], F32)
    c_lnb = ar.al("c_lnb", [128, 1], F32)
    c_eps = ar.al("c_eps", [128, 1], F32)
    assert ar.hi <= OV1

    ar = Arena(OV1)
    s_qt = ar.al("s_qt", [128, NH * TOK], BF16)
    assert ar.hi <= OV2
    ar = Arena(OV1)
    s_x2T = ar.al("s_x2T", [128, FC * TOK], BF16)
    assert ar.hi <= OV2

    ar = Arena(OV2)
    s_kh = ar.al("s_kh", [128, NH * TOK], BF16)
    s_v = ar.al("s_v", [128, NT * D], BF16)
    assert ar.hi <= SGRP
    ar = Arena(OV2)
    s_wo = ar.al("s_wo", [128, FC * D], BF16)
    assert ar.hi <= SGRP + 512
    ar = Arena(OV2)
    s_acc = ar.al("s_acc", [128, NT * D], BF16)
    s_x2tok = ar.al("s_x2tok", [128, NT * D], BF16)
    assert ar.hi <= SGRP

    ar = Arena(SGRP)
    s_cmb = ar.al("s_cmb", [128, NT * NE], F32)
    CMB_END = ar.hi
    ar = Arena(SGRP + 512)
    s_Sb = ar.al("s_Sb", [128, NH * 128], BF16)
    s_ea = ar.al("s_ea", [128, NH * NT], F32)
    s_pfr = ar.al("s_pfr", [128, NH], F32)
    s_pfa = ar.al("s_pfa", [128, NH * NT], F32)
    assert ar.hi <= OT0

    ar = Arena(OT0)
    s_oT = ar.al("s_oT", [128, NH * TOK], BF16)
    assert ar.hi <= OV3

    ar = Arena(OV3)
    s_xT = ar.al("s_xT", [128, FC * TOK], BF16)
    assert ar.hi <= ATMP

    ar = Arena(OV3)
    s_t1b = ar.al("s_t1b", [128, 128], F32)
    s_pfb = ar.al("s_pfb", [128, NH], BF16)
    s_sin = ar.al("s_sin", [128, NH * 128], F32)
    s_sinb = ar.al("s_sinb", [128, NH * 128], BF16)
    s_rb = [ar.al(f"s_rb{i}", [128, NH * 128 + NH], BF16) for i in range(2)]
    s_t1 = ar.al("s_t1", [128, 128], F32)
    s_dp = ar.al("s_dp", [128, 1], F32)
    s_ptm = [ar.al(f"s_ptm{i}", [128, 128], BF16) for i in range(2)]
    s_khtm = [ar.al(f"s_khtm{i}", [128, 128], BF16) for i in range(2)]
    s_qp = [ar.al(f"s_qp{i}", [128, 128], BF16) for i in range(2)]
    assert ar.hi <= ATMP

    ar = Arena(OV3)
    s_xr = [ar.al(f"s_xr{i}", [128, D], F32) for i in range(2)]
    s_r = [ar.al(f"s_r{i}", [128, D], F32) for i in range(2)]
    s_x2 = [ar.al(f"s_x2{i}", [128, D], BF16) for i in range(2)]
    s_sq = ar.al("s_sq", [128, D], F32)
    s_st4 = ar.al("s_st4", [128, 16], F32)
    s_rst = ar.al("s_rst", [128, 8], F32)
    s_rwt = ar.al("s_rwt", [128, FC * NE], BF16)
    s_lgt = ar.al("s_lgt", [128, NT * NE], F32)
    s_i1 = ar.al("s_i1", [128, NE], F32)
    s_lb = ar.al("s_lb", [128, NE], F32)
    s_ex = ar.al("s_ex", [128, NE], F32)
    assert base + ar.hi <= limit

    ar = Arena(ATMP)
    s_wr = [ar.al(f"s_wr{i}", [128, FC * 128], BF16) for i in range(2)]
    s_lq = [ar.al(f"s_lq{i}", [128, TOK], BF16) for i in range(2)]
    s_lk = [ar.al(f"s_lk{i}", [128, TOK], BF16) for i in range(2)]
    s_wv2 = [ar.al(f"s_wv2{i}", [128, FC * 256], BF16) for i in range(2)]
    assert base + ar.hi <= limit, ar.hi

    ar = Arena(DTMP)
    s_wring = [ar.al(f"s_wring{i}", [128, FC * 512], BF16) for i in range(3)]
    s_wd2 = [ar.al(f"s_wd2{i}", [128, 4 * D], BF16) for i in range(2)]
    SHOV = ar.hi
    ar = Arena(SHOV)
    s_Hsh = ar.al("s_Hsh", [128, 8 * TOK], BF16)
    SMALLS = ar.hi
    ar = Arena(SHOV)
    s_sel = ar.al("s_sel", [128, NT * CAP], BF16)
    s_selT = ar.al("s_selT", [128, 2 * TOK], BF16)
    s_xg = ar.al("s_xg", [128, FC * CAP], BF16)
    assert ar.hi <= SMALLS
    ar = Arena(SMALLS)
    s_su2 = [ar.al(f"s_su2{i}", [128, 512], F32) for i in range(2)]
    ar = Arena(SMALLS)
    s_su2s = [ar.al(f"s_su2s{i}", [128, CAP], F32) for i in range(2)]
    ar = Arena(SMALLS)
    s_deg2 = [ar.al(f"s_deg2{i}", [128, 2 * 512], BF16) for i in range(2)]
    s_hh = ar.al("s_hh", [128, 8 * CAP], BF16)
    s_selc8 = ar.al("s_selc8", [128, NT * CAP], BF16)
    s_pg = ar.al("s_pg", [128, NT * NE], F32)
    s_indf = ar.al("s_indf", [128, NT * NE], F32)
    s_indb = ar.al("s_indb", [128, NT * NE], BF16)
    assert base + ar.hi <= limit, ar.hi

    ar = Arena(DTMP)
    s_x2r = [ar.al(f"s_x2r{i}", [128, D], BF16) for i in range(2)]
    s_r2 = [ar.al(f"s_r2{i}", [128, D], F32) for i in range(2)]
    s_ob = [ar.al(f"s_ob{i}", [128, D], F32) for i in range(2)]
    s_sq2 = ar.al("s_sq2", [128, D], F32)
    s_st2 = ar.al("s_st2", [128, 16], F32)
    assert base + ar.hi <= limit

    pstack = []

    def pal(name, shape, dt):
        t = nc.psum_tensor(name, shape, dt)
        h = t.__enter__()
        pstack.append(t)
        return h

    def pfree(n):
        for _ in range(n):
            pstack.pop().__exit__(None, None, None)

    def _finish_early(dep_ev):
        P.op("sync", "dma_start", out=d_out[0:128, :], in_=s_sq[:] if False else s_xT[:, 0:D],
             wait=(dep_ev,), inc="out_st", dma=True, ev="early_st")
        P.op("sync", "wait_only", wait=("early_st",))
        with nc.Block() as blk:
            P.emit(blk)
        pfree(len(pstack))
        return nc

    P.op("sync", "dma_start", out=c_idb[:], in_=d_idb, inc="c_ld", dma=True)
    P.op("sync", "dma_start", out=c_iot[:], in_=d_iot, inc="c_ld", dma=True)
    P.op("sync", "dma_start", out=c_um[:], in_=d_um, inc="c_ld", dma=True)
    P.op("sync", "dma_start", out=c_fl[:], in_=d_fl, inc="c_ld", dma=True)
    P.op("sync", "dma_start", out=c_fl1[:], in_=d_fl1, inc="c_ld", dma=True)
    P.op("sync", "dma_start", out=c_bias[:], in_=d_bias, inc="c_ld", dma=True, ev="ld_const")
    P.op("vector", "memset", c_lnb[:], LN_DKS)
    P.op("vector", "memset", c_eps[:], EPS)
    P.op("vector", "memset", c_one[:], 1.0)
    P.op("vector", "memset", s_Sb[:], 0.0)
    P.op("vector", "memset", s_pfr[:], 1.0, inc="vs", ev="init_v")
    P.op("gpsimd", "dma_start",
         out=s_xT[:].rearrange("p (c t) -> p c t", c=FC),
         in_=d_xT.rearrange("(c p) t -> p c t", p=128),
         inc="xT_ld", dma=True, ev="ld_xT")

    p_q = pal("p_q", [128, TOK], F32)
    p_k = pal("p_k", [128, TOK], F32)
    p_z = pal("p_z", [128, TOK], F32)
    p_cu = pal("p_cu", [128, TOK], F32)

    wsrc = {0: d_wg, 1: d_wq, 2: d_wk}
    consume_ev = {0: "a_z", 1: "a_q", 2: "a_k"}

    for h in range(NH):
        for t3 in range(3):
            j = 3 * h + t3
            sl = j % 2
            w = []
            if j >= 2:
                pj = j - 2
                w.append(f"{consume_ev[pj % 3]}{pj // 3}")
            P.op("gpsimd", "dma_start",
                 out=s_wr[sl][:].rearrange("p (c d) -> p c d", c=FC),
                 in_=wsrc[t3][:, h * 128:(h + 1) * 128].rearrange("(c p) d -> p c d", p=128),
                 wait=tuple(w), inc=f"wr{j % 2}_ld", dma=True, ev=f"a_wl{j}")
        for (t3, pp, evn, freev) in (
            (0, p_z, f"a_z{h}", f"a_cu{h-1}"),
            (1, p_q, f"a_q{h}", f"a_qg{h-1}"),
            (2, p_k, f"a_k{h}", f"a_kg{h-1}"),
        ):
            j = 3 * h + t3
            sl = j % 2
            w = [f"a_wl{j}", "ld_xT"]
            if h >= 1:
                w.append(freev)
            for st in range(2):
                for fc in range(FC):
                    P.op("tensor", "matmul",
                         pp[:, st * 512:(st + 1) * 512],
                         s_wr[sl][:, fc * 128:(fc + 1) * 128],
                         s_xT[:, fc * TOK + st * 512: fc * TOK + (st + 1) * 512],
                         start=(fc == 0), stop=(fc == FC - 1),
                         wait=tuple(w) if (st == 0 and fc == 0) else ())
            P.op("tensor", "sem_inc", inc="ts", ev=evn)
        if stop == ("h", h, "mm"):
            return _finish_early(f"a_k{h}")
        P.op("scalar", "activation", p_cu[:], p_z[:], AF.Sigmoid,
             wait=(f"a_z{h}", "init_v") + ((f"a_ea{h-1}",) if h >= 1 else ()))
        P.op("scalar", "activation", p_z[:], p_cu[:], AF.Ln, inc="ss", ev=f"a_ln{h}")
        if stop == ("h", h, "ln"):
            return _finish_early(f"a_ln{h}")
        for c in range(NT):
            csl = slice(c * 128, (c + 1) * 128)
            P.op("vector", "tensor_tensor_scan", p_cu[:, csl], p_z[:, csl],
                 c_um[:], 0.0, ALU.add, ALU.bypass,
                 wait=((f"a_ln{h}",) + (("ld_const",) if h == 0 else ())) if c == 0 else ())
        P.op("vector", "sem_inc", inc="vs", ev=f"a_cu{h}")
        if stop == ("h", h, "cu"):
            return _finish_early(f"a_cu{h}")
        sl = h % 2
        P.op("scalar", "activation", s_lq[sl][:], p_cu[:], AF.Exp,
             bias=c_lnb[:, 0:1], scale=1.0 / 16.0,
             wait=(f"a_cu{h}",) + ((f"a_qg{h-2}",) if h >= 2 else ()),
             inc="ss", ev=f"a_lq{h}")
        P.op("scalar", "activation", s_lk[sl][:], p_cu[:], AF.Exp, scale=-1.0 / 16.0,
             wait=(f"a_kg{h-2}",) if h >= 2 else (),
             inc="ss", ev=f"a_lk{h}")
        for c in range(NT):
            P.op("scalar", "activation", s_ea[:, h * NT + c: h * NT + c + 1],
                 p_cu[:, (c + 1) * 128 - 1:(c + 1) * 128], AF.Exp, scale=1.0 / 16.0)
        P.op("scalar", "sem_inc", inc="ss", ev=f"a_ea{h}")
        if stop == ("h", h, "ea"):
            return _finish_early(f"a_ea{h}")
        P.op("vector", "tensor_mul", s_qt[:, h * TOK:(h + 1) * TOK], p_q[:], s_lq[sl][:],
             wait=(f"a_lq{h}", f"a_q{h}"), inc="vs", ev=f"a_qg{h}")
        P.op("vector", "tensor_mul", s_kh[:, h * TOK:(h + 1) * TOK], p_k[:], s_lk[sl][:],
             wait=(f"a_lk{h}", f"a_k{h}"), inc="vs", ev=f"a_kg{h}")
        if stop == ("h", h):
            return _finish_early(f"a_kg{h}")

    pfree(2)
    p_v = [pal(f"p_v{i}", [128, 256], F32) for i in range(2)]
    for dc in range(8):
        sl = dc % 2
        w = [f"a_cu{NH-1}", f"a_ea{NH-1}"]
        if dc >= 2:
            w.append(f"a_v{dc-2}_{NT-1}")
        P.op("gpsimd", "dma_start",
             out=s_wv2[sl][:].rearrange("p (c d) -> p c d", c=FC),
             in_=d_wv[:, dc * 256:(dc + 1) * 256].rearrange("(c p) d -> p c d", p=128),
             wait=tuple(w), inc=f"wv{sl}_ld", dma=True, ev=f"a_wvl{dc}")
        for tt in range(NT):
            pi = tt % 2
            j = dc * NT + tt
            w = [f"a_wvl{dc}"]
            if j >= 2:
                w.append(f"a_ve{(j-2)//NT}_{(j-2)%NT}")
            for fc in range(FC):
                P.op("tensor", "matmul", p_v[pi][:],
                     s_xT[:, fc * TOK + tt * 128: fc * TOK + (tt + 1) * 128],
                     s_wv2[sl][:, fc * 256:(fc + 1) * 256],
                     start=(fc == 0), stop=(fc == FC - 1),
                     wait=tuple(w) if fc == 0 else ())
            P.op("tensor", "sem_inc", inc="ts", ev=f"a_v{dc}_{tt}")
            P.op("vector", "tensor_copy",
                 s_v[:, tt * D + dc * 256: tt * D + (dc + 1) * 256],
                 p_v[pi][:], wait=(f"a_v{dc}_{tt}",), inc="vs", ev=f"a_ve{dc}_{tt}")

    if debug:
        P.op("gpsimd", "dma_start", out=d_dbg["qt"], in_=s_qt[:, 0:D], wait=("a_qg1",), inc="dbg_ld", dma=True)
        P.op("gpsimd", "dma_start", out=d_dbg["kh"], in_=s_kh[:, 0:D], wait=("a_kg1",), inc="dbg_ld", dma=True)
        P.op("gpsimd", "dma_start", out=d_dbg["v"], in_=s_v[:, 0:D], wait=(f"a_ve7_{NT-1}",), inc="dbg_ld", dma=True)
    P.barrier("endA")
    pfree(4)
    if stop == "A":
        return _finish_early("endA.vector")

    p_pt = [pal(f"p_pt{i}", [128, 128], F32) for i in range(2)]
    p_tr = [pal(f"p_tr{i}", [128, 128], BF16) for i in range(2)]
    p_ot = [pal(f"p_ot{i}", [128, 128], F32) for i in range(2)]
    p_kv = [pal(f"p_kv{i}", [128, 128], F32) for i in range(2)]

    for c in range(NT):
        for h in range(NH):
            i = c * NH + h
            p = i % 2
            qsl = s_qt[:, h * TOK + c * 128: h * TOK + (c + 1) * 128]
            ksl = s_kh[:, h * TOK + c * 128: h * TOK + (c + 1) * 128]
            vsl = s_v[:, c * D + h * 128: c * D + (h + 1) * 128]
            Sbsl = s_Sb[:, h * 128:(h + 1) * 128]
            osl = s_oT[:, h * TOK + c * 128: h * TOK + (c + 1) * 128]
            ea = s_ea[:, h * NT + c: h * NT + c + 1]
            w = ["endA.vector"] if i == 0 else ([f"b_ptm{i-2}"] if i >= 2 else [])
            P.op("tensor", "matmul", p_pt[p][:], ksl, qsl, start=True, stop=True,
                 wait=tuple(w), inc="ts", ev=f"b_pt{i}")
            w = ([f"b_khtm{i-2}"] if i >= 2 else []) + (["ld_const"] if i == 0 else [])
            P.op("tensor", "transpose", p_tr[p][:], ksl, c_idb[:],
                 wait=tuple(w), inc="ts", ev=f"b_tr{i}")
            w = [f"b_ptm{i}"]
            if i >= 2:
                w.append(f"b_oev{i-2}")
            if i >= NH:
                w.append(f"b_st{i-NH}")
            P.op("tensor", "matmul", p_ot[p][:], vsl, s_ptm[p][:], start=True, stop=False,
                 wait=tuple(w))
            P.op("tensor", "matmul", p_ot[p][:], Sbsl, qsl, start=False, stop=True,
                 inc="ts", ev=f"b_ot{i}")
            w = [f"b_khtm{i}"]
            if i >= 2:
                w.append(f"b_st{i-2}")
            P.op("tensor", "matmul", p_kv[p][:], s_khtm[p][:], vsl, start=True, stop=True,
                 wait=tuple(w), inc="ts", ev=f"b_kv{i}")
            P.op("vector", "tensor_mul", s_ptm[p][:], p_pt[p][:], c_um[:],
                 wait=(f"b_pt{i}",) + (("ld_const",) if i == 0 else ()),
                 inc="vs", ev=f"b_ptm{i}")
            P.op("scalar", "copy", s_khtm[p][:], p_tr[p][:],
                 wait=(f"b_tr{i}",), inc="ss", ev=f"b_khtm{i}")
            P.op("scalar", "copy", osl, p_ot[p][:],
                 wait=(f"b_ot{i}",), inc="ss", ev=f"b_oev{i}")
            P.op("vector", "tensor_add", s_t1b[:], Sbsl, p_kv[p][:], wait=(f"b_kv{i}",))
            P.op("scalar", "copy", s_pfa[:, h * NT + c: h * NT + c + 1],
                 s_pfr[:, h:h + 1])
            P.op("scalar", "mul", s_pfr[:, h:h + 1], s_pfr[:, h:h + 1], ea,
                 inc="ss", ev=f"b_pf{i}")
            P.op("vector", "tensor_scalar", Sbsl, s_t1b[:], ea, None, ALU.mult,
                 inc="vs", ev=f"b_st{i}")

    if stop == "B":
        return _finish_early(f"b_st{NT * NH - 1}")
    P.op("vector", "tensor_copy", s_pfb[:], s_pfr[:], wait=(f"b_pf{NT * NH - 1}",),
         inc="vs", ev="x_pfb")
    P.op("sync", "dma_start", out=d_ccin[:, :NH * 128], in_=s_Sb[:],
         wait=(f"b_st{NT * NH - 1}",), inc="ccin_ld", dma=True)
    P.op("sync", "dma_start", out=d_ccin[:, NH * 128:], in_=s_pfb[:],
         wait=("x_pfb",), inc="ccin_ld", dma=True, ev="x_ccin")
    P.op("gpsimd", "dma_start",
         out=s_wo[:].rearrange("p (c d) -> p c d", c=FC),
         in_=d_wo.rearrange("(c p) d -> p c d", p=128),
         wait=(f"b_kv{NT * NH - 1}",),
         inc="wo_ld", dma=True, ev="c_wo_l")
    P.op("gpsimd", "dma_start", out=s_rwt[:], in_=d_rwt, inc="rwt_ld", dma=True,
         ev="c_rw_l")
    P.op("gpsimd", "collective_compute", "AllGather", ALU.bypass,
         ins=[d_ccin], outs=[d_ccout],
         replica_groups=[[0, 1, 2, 3], [4, 5, 6, 7]],
         wait=("x_ccin",), inc="ccs", ev="x_cc")
    P.op("vector", "memset", s_sin[:], 0.0, wait=("endA.tensor",), inc="vs", ev="x_sin0")
    for r in range(4):
        w = ["x_cc"]
        if r >= 2:
            w.append(f"x_comb{r-2}")
        P.op("sync", "dma_start", out=s_rb[r % 2][:], in_=d_ccout[r],
             wait=tuple(w), inc=f"rb{r % 2}_ld", dma=True, ev=f"x_rb{r}")
        for h in range(NH):
            w = (f"x_rb{r}", "x_sin0", "ld_const") if h == 0 else ()
            sinh = s_sin[:, h * 128:(h + 1) * 128]
            P.op("vector", "tensor_scalar", s_dp[:],
                 s_rb[r % 2][:, NH * 128 + h: NH * 128 + h + 1],
                 c_fl[:, r:r + 1], c_fl1[:, r:r + 1], ALU.mult, ALU.add, wait=w)
            P.op("vector", "tensor_scalar", s_t1[:], sinh, s_dp[:, 0:1], None, ALU.mult)
            P.op("vector", "scalar_tensor_tensor", sinh,
                 s_rb[r % 2][:, h * 128:(h + 1) * 128], c_fl[:, r:r + 1], s_t1[:],
                 ALU.mult, ALU.add)
        P.op("vector", "sem_inc", inc="vs", ev=f"x_comb{r}")
    P.op("vector", "tensor_copy", s_sinb[:], s_sin[:], inc="vs", ev="x_sinb")

    if stop == "X":
        return _finish_early("x_sinb")
    pfree(8)
    p_om = [pal(f"p_om{i}", [128, 128], F32) for i in range(2)]
    j = 0
    for h in range(NH):
        for c in range(NT):
            p = j % 2
            qsl = s_qt[:, h * TOK + c * 128: h * TOK + (c + 1) * 128]
            osl = s_oT[:, h * TOK + c * 128: h * TOK + (c + 1) * 128]
            w = []
            if j == 0:
                w.append("x_sinb")
                w.append(f"b_pf{NT * NH - 1}")
            if j >= 2:
                w.append(f"a3_om{j-2}")
            P.op("vector", "tensor_scalar", s_qp[p][:], qsl,
                 s_pfa[:, h * NT + c: h * NT + c + 1], None, ALU.mult,
                 wait=tuple(w), inc="vs", ev=f"a3_qp{j}")
            w = [f"a3_qp{j}", "x_sinb"]
            if j >= 2:
                w.append(f"a3_oe{j-2}")
            P.op("tensor", "matmul", p_om[p][:], s_sinb[:, h * 128:(h + 1) * 128],
                 s_qp[p][:], start=True, stop=True, wait=tuple(w), inc="ts",
                 ev=f"a3_om{j}")
            P.op("vector", "tensor_add", osl, osl, p_om[p][:],
                 wait=(f"a3_om{j}",) + ((f"b_oev{NT * NH - 1}",) if j == 0 else ()),
                 inc="vs", ev=f"a3_oe{j}")
            j += 1

    if debug:
        P.op("gpsimd", "dma_start", out=d_dbg["S"][:, 0:NH * 128], in_=s_Sb[:], wait=(f"b_st{NT*NH-1}",), inc="dbg_ld", dma=True)
        P.op("sync", "dma_start", out=d_dbg["sin"][:, 0:NH * 128], in_=s_sin[:], wait=("x_sinb",), inc="dbg_ld", dma=True)
        P.op("gpsimd", "dma_start", out=d_dbg["oT"], in_=s_oT[:, 0:D], wait=(f"a3_oe{NH*NT-1}",), inc="dbg_ld", dma=True)
    P.barrier("endB")
    pfree(2)
    if stop == "A3":
        return _finish_early("endB.vector")

    p_w = [pal(f"p_w{i}", [128, 512], F32) for i in range(2)]
    p_t2 = [pal(f"p_t2{i}", [128, 128], BF16) for i in range(2)]
    p_rt = [pal(f"p_rt{i}", [128, NE], F32) for i in range(2)]
    p_pg = [pal(f"p_pg{i}", [128, NE], F32) for i in range(2)]

    for tt in range(NT):
        bi = tt % 2
        sb4 = 8 * (tt % 2)
        w = ["endB.vector"]
        if tt >= 2:
            w.append(f"c_x2{tt-2}")
        P.op("sync", "dma_start", out=s_xr[bi][:], in_=d_x[tt * 128:(tt + 1) * 128, :],
             wait=tuple(w), inc=f"xr{bi}_ld", dma=True, ev=f"c_x{tt}")
        for fc in range(4):
            pj = fc % 2
            w = ["c_wo_l"]
            if tt * 4 + fc >= 2:
                pidx = tt * 4 + fc - 2
                w.append(f"c_ev{pidx//4}_{pidx%4}")
            for ec in range(NH):
                P.op("tensor", "matmul", p_w[pj][:],
                     s_oT[:, ec * TOK + tt * 128: ec * TOK + (tt + 1) * 128],
                     s_wo[:, ec * D + fc * 512: ec * D + (fc + 1) * 512],
                     start=(ec == 0), stop=(ec == NH - 1),
                     wait=tuple(w) if ec == 0 else ())
            P.op("tensor", "sem_inc", inc="ts", ev=f"c_mm{tt}_{fc}")
            wv = [f"c_mm{tt}_{fc}", f"c_x{tt}"]
            if tt >= 2:
                wv.append(f"c_std{tt-2}")
            P.op("vector", "scalar_tensor_tensor",
                 s_r[bi][:, fc * 512:(fc + 1) * 512], p_w[pj][:], 1.0,
                 s_xr[bi][:, fc * 512:(fc + 1) * 512], ALU.mult, ALU.add,
                 accum_out=s_st4[:, sb4 + fc: sb4 + fc + 1],
                 wait=tuple(wv), inc="vs", ev=f"c_ev{tt}_{fc}")
        P.op("vector", "reduce_sum", s_st4[:, sb4 + 4: sb4 + 5], s_st4[:, sb4: sb4 + 4],
             axis=mybir.AxisListType.X, wait=(f"!c_ev{tt}_3",), inc="vs", ev=f"c_rs{tt}")
        P.op("vector", "tensor_scalar", s_st4[:, sb4 + 5: sb4 + 6],
             s_st4[:, sb4 + 4: sb4 + 5], -1.0 / D, None,
             ALU.mult, wait=(f"!c_rs{tt}",), inc="vs", ev=f"c_nm{tt}")
        P.op("scalar", "activation", s_sq[:], s_r[bi][:], AF.Square,
             bias=s_st4[:, sb4 + 5: sb4 + 6], accum_out=s_st4[:, sb4 + 6: sb4 + 7],
             wait=(f"c_nm{tt}",), inc="ss", ev=f"c_vsm{tt}")
        P.op("scalar", "activation", s_st4[:, sb4 + 7: sb4 + 8],
             s_st4[:, sb4 + 6: sb4 + 7], AF.Sqrt,
             bias=c_eps[:, 0:1], scale=1.0 / D, wait=(f"!c_vsm{tt}",),
             inc="ss", ev=f"c_std{tt}")
        P.op("vector", "reciprocal", s_st4[:, sb4 + 6: sb4 + 7],
             s_st4[:, sb4 + 7: sb4 + 8], wait=(f"c_std{tt}",), inc="vs", ev=f"c_rc{tt}")
        P.op("vector", "tensor_scalar", s_st4[:, sb4 + 4: sb4 + 5],
             s_st4[:, sb4 + 5: sb4 + 6], -1.0, None, ALU.mult)
        wv = [f"!c_rc{tt}"]
        if tt >= 2:
            wv.append(f"c_sp{tt-2}")
            wv.append(f"c_te{tt-2}_{FC-1}")
        P.op("vector", "tensor_scalar", s_x2[bi][:], s_r[bi][:],
             s_st4[:, sb4 + 4: sb4 + 5],
             s_st4[:, sb4 + 6: sb4 + 7], ALU.subtract, ALU.mult, wait=tuple(wv),
             inc="vs", ev=f"c_x2{tt}")
        P.op("sync", "dma_start", out=d_x2[tt * 128:(tt + 1) * 128, :], in_=s_x2[bi][:],
             wait=(f"c_x2{tt}",), inc=f"sp{bi}_ld", dma=True, ev=f"c_sp{tt}")
        for fc in range(FC):
            pj = fc % 2
            w = [f"c_x2{tt}"]
            if tt == 0 and fc == 0:
                w.append("ld_const")
            if tt * FC + fc >= 2:
                pidx = tt * FC + fc - 2
                w.append(f"c_te{pidx//FC}_{pidx%FC}")
            P.op("tensor", "transpose", p_t2[pj][:],
                 s_x2[bi][:, fc * 128:(fc + 1) * 128], c_idb[:],
                 wait=tuple(w), inc="ts", ev=f"c_tr{tt}_{fc}")
            P.op("vector", "tensor_copy",
                 s_x2T[:, fc * TOK + tt * 128: fc * TOK + (tt + 1) * 128],
                 p_t2[pj][:], wait=(f"c_tr{tt}_{fc}",), inc="vs", ev=f"c_te{tt}_{fc}")

    for tt in range(NT):
        pj = tt % 2
        w = ["c_rw_l", f"c_te{tt}_{FC-1}"]
        if tt >= 2:
            w.append(f"c_lg{tt-2}")
        for fc in range(FC):
            P.op("tensor", "matmul", p_rt[pj][:],
                 s_x2T[:, fc * TOK + tt * 128: fc * TOK + (tt + 1) * 128],
                 s_rwt[:, fc * NE:(fc + 1) * NE],
                 start=(fc == 0), stop=(fc == FC - 1),
                 wait=tuple(w) if fc == 0 else ())
        P.op("tensor", "sem_inc", inc="ts", ev=f"c_rt{tt}")
        lg = s_lgt[:, tt * NE:(tt + 1) * NE]
        P.op("vector", "tensor_copy", lg, p_rt[pj][:], wait=(f"c_rt{tt}",),
             inc="vs", ev=f"c_lg{tt}")
        rb4 = 4 * (tt % 2)
        P.op("vector", "tensor_add", s_lb[:], lg, c_bias[:],
             wait=("ld_const",) if tt == 0 else ())
        P.op("vector", "reduce_max", s_rst[:, rb4: rb4 + 1], s_lb[:],
             axis=mybir.AxisListType.X, inc="vs", ev=f"c_m1{tt}")
        P.op("vector", "tensor_scalar", s_i1[:], s_lb[:], s_rst[:, rb4: rb4 + 1], None,
             ALU.is_ge, wait=(f"!c_m1{tt}",))
        P.op("vector", "scalar_tensor_tensor", s_i1[:], s_i1[:], -BIG, s_lb[:],
             ALU.mult, ALU.add)
        P.op("vector", "reduce_max", s_rst[:, rb4 + 1: rb4 + 2], s_i1[:],
             axis=mybir.AxisListType.X, inc="vs", ev=f"c_m2{tt}")
        P.op("vector", "tensor_scalar", s_i1[:], s_lb[:], s_rst[:, rb4 + 1: rb4 + 2], None,
             ALU.is_ge, wait=(f"!c_m2{tt}",))
        P.op("vector", "reduce_max", s_rst[:, rb4 + 2: rb4 + 3], lg, axis=mybir.AxisListType.X,
             negate=True, inc="vs", ev=f"c_pm{tt}")
        P.op("scalar", "activation", s_ex[:], lg, AF.Exp, bias=s_rst[:, rb4 + 2: rb4 + 3],
             wait=(f"c_pm{tt}",), inc="ss", ev=f"c_ex{tt}")
        P.op("vector", "scalar_tensor_tensor", s_ex[:], s_ex[:], 1.0, s_i1[:],
             ALU.bypass, ALU.mult, accum_out=s_rst[:, rb4 + 3: rb4 + 4],
             wait=(f"c_ex{tt}",), inc="vs", ev=f"c_cs{tt}")
        P.op("vector", "reciprocal", s_rst[:, rb4 + 3: rb4 + 4], s_rst[:, rb4 + 3: rb4 + 4],
             wait=(f"!c_cs{tt}",), inc="vs", ev=f"c_rc2{tt}")
        P.op("vector", "tensor_scalar", s_cmb[:, tt * NE:(tt + 1) * NE], s_ex[:],
             s_rst[:, rb4 + 3: rb4 + 4], None, ALU.mult, wait=(f"!c_rc2{tt}",),
             inc="vs", ev=f"c_cb{tt}")
        P.op("vector", "tensor_scalar", s_indf[:, tt * NE:(tt + 1) * NE],
             s_cmb[:, tt * NE:(tt + 1) * NE], 0.0, None, ALU.is_gt,
             wait=(f"!c_cb{tt}",))
        P.op("vector", "tensor_copy", s_indb[:, tt * NE:(tt + 1) * NE],
             s_indf[:, tt * NE:(tt + 1) * NE], inc="vs", ev=f"c_ind{tt}")
        w = [f"c_ind{tt}"]
        if tt == 0:
            w.append("init_v")
        if tt >= 2:
            w.append(f"c_pge{tt-2}")
        for tp in range(tt + 1):
            P.op("tensor", "matmul", p_pg[tt % 2][:],
                 c_um[:] if tp == tt else c_one[:],
                 s_indb[:, tp * NE:(tp + 1) * NE],
                 start=(tp == 0), stop=(tp == tt),
                 wait=tuple(w) if tp == 0 else ())
        P.op("tensor", "sem_inc", inc="ts", ev=f"c_pgm{tt}")
        P.op("vector", "tensor_copy", s_pg[:, tt * NE:(tt + 1) * NE], p_pg[tt % 2][:],
             wait=(f"c_pgm{tt}",), inc="vs", ev=f"c_pge{tt}")

    if debug:
        P.op("gpsimd", "dma_start", out=d_dbg["x2T"], in_=s_x2T[:, 0:D], wait=(f"c_te7_{FC-1}", f"c_te{NT-1}_{FC-1}"), inc="dbg_ld", dma=True)
        P.op("sync", "dma_start", out=d_dbg["cmb"][:, 0:NT * NE], in_=s_cmb[:], wait=(f"c_cb{NT-1}",), inc="dbg_ld", dma=True)
        P.op("sync", "dma_start", out=d_dbg["lgt"][:, 0:NT * NE], in_=s_lgt[:], wait=(f"c_cb{NT-1}",), inc="dbg_ld", dma=True)
    P.barrier("endC")
    pfree(8)
    if stop == "C":
        return _finish_early("endC.vector")

    p_u = [pal(f"p_u{i}", [128, 512], F32) for i in range(2)]
    p_g = [pal(f"p_g{i}", [128, 512], F32) for i in range(2)]
    p_d = [pal(f"p_d{i}", [128, 512], F32) for i in range(2)]

    NPOS = NE + 2
    last_de = {}
    gate0 = ("endC.tensor", "endC.vector", "endC.scalar", "endC.sync",
             f"c_sp{NT-2}", f"c_sp{NT-1}")

    def emit_loads(pos):
        if pos < 2:
            half = pos
            wu_src = d_su[:, half * EDIM:(half + 1) * EDIM]
            wg_src = d_sg[:, half * EDIM:(half + 1) * EDIM]
            wd_src = d_sd[half * EDIM:(half + 1) * EDIM, :]
        else:
            e = pos - 2
            wu_src, wg_src, wd_src = d_wu[e], d_wgf[e], d_wd[e]
        for jh in range(2):
            for gg, src in ((0, wu_src), (1, wg_src)):
                k = pos * 4 + jh * 2 + gg
                sl = k % 3
                w = []
                if k >= 3:
                    ppos, rem = divmod(k - 3, 4)
                    pjh, pgg = divmod(rem, 2)
                    w.append(f"d_{'uh' if pgg == 0 else 'gh'}{ppos}_{pjh}")
                elif k == 0:
                    w += list(gate0)
                P.op("gpsimd", "dma_start",
                     out=s_wring[sl][:].rearrange("p (c j) -> p c j", c=FC),
                     in_=src[:, jh * 512:(jh + 1) * 512].rearrange("(c p) j -> p c j", p=128),
                     wait=tuple(w), inc=f"wring{sl}_ld", dma=True, ev=f"d_wl{k}")
        for dh in range(2):
            w = []
            if pos >= 1:
                w.append(last_de[pos - 1])
            P.op("gpsimd", "dma_start",
                 out=s_wd2[dh][:].rearrange("p (c f) -> p c f", c=4),
                 in_=wd_src[dh * 512:(dh + 1) * 512, :].rearrange("(c p) f -> p c f", p=128),
                 wait=tuple(w), inc=f"wd{dh}_ld", dma=True, ev=f"d_wd{pos}_{dh}")
        if pos == 0:
            P.op("gpsimd", "dma_start",
                 out=s_x2tok[:].rearrange("p (t d) -> p t d", t=NT),
                 in_=d_x2.rearrange("(t p) d -> p t d", p=128),
                 inc="x2tok_ld", dma=True, ev="d_x2tok")

    def shared_compute(pos):
        for jh in range(2):
            ku = pos * 4 + jh * 2
            kg = ku + 1
            for st in range(2):
                for jl in range(4):
                    pj = jl % 2
                    idx = (jh * 2 + st) * 4 + jl
                    w = [f"d_wl{ku}"]
                    if idx >= 2:
                        w.append(f"d_si{pos}_{idx-2}")
                    elif pos >= 1:
                        w.append(f"d_si{pos-1}_{14 + idx}")
                    for fc in range(FC):
                        P.op("tensor", "matmul", p_u[pj][:],
                             s_wring[ku % 3][:, fc * 512 + jl * 128: fc * 512 + (jl + 1) * 128],
                             s_x2T[:, fc * TOK + st * 512: fc * TOK + (st + 1) * 512],
                             start=(fc == 0), stop=(fc == FC - 1),
                             wait=tuple(w) if fc == 0 else ())
                    uev = f"d_uh{pos}_{jh}" if (st, jl) == (1, 3) else f"d_u{pos}_{jh}_{st}_{jl}"
                    P.op("tensor", "sem_inc", inc="ts", ev=uev)
                    w = [f"d_wl{kg}"]
                    if idx >= 2:
                        w.append(f"d_h{pos}_{idx-2}")
                    elif pos >= 1:
                        w.append(f"d_h{pos-1}_{14 + idx}")
                    for fc in range(FC):
                        P.op("tensor", "matmul", p_g[pj][:],
                             s_wring[kg % 3][:, fc * 512 + jl * 128: fc * 512 + (jl + 1) * 128],
                             s_x2T[:, fc * TOK + st * 512: fc * TOK + (st + 1) * 512],
                             start=(fc == 0), stop=(fc == FC - 1),
                             wait=tuple(w) if fc == 0 else ())
                    gev = f"d_gh{pos}_{jh}" if (st, jl) == (1, 3) else f"d_g{pos}_{jh}_{st}_{jl}"
                    P.op("tensor", "sem_inc", inc="ts", ev=gev)
                    w = [uev]
                    if idx >= 2:
                        w.append(f"d_h{pos}_{idx-2}")
                    elif pos >= 1:
                        w.append(f"d_h{pos-1}_{14 + idx}")
                    P.op("scalar", "activation", s_su2[pj][:], p_u[pj][:], AF.Silu,
                         wait=tuple(w), inc="ss", ev=f"d_si{pos}_{idx}")
                    jc = jh * 4 + jl
                    P.op("vector", "tensor_mul",
                         s_Hsh[:, jc * TOK + st * 512: jc * TOK + (st + 1) * 512],
                         s_su2[pj][:], p_g[pj][:],
                         wait=(f"d_si{pos}_{idx}", gev),
                         inc="vs", ev=f"d_h{pos}_{idx}")
        for dh in range(2):
            for tsub in range(NT):
                for fch in range(4):
                    idx = tsub * 4 + fch
                    pj = idx % 2
                    w = [f"d_h{pos}_15", f"d_wd{pos}_{dh}"]
                    if idx >= 2:
                        w.append(f"d_dev{pos}_{dh}_{idx-2}")
                    elif dh == 1:
                        w.append(f"d_dev{pos}_0_{30 + idx}")
                    elif pos >= 1:
                        w.append(f"d_dev{pos-1}_1_{30 + idx}")
                    for jl in range(4):
                        jc = dh * 4 + jl
                        P.op("tensor", "matmul", p_d[pj][:],
                             s_Hsh[:, jc * TOK + tsub * 128: jc * TOK + (tsub + 1) * 128],
                             s_wd2[dh][:, jl * D + fch * 512: jl * D + (fch + 1) * 512],
                             start=(jl == 0), stop=(jl == 3),
                             wait=tuple(w) if jl == 0 else ())
                    ev = f"d_de{pos}_{dh}_{idx}"
                    P.op("tensor", "sem_inc", inc="ts", ev=ev)
                    if dh == 1 and idx == 31:
                        last_de[pos] = ev
                    acc = s_acc[:, tsub * D + fch * 512: tsub * D + (fch + 1) * 512]
                    if pos == 0 and dh == 0:
                        P.op("vector", "tensor_scalar", acc, p_d[pj][:], 1.0, None,
                             ALU.mult, wait=(ev,), inc="vs", ev=f"d_dev{pos}_{dh}_{idx}")
                    else:
                        P.op("vector", "scalar_tensor_tensor", acc, p_d[pj][:], 1.0, acc,
                             ALU.mult, ALU.add, wait=(ev,),
                             inc="vs", ev=f"d_dev{pos}_{dh}_{idx}")

    def sel_build(pos):
        e = pos - 2
        for tt in range(NT):
            w = []
            if tt == 0:
                w.append(last_de[1] if pos == 2 else f"d_gmm{pos-1}_{15}")
            P.op("vector", "tensor_scalar", s_sel[:, tt * CAP:(tt + 1) * CAP],
                 c_iot[:], s_pg[:, tt * NE + e: tt * NE + e + 1],
                 s_indf[:, tt * NE + e: tt * NE + e + 1], ALU.is_equal, ALU.mult,
                 wait=tuple(w))
            w = []
            if tt == 0 and pos > 2:
                w.append(f"d_stm{pos-1}_{NT-1}")
            P.op("vector", "tensor_scalar", s_selc8[:, tt * CAP:(tt + 1) * CAP],
                 c_iot[:], s_pg[:, tt * NE + e: tt * NE + e + 1],
                 s_cmb[:, tt * NE + e: tt * NE + e + 1], ALU.is_equal, ALU.mult,
                 wait=tuple(w), inc="vs", ev=f"d_slc{pos}_{tt}")

    def sparse_compute(pos):
        e = pos - 2
        for tt in range(NT):
            for cs in range(2):
                w = [f"d_slc{pos}_{tt}"]
                if tt == 0 and cs == 0:
                    w.append(f"d_dev1_1_31" if pos == 2 else f"d_ste{pos-1}")
                if tt >= 2 and cs == 0:
                    w.append(f"d_stv{pos}_{tt-2}")
                P.op("tensor", "transpose",
                     p_stb[tt % 2][:, cs * 128:(cs + 1) * 128],
                     s_selc8[:, tt * CAP + cs * 128: tt * CAP + (cs + 1) * 128],
                     c_idb[:],
                     wait=tuple(w), inc="ts" if cs == 1 else None,
                     ev=f"d_stm{pos}_{tt}" if cs == 1 else None)
            P.op("scalar", "copy",
                 s_selT[:, tt * 2 * 128: (tt + 1) * 2 * 128],
                 p_stb[tt % 2][:],
                 wait=(f"d_stm{pos}_{tt}",), inc="ss",
                 ev=f"d_ste{pos}" if tt == NT - 1 else f"d_stv{pos}_{tt}")
        for fc in range(FC):
            pgx = p_su if fc % 2 == 0 else p_sg
            w = []
            if fc == 0:
                w += [f"d_slc{pos}_{NT-1}", "d_x2tok"]
                if pos > 2:
                    w += [f"d_gev{pos-1}_{15}", f"d_si{pos-1}_{7}", f"d_h{pos-1}_{7}"]
            if fc >= 2:
                w.append(f"d_gev{pos}_{fc-2}")
            for tt in range(NT):
                P.op("tensor", "matmul", pgx[:],
                     s_x2tok[:, tt * D + fc * 128: tt * D + (fc + 1) * 128],
                     s_sel[:, tt * CAP:(tt + 1) * CAP],
                     start=(tt == 0), stop=(tt == NT - 1),
                     wait=tuple(w) if tt == 0 else ())
            P.op("tensor", "sem_inc", inc="ts", ev=f"d_gmm{pos}_{fc}")
            P.op("scalar", "copy", s_xg[:, fc * CAP:(fc + 1) * CAP], pgx[:],
                 wait=(f"d_gmm{pos}_{fc}",), inc="ss", ev=f"d_gev{pos}_{fc}")
        for jh in range(2):
            ku = pos * 4 + jh * 2
            kg = ku + 1
            for jl in range(4):
                jc = jh * 4 + jl
                pj = jc % 2
                w = [f"d_wl{ku}"]
                w.append(f"d_gev{pos}_{FC-1}" if jc == 0 else f"d_si{pos}_{jc-1}")
                for fc in range(FC):
                    P.op("tensor", "matmul", p_su[:],
                         s_wring[ku % 3][:, fc * 512 + jl * 128: fc * 512 + (jl + 1) * 128],
                         s_xg[:, fc * CAP:(fc + 1) * CAP],
                         start=(fc == 0), stop=(fc == FC - 1),
                         wait=tuple(w) if fc == 0 else ())
                uev = f"d_uh{pos}_{jh}" if jl == 3 else f"d_u{pos}_{jh}_{jl}"
                P.op("tensor", "sem_inc", inc="ts", ev=uev)
                w = [f"d_wl{kg}"]
                w.append(f"d_gev{pos}_{FC-1}" if jc == 0 else f"d_h{pos}_{jc-1}")
                for fc in range(FC):
                    P.op("tensor", "matmul", p_sg[:],
                         s_wring[kg % 3][:, fc * 512 + jl * 128: fc * 512 + (jl + 1) * 128],
                         s_xg[:, fc * CAP:(fc + 1) * CAP],
                         start=(fc == 0), stop=(fc == FC - 1),
                         wait=tuple(w) if fc == 0 else ())
                gev = f"d_gh{pos}_{jh}" if jl == 3 else f"d_g{pos}_{jh}_{jl}"
                P.op("tensor", "sem_inc", inc="ts", ev=gev)
                w = [uev]
                if jc >= 2:
                    w.append(f"d_h{pos}_{jc-2}")
                elif pos > 2:
                    w.append(f"d_h{pos-1}_{6 + jc}")
                P.op("scalar", "activation", s_su2s[pj][:], p_su[:], AF.Silu,
                     wait=tuple(w), inc="ss", ev=f"d_si{pos}_{jc}")
                P.op("vector", "tensor_mul", s_hh[:, jc * CAP:(jc + 1) * CAP],
                     s_su2s[pj][:], p_sg[:],
                     wait=(f"d_si{pos}_{jc}", gev), inc="vs", ev=f"d_h{pos}_{jc}")

        def de_mm(fch, cs):
            idx = fch * 2 + cs
            pj = idx % 2
            w = []
            if idx == 0:
                w += [f"d_h{pos}_{7}", f"d_wd{pos}_0", f"d_wd{pos}_1"]
            if idx >= 2:
                w.append(f"d_dv{pos}_{idx-2}")
            for dh in range(2):
                for jl in range(4):
                    jc = dh * 4 + jl
                    P.op("tensor", "matmul", p_sd[pj][:],
                         s_hh[:, jc * CAP + cs * 128: jc * CAP + cs * 128 + 128],
                         s_wd2[dh][:, jl * D + fch * 512: jl * D + (fch + 1) * 512],
                         start=(dh == 0 and jl == 0), stop=(dh == 1 and jl == 3),
                         wait=tuple(w) if (dh == 0 and jl == 0) else ())
            ev = f"d_dm{pos}_{idx}"
            P.op("tensor", "sem_inc", inc="ts", ev=ev)
            if idx == 7:
                last_de[pos] = ev
            w = [ev]
            if fch >= 2:
                w.append(f"d_scm{pos}_{fch-2}_{NT-1}")
            elif pos > 2:
                w.append(f"d_scm{pos-1}_{fch+2}_{NT-1}")
            P.op("scalar", "copy", s_deg2[fch % 2][:, cs * 512:(cs + 1) * 512],
                 p_sd[pj][:], wait=tuple(w), inc="ss", ev=f"d_dv{pos}_{idx}")

        def sc_mm(fch):
            for tt in range(NT):
                g = fch * NT + tt
                pj = g % 2
                w = []
                if tt == 0:
                    w.append(f"d_dv{pos}_{fch * 2 + 1}")
                    if g == 0:
                        w.append(f"d_ste{pos}")
                        if pos > 2:
                            w.append(f"d_sv{pos-1}_3_{NT-1}")
                if g >= 2:
                    fp, tp = divmod(g - 2, NT)
                    w.append(f"d_sv{pos}_{fp}_{tp}")
                for cs in range(2):
                    P.op("tensor", "matmul", p_ss[pj][:],
                         s_selT[:, tt * 256 + cs * 128: tt * 256 + (cs + 1) * 128],
                         s_deg2[fch % 2][:, cs * 512:(cs + 1) * 512],
                         start=(cs == 0), stop=(cs == 1),
                         wait=tuple(w) if cs == 0 else ())
                P.op("tensor", "sem_inc", inc="ts", ev=f"d_scm{pos}_{fch}_{tt}")
                acc = s_acc[:, tt * D + fch * 512: tt * D + (fch + 1) * 512]
                P.op("vector", "scalar_tensor_tensor", acc, p_ss[pj][:], 1.0, acc,
                     ALU.mult, ALU.add, wait=(f"d_scm{pos}_{fch}_{tt}",),
                     inc="vs", ev=f"d_sv{pos}_{fch}_{tt}")

        if pos + 1 < NPOS:
            sel_build(pos + 1)
        de_mm(0, 0); de_mm(0, 1); de_mm(1, 0); de_mm(1, 1)
        sc_mm(0)
        de_mm(2, 0); de_mm(2, 1)
        sc_mm(1)
        de_mm(3, 0); de_mm(3, 1)
        sc_mm(2)
        sc_mm(3)

    for pos in range(2):
        emit_loads(pos)
        shared_compute(pos)

    pfree(6)
    p_su = pal("p_su", [128, CAP], F32)
    p_sg = pal("p_sg", [128, CAP], F32)
    p_sd = [pal(f"p_sd{i}", [128, 512], F32) for i in range(2)]
    p_ss = [pal(f"p_ss{i}", [128, 512], F32) for i in range(2)]
    p_stb = [pal(f"p_stb{i}", [128, 256], BF16) for i in range(2)]

    sel_build(2)
    for pos in range(2, NPOS):
        emit_loads(pos)
        sparse_compute(pos)

    P.barrier("endD", engines=("sync", "tensor", "vector", "scalar"))
    pfree(8)
    if stop == "D":
        return _finish_early("endD.vector")

    for tt in range(NT):
        bi = tt % 2
        eb = 8 * (tt % 2)
        w = ["endD.vector", "endD.tensor", f"c_sp{tt}"]
        if tt >= 2:
            w.append(f"e_r2{tt-2}")
        P.op("sync", "dma_start", out=s_x2r[bi][:], in_=d_x2[tt * 128:(tt + 1) * 128, :],
             wait=tuple(w), inc=f"x2r{bi}_ld", dma=True, ev=f"e_x2{tt}")
        wv = [f"e_x2{tt}"]
        if tt >= 2:
            wv.append(f"e_ob{tt-2}")
        P.op("vector", "scalar_tensor_tensor", s_r2[bi][:],
             s_acc[:, tt * D:(tt + 1) * D], 1.0, s_x2r[bi][:], ALU.mult, ALU.add,
             accum_out=s_st2[:, eb: eb + 1], wait=tuple(wv), inc="vs", ev=f"e_r2{tt}")
        P.op("vector", "tensor_scalar", s_st2[:, eb + 1: eb + 2], s_st2[:, eb: eb + 1],
             -1.0 / D, None, ALU.mult, wait=(f"!e_r2{tt}",), inc="vs", ev=f"e_nm{tt}")
        P.op("scalar", "activation", s_sq2[:], s_r2[bi][:], AF.Square,
             bias=s_st2[:, eb + 1: eb + 2], accum_out=s_st2[:, eb + 2: eb + 3],
             wait=(f"e_nm{tt}", "endD.vector"), inc="ss", ev=f"e_vsm{tt}")
        P.op("scalar", "activation", s_st2[:, eb + 3: eb + 4], s_st2[:, eb + 2: eb + 3],
             AF.Sqrt, bias=c_eps[:, 0:1], scale=1.0 / D, wait=(f"!e_vsm{tt}",),
             inc="ss", ev=f"e_std{tt}")
        P.op("vector", "reciprocal", s_st2[:, eb + 2: eb + 3], s_st2[:, eb + 3: eb + 4],
             wait=(f"e_std{tt}",), inc="vs", ev=f"e_rc{tt}")
        P.op("vector", "tensor_scalar", s_st2[:, eb + 4: eb + 5], s_st2[:, eb + 1: eb + 2],
             -1.0, None, ALU.mult)
        P.op("vector", "tensor_scalar", s_ob[bi][:], s_r2[bi][:], s_st2[:, eb + 4: eb + 5],
             s_st2[:, eb + 2: eb + 3], ALU.subtract, ALU.mult,
             wait=(f"!e_rc{tt}",) + ((f"e_st{tt-2}",) if tt >= 2 else ()),
             inc="vs", ev=f"e_ob{tt}")
        P.op("sync", "dma_start", out=d_out[tt * 128:(tt + 1) * 128, :], in_=s_ob[bi][:],
             wait=(f"e_ob{tt}",), inc=f"st{bi}_st", dma=True, ev=f"e_st{tt}")

    P.op("sync", "wait_only", wait=(f"e_st{NT-2}", f"e_st{NT-1}"))

    with nc.Block() as block:
        P.emit(block)
    pfree(len(pstack))
    return nc


_NC_CACHE = None


def _get_nc():
    global _NC_CACHE
    if _NC_CACHE is None:
        _NC_CACHE = build_program()
    return _NC_CACHE


def make_in_maps(L0_attn_x, L0_attn_Wq, L0_attn_Wk, L0_attn_Wv, L0_attn_Wg, L0_attn_Wo,
                 L0_ffn_router_w, L0_ffn_router_bias, L0_ffn_Wu, L0_ffn_Wg, L0_ffn_Wd,
                 L0_ffn_Su, L0_ffn_Sg, L0_ffn_Sd):
    x = np.ascontiguousarray(np.asarray(L0_attn_x, dtype=np.float32)).reshape(B * S, D)
    wq = np.ascontiguousarray(np.asarray(L0_attn_Wq, dtype=np.float32))
    wk = np.ascontiguousarray(np.asarray(L0_attn_Wk, dtype=np.float32))
    wv = np.ascontiguousarray(np.asarray(L0_attn_Wv, dtype=np.float32))
    wg = np.ascontiguousarray(np.asarray(L0_attn_Wg, dtype=np.float32))
    wo = np.ascontiguousarray(np.asarray(L0_attn_Wo, dtype=np.float32))
    rw = np.ascontiguousarray(np.asarray(L0_ffn_router_w, dtype=np.float32))
    rb = np.ascontiguousarray(np.asarray(L0_ffn_router_bias, dtype=np.float32))
    wu = np.ascontiguousarray(np.asarray(L0_ffn_Wu, dtype=np.float32))
    wgf = np.ascontiguousarray(np.asarray(L0_ffn_Wg, dtype=np.float32))
    wd = np.ascontiguousarray(np.asarray(L0_ffn_Wd, dtype=np.float32))
    su = np.ascontiguousarray(np.asarray(L0_ffn_Su, dtype=np.float32).reshape(D, 2 * EDIM))
    sg = np.ascontiguousarray(np.asarray(L0_ffn_Sg, dtype=np.float32).reshape(D, 2 * EDIM))
    sd = np.ascontiguousarray(np.asarray(L0_ffn_Sd, dtype=np.float32).reshape(2 * EDIM, D))

    rwt = np.ascontiguousarray(
        np.transpose(rw.reshape(NE, FC, 128), (2, 1, 0)).reshape(128, FC * NE))
    biasb = np.ascontiguousarray(np.broadcast_to(rb, (128, NE)))
    identb = np.eye(128).astype(ml_dtypes.bfloat16)
    utri = np.triu(np.ones((128, 128))).astype(ml_dtypes.bfloat16)
    iotac = np.ascontiguousarray(np.broadcast_to(
        np.arange(1, CAP + 1, dtype=np.float32), (128, CAP))).astype(ml_dtypes.bfloat16)

    in_maps = []
    for core in range(NCORE):
        xs = np.ascontiguousarray(x[core * TOK:(core + 1) * TOK])
        flags = np.zeros((128, 4), dtype=np.float32)
        flags[:, :core % 4] = 1.0
        in_maps.append({
            "x_tok": xs,
            "x_feat": np.ascontiguousarray(xs.T),
            "wq": wq, "wk": wk, "wv": wv, "wgt": wg, "wo": wo,
            "rwt": rwt, "rbias": biasb,
            "wu": wu, "wgf": wgf, "wd": wd,
            "su": su, "sg": sg, "sd": sd,
            "identb": identb, "iotac": iotac, "utri": utri,
            "flags": flags,
            "flags1m": np.ascontiguousarray(1.0 - flags),
        })
    return in_maps


def kernel(**inputs):
    in_maps = make_in_maps(**inputs)
    nc = _get_nc()
    res = run_bass_kernel_spmd(nc, in_maps, core_ids=list(range(NCORE)))
    out = np.concatenate([np.asarray(r["out"]) for r in res.results], axis=0)
    return out.reshape(B, S, D)


# version 09
# speedup vs baseline: 1.4771x; 1.0175x over previous
import numpy as np
import ml_dtypes

import concourse.bass as bass
import concourse.mybir as mybir
from concourse.bass_utils import run_bass_kernel_spmd

F32 = mybir.dt.float32
BF16 = mybir.dt.bfloat16
AF = mybir.ActivationFunctionType
ALU = mybir.AluOpType

B, S, D = 2, 4096, 2048
NCORE = 8
TOK = (B * S) // NCORE
NT = TOK // 128
NH = 16
DK = 128
FC = D // 128
NE = 16
EDIM = 1024
CAP = 256
LN_DKS = float(np.log(DK ** -0.5))
BIG = 1.0e9
EPS = 1e-5

ENGINES = ("sync", "gpsimd", "tensor", "vector", "scalar")
PSEM = {"sync": "ys", "gpsimd": "gs", "tensor": "ts", "vector": "vs", "scalar": "ss"}


_OP_RW = {
    "memset": ((0,), ()),
    "tensor_copy": ((0,), (1,)),
    "tensor_tensor": ((0,), (1, 2)),
    "tensor_add": ((0,), (1, 2)),
    "tensor_sub": ((0,), (1, 2)),
    "tensor_mul": ((0,), (1, 2)),
    "tensor_max": ((0,), (1, 2)),
    "tensor_scalar": ((0,), (1, 2, 3)),
    "tensor_single_scalar": ((0,), (1, 2)),
    "scalar_tensor_tensor": ((0,), (1, 2, 3)),
    "tensor_tensor_scan": ((0,), (1, 2, 3)),
    "reduce_max": ((0,), (1,)),
    "reduce_sum": ((0,), (1,)),
    "tensor_reduce": ((0,), (1,)),
    "reciprocal": ((0,), (1,)),
    "activation": ((0,), (1,)),
    "select": ((0,), (1, 2, 3)),
    "copy_predicated": ((0,), (1, 2)),
    "copy": ((0,), (1,)),
    "mul": ((0,), (1, 2)),
    "add": ((0,), (1, 2)),
}


def _ap_name(x):
    t = getattr(x, "tensor", None)
    return getattr(t, "name", None) if t is not None else None


class Prog:

    AUTO_SEM = {"vector": "var", "scalar": "sar"}

    def __init__(self, nc):
        self.nc = nc
        self.ops = {e: [] for e in ENGINES}
        self.semh = {}
        self.count = {}
        self.events = {}
        self.dma_sems = set()
        self.lastw = {e: {} for e in ENGINES}
        self.lastr = {e: {} for e in ENGINES}
        self.watermark = {e: -1 for e in ENGINES}

    def _sem(self, name):
        if name not in self.semh:
            self.semh[name] = self.nc.alloc_semaphore(name=name)
            self.count[name] = 0
        return self.semh[name]

    def op(self, eng, fn, *args, wait=(), inc=None, ev=None, dma=False, **kwargs):
        waits = [w for w in wait if w is not None]
        rec = [fn, args, kwargs, waits, inc, 16 if dma else 1, None]
        recs = self.ops[eng]
        if eng in ("vector", "scalar") and fn in _OP_RW:
            widx, ridx = _OP_RW[fn]
            reads = [_ap_name(args[i]) for i in ridx if i < len(args)]
            for kw in ("bias", "scale"):
                if kw in kwargs:
                    reads.append(_ap_name(kwargs[kw]))
            writes = [_ap_name(args[i]) for i in widx if i < len(args)]
            if kwargs.get("accum_out") is not None:
                writes.append(_ap_name(kwargs["accum_out"]))
            dep = -1
            for nm in reads:
                if nm is not None:
                    dep = max(dep, self.lastw[eng].get(nm, -1))
            for nm in writes:
                if nm is not None:
                    dep = max(dep, self.lastw[eng].get(nm, -1))
                    dep = max(dep, self.lastr[eng].get(nm, -1))
            if dep > self.watermark[eng]:
                rec[6] = dep
                self.watermark[eng] = dep
            myidx = len(recs)
            for nm in reads:
                if nm is not None:
                    self.lastr[eng][nm] = myidx
            for nm in writes:
                if nm is not None:
                    self.lastw[eng][nm] = myidx
        recs.append(rec)
        if inc is not None:
            self._sem(inc)
            if dma:
                self.dma_sems.add(inc)
            self.count[inc] += 16 if dma else 1
            if ev is not None:
                self.events[ev] = (inc, self.count[inc], eng)

    def barrier(self, name, engines=ENGINES):
        for e in engines:
            self.op(e, "sem_inc", inc=PSEM[e], ev=f"{name}.{e}")
        for e in engines:
            for o in engines:
                if o != e:
                    self.op(e, "wait_only", wait=(f"{name}.{o}",))

    def emit(self, block):
        auto_edge = {}
        for eng in ("vector", "scalar"):
            recs = self.ops[eng]
            need = {}
            for ci, rec in enumerate(recs):
                if rec[6] is not None:
                    need[rec[6]] = True
            counts = dict.fromkeys(self.count, 0)
            post = {}
            asem = self.AUTO_SEM[eng]
            self._sem(asem)
            counts[asem] = 0
            for i, rec in enumerate(recs):
                if i in need and rec[4] is None and rec[0] not in ("sem_inc", "wait_only"):
                    rec[4] = asem
                    rec[5] = 1
                if rec[4] is not None:
                    counts[rec[4]] = counts.get(rec[4], 0) + rec[5]
                post[i] = (rec[4], counts.get(rec[4], 0)) if rec[4] else None
            for ci, rec in enumerate(recs):
                if rec[6] is not None:
                    p = post[rec[6]]
                    assert p is not None, (eng, rec[6], recs[rec[6]][0])
                    auto_edge[(eng, ci)] = p

        for eng in ENGINES:
            recs = self.ops[eng]
            if not recs:
                continue

            def body(handle, recs=recs, eng=eng):
                waited = {}
                last_inst = None
                last_has_update = False
                for idx, (fn, args, kwargs, waits, inc, amt, dep) in enumerate(recs):
                    for evname in waits:
                        force = evname.startswith("!")
                        if force:
                            evname = evname[1:]
                        sem, val, prod = self.events[evname]
                        if prod == eng and sem not in self.dma_sems and not force:
                            continue
                        if waited.get(sem, 0) >= val:
                            continue
                        handle.wait_ge(self.semh[sem], val)
                        waited[sem] = val
                    ae = auto_edge.get((eng, idx))
                    if ae is not None and waited.get(ae[0], 0) < ae[1]:
                        handle.wait_ge(self.semh[ae[0]], ae[1])
                        waited[ae[0]] = ae[1]
                    if fn == "wait_only":
                        continue
                    if fn == "sem_inc":
                        if (eng in ("tensor", "vector", "scalar")
                                and last_inst is not None and not last_has_update):
                            last_inst.then_inc(self.semh[inc], amt)
                            last_has_update = True
                        elif eng in ("tensor", "vector", "scalar"):
                            handle.drain()
                            handle.sem_inc(self.semh[inc], amt)
                        else:
                            handle.sem_inc(self.semh[inc], amt)
                        continue
                    inst = getattr(handle, fn)(*args, **kwargs)
                    last_inst = inst
                    last_has_update = inc is not None
                    if inc is not None:
                        inst.then_inc(self.semh[inc], amt)

            getattr(block, eng)(body)


def build_program(stop=None, debug=False):
    nc = bass.Bass("TRN2", target_bir_lowering=False, detect_race_conditions=True)
    P = Prog(nc)

    d_x = nc.dram_tensor("x_tok", [TOK, D], F32, kind="ExternalInput").ap()
    d_xT = nc.dram_tensor("x_feat", [D, TOK], F32, kind="ExternalInput").ap()
    d_wq = nc.dram_tensor("wq", [D, D], F32, kind="ExternalInput").ap()
    d_wk = nc.dram_tensor("wk", [D, D], F32, kind="ExternalInput").ap()
    d_wv = nc.dram_tensor("wv", [D, D], F32, kind="ExternalInput").ap()
    d_wg = nc.dram_tensor("wgt", [D, D], F32, kind="ExternalInput").ap()
    d_wo = nc.dram_tensor("wo", [D, D], F32, kind="ExternalInput").ap()
    d_rwt = nc.dram_tensor("rwt", [128, FC * NE], F32, kind="ExternalInput").ap()
    d_bias = nc.dram_tensor("rbias", [128, NE], F32, kind="ExternalInput").ap()
    d_wu = nc.dram_tensor("wu", [NE, D, EDIM], F32, kind="ExternalInput").ap()
    d_wgf = nc.dram_tensor("wgf", [NE, D, EDIM], F32, kind="ExternalInput").ap()
    d_wd = nc.dram_tensor("wd", [NE, EDIM, D], F32, kind="ExternalInput").ap()
    d_su = nc.dram_tensor("su", [D, D], F32, kind="ExternalInput").ap()
    d_sg = nc.dram_tensor("sg", [D, D], F32, kind="ExternalInput").ap()
    d_sd = nc.dram_tensor("sd", [D, D], F32, kind="ExternalInput").ap()
    d_idb = nc.dram_tensor("identb", [128, 128], BF16, kind="ExternalInput").ap()
    d_iot = nc.dram_tensor("iotac", [128, CAP], BF16, kind="ExternalInput").ap()
    d_um = nc.dram_tensor("utri", [128, 128], BF16, kind="ExternalInput").ap()
    d_fl = nc.dram_tensor("flags", [128, 4], F32, kind="ExternalInput").ap()
    d_fl1 = nc.dram_tensor("flags1m", [128, 4], F32, kind="ExternalInput").ap()
    d_out = nc.dram_tensor("out", [TOK, D], F32, kind="ExternalOutput").ap()
    if debug:
        d_dbg = {k: nc.dram_tensor(f"dbg_{k}", [128, D], F32, kind="ExternalOutput").ap()
                 for k in ("qt", "kh", "v", "oT", "S", "sin", "x2T", "cmb", "acc", "lgt")}
    d_x2 = nc.dram_tensor("x2_scr", [TOK, D], BF16, kind="Internal").ap()
    d_ccin = nc.dram_tensor("cc_in", [128, NH * 128 + NH], BF16, kind="Internal").ap()
    d_ccout = nc.dram_tensor("cc_out", [4, 128, NH * 128 + NH], BF16, kind="Internal").ap()

    base = (nc.sbuf_base + 31) & ~31
    limit = nc.sbuf_top

    class Arena:
        def __init__(self, start):
            self.off = start
            self.hi = start

        def al(self, name, shape, dt):
            nbytes = int(np.prod(shape[1:])) * mybir.dt.size(dt)
            off = (self.off + 31) & ~31
            h = nc.alloc_sbuf_tensor_at(name, list(shape), dt, offset=base + off)
            self.off = off + nbytes
            self.hi = max(self.hi, self.off)
            assert base + self.off <= limit, (name, self.off, limit - base)
            return h

    CONST0 = 0
    OV1 = 2048
    OV2 = 34816
    SGRP = 100352
    OT0 = 114304
    OV3 = 147072
    ATMP = 179840
    DTMP = 100864

    ar = Arena(CONST0)
    c_idb = ar.al("c_idb", [128, 128], BF16)
    c_iot = ar.al("c_iot", [128, CAP], BF16)
    c_one = ar.al("c_one", [128, 128], BF16)
    c_um = ar.al("c_um", [128, 128], BF16)
    c_fl = ar.al("c_fl", [128, 4], F32)
    c_fl1 = ar.al("c_fl1", [128, 4], F32)
    c_bias = ar.al("c_bias", [128, NE], F32)
    c_lnb = ar.al("c_lnb", [128, 1], F32)
    c_eps = ar.al("c_eps", [128, 1], F32)
    assert ar.hi <= OV1

    ar = Arena(OV1)
    s_qt = ar.al("s_qt", [128, NH * TOK], BF16)
    assert ar.hi <= OV2
    ar = Arena(OV1)
    s_x2T = ar.al("s_x2T", [128, FC * TOK], BF16)
    assert ar.hi <= OV2

    ar = Arena(OV2)
    s_kh = ar.al("s_kh", [128, NH * TOK], BF16)
    s_v = ar.al("s_v", [128, NT * D], BF16)
    assert ar.hi <= SGRP
    ar = Arena(OV2)
    s_wo = ar.al("s_wo", [128, FC * D], BF16)
    assert ar.hi <= SGRP + 512
    ar = Arena(OV2)
    s_acc = ar.al("s_acc", [128, NT * D], BF16)
    s_x2tok = ar.al("s_x2tok", [128, NT * D], BF16)
    assert ar.hi <= SGRP
    ar = Arena(ATMP)
    s_SS = ar.al("s_SS", [128, NH * (NT - 1) * 128], BF16)
    assert base + ar.hi <= limit

    ar = Arena(SGRP)
    s_cmb = ar.al("s_cmb", [128, NT * NE], F32)
    CMB_END = ar.hi
    ar = Arena(SGRP + 512)
    s_Sb = ar.al("s_Sb", [128, NH * 128], BF16)
    s_ea = ar.al("s_ea", [128, NH * NT], F32)
    s_pfr = ar.al("s_pfr", [128, NH], F32)
    s_pfa = ar.al("s_pfa", [128, NH * NT], F32)
    assert ar.hi <= OT0

    ar = Arena(OT0)
    s_oT = ar.al("s_oT", [128, NH * TOK], BF16)
    assert ar.hi <= OV3

    ar = Arena(OV3)
    s_xT = ar.al("s_xT", [128, FC * TOK], BF16)
    assert ar.hi <= ATMP

    ar = Arena(OV3)
    s_t1b = ar.al("s_t1b", [128, 128], F32)
    s_pfb = ar.al("s_pfb", [128, NH], BF16)
    s_sin = ar.al("s_sin", [128, NH * 128], F32)
    s_sinb = ar.al("s_sinb", [128, NH * 128], BF16)
    s_rb = [ar.al(f"s_rb{i}", [128, NH * 128 + NH], BF16) for i in range(2)]
    s_t1 = ar.al("s_t1", [128, 128], F32)
    s_dp = ar.al("s_dp", [128, 1], F32)
    s_ptm = [ar.al(f"s_ptm{i}", [128, 128], BF16) for i in range(2)]
    s_khtm = [ar.al(f"s_khtm{i}", [128, 128], BF16) for i in range(2)]
    s_qp = [ar.al(f"s_qp{i}", [128, 128], BF16) for i in range(2)]
    assert ar.hi <= ATMP

    ar = Arena(OV3)
    s_xr = [ar.al(f"s_xr{i}", [128, D], F32) for i in range(2)]
    s_r = [ar.al(f"s_r{i}", [128, D], F32) for i in range(2)]
    s_x2 = [ar.al(f"s_x2{i}", [128, D], BF16) for i in range(2)]
    s_sq = ar.al("s_sq", [128, D], F32)
    s_st4 = ar.al("s_st4", [128, 16], F32)
    s_rst = ar.al("s_rst", [128, 8], F32)
    s_rwt = ar.al("s_rwt", [128, FC * NE], BF16)
    s_lgt = ar.al("s_lgt", [128, NT * NE], F32)
    s_i1 = ar.al("s_i1", [128, NE], F32)
    s_lb = ar.al("s_lb", [128, NE], F32)
    s_ex = ar.al("s_ex", [128, NE], F32)
    assert base + ar.hi <= limit

    ar = Arena(ATMP)
    s_wr = [ar.al(f"s_wr{i}", [128, FC * 128], BF16) for i in range(2)]
    s_lq = [ar.al(f"s_lq{i}", [128, TOK], BF16) for i in range(2)]
    s_lk = [ar.al(f"s_lk{i}", [128, TOK], BF16) for i in range(2)]
    s_wv2 = [ar.al(f"s_wv2{i}", [128, FC * 256], BF16) for i in range(2)]
    assert base + ar.hi <= limit, ar.hi

    ar = Arena(DTMP)
    s_wring = [ar.al(f"s_wring{i}", [128, FC * 512], BF16) for i in range(3)]
    s_wd2 = [ar.al(f"s_wd2{i}", [128, 4 * D], BF16) for i in range(2)]
    SHOV = ar.hi
    ar = Arena(SHOV)
    s_Hsh = ar.al("s_Hsh", [128, 8 * TOK], BF16)
    SMALLS = ar.hi
    ar = Arena(SHOV)
    s_sel = ar.al("s_sel", [128, NT * CAP], BF16)
    s_selT = ar.al("s_selT", [128, 2 * TOK], BF16)
    s_xg = ar.al("s_xg", [128, FC * CAP], BF16)
    assert ar.hi <= SMALLS
    ar = Arena(SMALLS)
    s_su2 = [ar.al(f"s_su2{i}", [128, 512], F32) for i in range(2)]
    ar = Arena(SMALLS)
    s_su2s = [ar.al(f"s_su2s{i}", [128, CAP], F32) for i in range(2)]
    ar = Arena(SMALLS)
    s_deg2 = [ar.al(f"s_deg2{i}", [128, 2 * 512], BF16) for i in range(2)]
    s_hh = ar.al("s_hh", [128, 8 * CAP], BF16)
    s_selc8 = ar.al("s_selc8", [128, NT * CAP], BF16)
    s_pg = ar.al("s_pg", [128, NT * NE], F32)
    s_indf = ar.al("s_indf", [128, NT * NE], F32)
    s_indb = ar.al("s_indb", [128, NT * NE], BF16)
    assert base + ar.hi <= limit, ar.hi

    ar = Arena(DTMP)
    s_x2r = [ar.al(f"s_x2r{i}", [128, D], BF16) for i in range(2)]
    s_r2 = [ar.al(f"s_r2{i}", [128, D], F32) for i in range(2)]
    s_ob = [ar.al(f"s_ob{i}", [128, D], F32) for i in range(2)]
    s_sq2 = ar.al("s_sq2", [128, D], F32)
    s_st2 = ar.al("s_st2", [128, 16], F32)
    assert base + ar.hi <= limit

    pstack = []

    def pal(name, shape, dt):
        t = nc.psum_tensor(name, shape, dt)
        h = t.__enter__()
        pstack.append(t)
        return h

    def pfree(n):
        for _ in range(n):
            pstack.pop().__exit__(None, None, None)

    def _finish_early(dep_ev):
        P.op("sync", "dma_start", out=d_out[0:128, :], in_=s_sq[:] if False else s_xT[:, 0:D],
             wait=(dep_ev,), inc="out_st", dma=True, ev="early_st")
        P.op("sync", "wait_only", wait=("early_st",))
        with nc.Block() as blk:
            P.emit(blk)
        pfree(len(pstack))
        return nc

    P.op("sync", "dma_start", out=c_idb[:], in_=d_idb, inc="c_ld", dma=True)
    P.op("sync", "dma_start", out=c_iot[:], in_=d_iot, inc="c_ld", dma=True)
    P.op("sync", "dma_start", out=c_um[:], in_=d_um, inc="c_ld", dma=True)
    P.op("sync", "dma_start", out=c_fl[:], in_=d_fl, inc="c_ld", dma=True)
    P.op("sync", "dma_start", out=c_fl1[:], in_=d_fl1, inc="c_ld", dma=True)
    P.op("sync", "dma_start", out=c_bias[:], in_=d_bias, inc="c_ld", dma=True, ev="ld_const")
    P.op("vector", "memset", c_lnb[:], LN_DKS)
    P.op("vector", "memset", c_eps[:], EPS)
    P.op("vector", "memset", c_one[:], 1.0)
    P.op("vector", "memset", s_Sb[:], 0.0)
    P.op("vector", "memset", s_pfr[:], 1.0, inc="vs", ev="init_v")
    P.op("gpsimd", "dma_start",
         out=s_xT[:].rearrange("p (c t) -> p c t", c=FC),
         in_=d_xT.rearrange("(c p) t -> p c t", p=128),
         inc="xT_ld", dma=True, ev="ld_xT")

    p_q = pal("p_q", [128, TOK], F32)
    p_k = pal("p_k", [128, TOK], F32)
    p_z = pal("p_z", [128, TOK], F32)
    p_cu = pal("p_cu", [128, TOK], F32)

    wsrc = {0: d_wg, 1: d_wq, 2: d_wk}
    consume_ev = {0: "a_z", 1: "a_q", 2: "a_k"}

    for h in range(NH):
        for t3 in range(3):
            j = 3 * h + t3
            sl = j % 2
            w = []
            if j >= 2:
                pj = j - 2
                w.append(f"{consume_ev[pj % 3]}{pj // 3}")
            P.op("gpsimd", "dma_start",
                 out=s_wr[sl][:].rearrange("p (c d) -> p c d", c=FC),
                 in_=wsrc[t3][:, h * 128:(h + 1) * 128].rearrange("(c p) d -> p c d", p=128),
                 wait=tuple(w), inc=f"wr{j % 2}_ld", dma=True, ev=f"a_wl{j}")
        for (t3, pp, evn, freev) in (
            (0, p_z, f"a_z{h}", f"a_cu{h-1}"),
            (1, p_q, f"a_q{h}", f"a_qg{h-1}"),
            (2, p_k, f"a_k{h}", f"a_kg{h-1}"),
        ):
            j = 3 * h + t3
            sl = j % 2
            w = [f"a_wl{j}", "ld_xT"]
            if h >= 1:
                w.append(freev)
            for st in range(2):
                for fc in range(FC):
                    P.op("tensor", "matmul",
                         pp[:, st * 512:(st + 1) * 512],
                         s_wr[sl][:, fc * 128:(fc + 1) * 128],
                         s_xT[:, fc * TOK + st * 512: fc * TOK + (st + 1) * 512],
                         start=(fc == 0), stop=(fc == FC - 1),
                         wait=tuple(w) if (st == 0 and fc == 0) else ())
            P.op("tensor", "sem_inc", inc="ts", ev=evn)
        if stop == ("h", h, "mm"):
            return _finish_early(f"a_k{h}")
        P.op("scalar", "activation", p_cu[:], p_z[:], AF.Sigmoid,
             wait=(f"a_z{h}", "init_v") + ((f"a_ea{h-1}",) if h >= 1 else ()))
        P.op("scalar", "activation", p_z[:], p_cu[:], AF.Ln, inc="ss", ev=f"a_ln{h}")
        if stop == ("h", h, "ln"):
            return _finish_early(f"a_ln{h}")
        for c in range(NT):
            csl = slice(c * 128, (c + 1) * 128)
            P.op("vector", "tensor_tensor_scan", p_cu[:, csl], p_z[:, csl],
                 c_um[:], 0.0, ALU.add, ALU.bypass,
                 wait=((f"a_ln{h}",) + (("ld_const",) if h == 0 else ())) if c == 0 else ())
        P.op("vector", "sem_inc", inc="vs", ev=f"a_cu{h}")
        if stop == ("h", h, "cu"):
            return _finish_early(f"a_cu{h}")
        sl = h % 2
        P.op("scalar", "activation", s_lq[sl][:], p_cu[:], AF.Exp,
             bias=c_lnb[:, 0:1], scale=1.0 / 16.0,
             wait=(f"a_cu{h}",) + ((f"a_qg{h-2}",) if h >= 2 else ()),
             inc="ss", ev=f"a_lq{h}")
        P.op("scalar", "activation", s_lk[sl][:], p_cu[:], AF.Exp, scale=-1.0 / 16.0,
             wait=(f"a_kg{h-2}",) if h >= 2 else (),
             inc="ss", ev=f"a_lk{h}")
        for c in range(NT):
            P.op("scalar", "activation", s_ea[:, h * NT + c: h * NT + c + 1],
                 p_cu[:, (c + 1) * 128 - 1:(c + 1) * 128], AF.Exp, scale=1.0 / 16.0)
        P.op("scalar", "sem_inc", inc="ss", ev=f"a_ea{h}")
        if stop == ("h", h, "ea"):
            return _finish_early(f"a_ea{h}")
        P.op("vector", "tensor_mul", s_qt[:, h * TOK:(h + 1) * TOK], p_q[:], s_lq[sl][:],
             wait=(f"a_lq{h}", f"a_q{h}"), inc="vs", ev=f"a_qg{h}")
        P.op("vector", "tensor_mul", s_kh[:, h * TOK:(h + 1) * TOK], p_k[:], s_lk[sl][:],
             wait=(f"a_lk{h}", f"a_k{h}"), inc="vs", ev=f"a_kg{h}")
        if stop == ("h", h):
            return _finish_early(f"a_kg{h}")

    pfree(2)
    p_v = [pal(f"p_v{i}", [128, 256], F32) for i in range(2)]
    for dc in range(8):
        sl = dc % 2
        w = [f"a_cu{NH-1}", f"a_ea{NH-1}"]
        if dc >= 2:
            w.append(f"a_v{dc-2}_{NT-1}")
        P.op("gpsimd", "dma_start",
             out=s_wv2[sl][:].rearrange("p (c d) -> p c d", c=FC),
             in_=d_wv[:, dc * 256:(dc + 1) * 256].rearrange("(c p) d -> p c d", p=128),
             wait=tuple(w), inc=f"wv{sl}_ld", dma=True, ev=f"a_wvl{dc}")
        for tt in range(NT):
            pi = tt % 2
            j = dc * NT + tt
            w = [f"a_wvl{dc}"]
            if j >= 2:
                w.append(f"a_ve{(j-2)//NT}_{(j-2)%NT}")
            for fc in range(FC):
                P.op("tensor", "matmul", p_v[pi][:],
                     s_xT[:, fc * TOK + tt * 128: fc * TOK + (tt + 1) * 128],
                     s_wv2[sl][:, fc * 256:(fc + 1) * 256],
                     start=(fc == 0), stop=(fc == FC - 1),
                     wait=tuple(w) if fc == 0 else ())
            P.op("tensor", "sem_inc", inc="ts", ev=f"a_v{dc}_{tt}")
            P.op("vector", "tensor_copy",
                 s_v[:, tt * D + dc * 256: tt * D + (dc + 1) * 256],
                 p_v[pi][:], wait=(f"a_v{dc}_{tt}",), inc="vs", ev=f"a_ve{dc}_{tt}")

    if debug:
        P.op("gpsimd", "dma_start", out=d_dbg["qt"], in_=s_qt[:, 0:D], wait=("a_qg1",), inc="dbg_ld", dma=True)
        P.op("gpsimd", "dma_start", out=d_dbg["kh"], in_=s_kh[:, 0:D], wait=("a_kg1",), inc="dbg_ld", dma=True)
        P.op("gpsimd", "dma_start", out=d_dbg["v"], in_=s_v[:, 0:D], wait=(f"a_ve7_{NT-1}",), inc="dbg_ld", dma=True)
    P.barrier("endA")
    pfree(4)
    if stop == "A":
        return _finish_early("endA.vector")

    p_pt = [pal(f"p_pt{i}", [128, 128], F32) for i in range(2)]
    p_tr = [pal(f"p_tr{i}", [128, 128], BF16) for i in range(2)]
    p_ot = [pal(f"p_ot{i}", [128, 128], F32) for i in range(2)]
    p_kv = [pal(f"p_kv{i}", [128, 128], F32) for i in range(2)]

    for c in range(NT):
        for h in range(NH):
            i = c * NH + h
            p = i % 2
            ksl = s_kh[:, h * TOK + c * 128: h * TOK + (c + 1) * 128]
            vsl = s_v[:, c * D + h * 128: c * D + (h + 1) * 128]
            Sbsl = s_Sb[:, h * 128:(h + 1) * 128]
            ea = s_ea[:, h * NT + c: h * NT + c + 1]
            w = ([f"b_khtm{i-2}"] if i >= 2 else ["endA.vector", "ld_const"])
            P.op("tensor", "transpose", p_tr[p][:], ksl, c_idb[:],
                 wait=tuple(w), inc="ts", ev=f"b_tr{i}")
            w = [f"b_khtm{i}"]
            if i >= 2:
                w.append(f"b_st{i-2}")
            P.op("tensor", "matmul", p_kv[p][:], s_khtm[p][:], vsl, start=True, stop=True,
                 wait=tuple(w), inc="ts", ev=f"b_kv{i}")
            P.op("scalar", "copy", s_khtm[p][:], p_tr[p][:],
                 wait=(f"b_tr{i}",), inc="ss", ev=f"b_khtm{i}")
            P.op("vector", "tensor_add", s_t1b[:], Sbsl, p_kv[p][:], wait=(f"b_kv{i}",))
            P.op("scalar", "copy", s_pfa[:, h * NT + c: h * NT + c + 1],
                 s_pfr[:, h:h + 1])
            P.op("scalar", "mul", s_pfr[:, h:h + 1], s_pfr[:, h:h + 1], ea,
                 inc="ss", ev=f"b_pf{i}")
            P.op("vector", "tensor_scalar", Sbsl, s_t1b[:], ea, None, ALU.mult,
                 inc="vs", ev=f"b_st{i}")
            if c < NT - 1:
                P.op("vector", "tensor_copy",
                     s_SS[:, (h * (NT - 1) + c) * 128:(h * (NT - 1) + c + 1) * 128],
                     Sbsl, inc="vs", ev=f"b_sv{i}")

    if stop == "B":
        return _finish_early(f"b_st{NT * NH - 1}")
    P.op("vector", "tensor_copy", s_pfb[:], s_pfr[:], wait=(f"b_pf{NT * NH - 1}",),
         inc="vs", ev="x_pfb")
    P.op("sync", "dma_start", out=d_ccin[:, :NH * 128], in_=s_Sb[:],
         wait=(f"b_st{NT * NH - 1}",), inc="ccin_ld", dma=True)
    P.op("sync", "dma_start", out=d_ccin[:, NH * 128:], in_=s_pfb[:],
         wait=("x_pfb",), inc="ccin_ld", dma=True, ev="x_ccin")
    P.op("gpsimd", "collective_compute", "AllGather", ALU.bypass,
         ins=[d_ccin], outs=[d_ccout],
         replica_groups=[[0, 1, 2, 3], [4, 5, 6, 7]],
         wait=("x_ccin",), inc="ccs", ev="x_cc")
    P.op("gpsimd", "dma_start",
         out=s_wo[:].rearrange("p (c d) -> p c d", c=FC),
         in_=d_wo.rearrange("(c p) d -> p c d", p=128),
         wait=(f"b2_ot{NT * NH - 1}",),
         inc="wo_ld", dma=True, ev="c_wo_l")
    P.op("gpsimd", "dma_start", out=s_rwt[:], in_=d_rwt, inc="rwt_ld", dma=True,
         ev="c_rw_l")
    P.op("vector", "memset", s_sin[:], 0.0, wait=("endA.tensor",), inc="vs", ev="x_sin0")

    for c in range(NT):
        for h in range(NH):
            i = c * NH + h
            p = i % 2
            qsl = s_qt[:, h * TOK + c * 128: h * TOK + (c + 1) * 128]
            ksl = s_kh[:, h * TOK + c * 128: h * TOK + (c + 1) * 128]
            vsl = s_v[:, c * D + h * 128: c * D + (h + 1) * 128]
            osl = s_oT[:, h * TOK + c * 128: h * TOK + (c + 1) * 128]
            w = [f"b2_ptm{i-2}"] if i >= 2 else []
            P.op("tensor", "matmul", p_pt[p][:], ksl, qsl, start=True, stop=True,
                 wait=tuple(w), inc="ts", ev=f"b2_pt{i}")
            w = [f"b2_ptm{i}"]
            if i >= 2:
                w.append(f"b2_oev{i-2}")
            if c == 0:
                P.op("tensor", "matmul", p_ot[p][:], vsl, s_ptm[p][:],
                     start=True, stop=True, wait=tuple(w), inc="ts", ev=f"b2_ot{i}")
            else:
                P.op("tensor", "matmul", p_ot[p][:], vsl, s_ptm[p][:],
                     start=True, stop=False, wait=tuple(w))
                P.op("tensor", "matmul", p_ot[p][:],
                     s_SS[:, (h * (NT - 1) + c - 1) * 128:(h * (NT - 1) + c) * 128],
                     qsl, start=False, stop=True,
                     wait=(f"b_sv{(c - 1) * NH + h}",), inc="ts", ev=f"b2_ot{i}")
            P.op("vector", "tensor_mul", s_ptm[p][:], p_pt[p][:], c_um[:],
                 wait=(f"b2_pt{i}",), inc="vs", ev=f"b2_ptm{i}")
            P.op("scalar", "copy", osl, p_ot[p][:],
                 wait=(f"b2_ot{i}",), inc="ss", ev=f"b2_oev{i}")

    for r in range(4):
        w = ["x_cc"]
        if r >= 2:
            w.append(f"x_comb{r-2}")
        P.op("sync", "dma_start", out=s_rb[r % 2][:], in_=d_ccout[r],
             wait=tuple(w), inc=f"rb{r % 2}_ld", dma=True, ev=f"x_rb{r}")
        for h in range(NH):
            w = (f"x_rb{r}", "x_sin0", "ld_const") if h == 0 else ()
            sinh = s_sin[:, h * 128:(h + 1) * 128]
            P.op("vector", "tensor_scalar", s_dp[:],
                 s_rb[r % 2][:, NH * 128 + h: NH * 128 + h + 1],
                 c_fl[:, r:r + 1], c_fl1[:, r:r + 1], ALU.mult, ALU.add, wait=w)
            P.op("vector", "tensor_scalar", s_t1[:], sinh, s_dp[:, 0:1], None, ALU.mult)
            P.op("vector", "scalar_tensor_tensor", sinh,
                 s_rb[r % 2][:, h * 128:(h + 1) * 128], c_fl[:, r:r + 1], s_t1[:],
                 ALU.mult, ALU.add)
        P.op("vector", "sem_inc", inc="vs", ev=f"x_comb{r}")
    P.op("vector", "tensor_copy", s_sinb[:], s_sin[:], inc="vs", ev="x_sinb")

    if stop == "X":
        return _finish_early("x_sinb")
    pfree(8)
    p_om = [pal(f"p_om{i}", [128, 128], F32) for i in range(2)]
    j = 0
    for h in range(NH):
        for c in range(NT):
            p = j % 2
            qsl = s_qt[:, h * TOK + c * 128: h * TOK + (c + 1) * 128]
            osl = s_oT[:, h * TOK + c * 128: h * TOK + (c + 1) * 128]
            w = []
            if j == 0:
                w.append("x_sinb")
                w.append(f"b_pf{NT * NH - 1}")
            if j >= 2:
                w.append(f"a3_om{j-2}")
            P.op("vector", "tensor_scalar", s_qp[p][:], qsl,
                 s_pfa[:, h * NT + c: h * NT + c + 1], None, ALU.mult,
                 wait=tuple(w), inc="vs", ev=f"a3_qp{j}")
            w = [f"a3_qp{j}", "x_sinb"]
            if j >= 2:
                w.append(f"a3_oe{j-2}")
            P.op("tensor", "matmul", p_om[p][:], s_sinb[:, h * 128:(h + 1) * 128],
                 s_qp[p][:], start=True, stop=True, wait=tuple(w), inc="ts",
                 ev=f"a3_om{j}")
            P.op("vector", "tensor_add", osl, osl, p_om[p][:],
                 wait=(f"a3_om{j}",) + ((f"b2_oev{NT * NH - 1}",) if j == 0 else ()),
                 inc="vs", ev=f"a3_oe{j}")
            j += 1

    if debug:
        P.op("gpsimd", "dma_start", out=d_dbg["S"][:, 0:NH * 128], in_=s_Sb[:], wait=(f"b_st{NT*NH-1}",), inc="dbg_ld", dma=True)
        P.op("sync", "dma_start", out=d_dbg["sin"][:, 0:NH * 128], in_=s_sin[:], wait=("x_sinb",), inc="dbg_ld", dma=True)
        P.op("gpsimd", "dma_start", out=d_dbg["oT"], in_=s_oT[:, 0:D], wait=(f"a3_oe{NH*NT-1}",), inc="dbg_ld", dma=True)
    P.barrier("endB")
    pfree(2)
    if stop == "A3":
        return _finish_early("endB.vector")

    p_w = [pal(f"p_w{i}", [128, 512], F32) for i in range(2)]
    p_t2 = [pal(f"p_t2{i}", [128, 128], BF16) for i in range(2)]
    p_rt = [pal(f"p_rt{i}", [128, NE], F32) for i in range(2)]
    p_pg = [pal(f"p_pg{i}", [128, NE], F32) for i in range(2)]

    for tt in range(NT):
        bi = tt % 2
        sb4 = 8 * (tt % 2)
        w = ["endB.vector"]
        if tt >= 2:
            w.append(f"c_x2{tt-2}")
        P.op("sync", "dma_start", out=s_xr[bi][:], in_=d_x[tt * 128:(tt + 1) * 128, :],
             wait=tuple(w), inc=f"xr{bi}_ld", dma=True, ev=f"c_x{tt}")
        for fc in range(4):
            pj = fc % 2
            w = ["c_wo_l"]
            if tt * 4 + fc >= 2:
                pidx = tt * 4 + fc - 2
                w.append(f"c_ev{pidx//4}_{pidx%4}")
            for ec in range(NH):
                P.op("tensor", "matmul", p_w[pj][:],
                     s_oT[:, ec * TOK + tt * 128: ec * TOK + (tt + 1) * 128],
                     s_wo[:, ec * D + fc * 512: ec * D + (fc + 1) * 512],
                     start=(ec == 0), stop=(ec == NH - 1),
                     wait=tuple(w) if ec == 0 else ())
            P.op("tensor", "sem_inc", inc="ts", ev=f"c_mm{tt}_{fc}")
            wv = [f"c_mm{tt}_{fc}", f"c_x{tt}"]
            if tt >= 2:
                wv.append(f"c_std{tt-2}")
            P.op("vector", "scalar_tensor_tensor",
                 s_r[bi][:, fc * 512:(fc + 1) * 512], p_w[pj][:], 1.0,
                 s_xr[bi][:, fc * 512:(fc + 1) * 512], ALU.mult, ALU.add,
                 accum_out=s_st4[:, sb4 + fc: sb4 + fc + 1],
                 wait=tuple(wv), inc="vs", ev=f"c_ev{tt}_{fc}")
        P.op("vector", "reduce_sum", s_st4[:, sb4 + 4: sb4 + 5], s_st4[:, sb4: sb4 + 4],
             axis=mybir.AxisListType.X, wait=(f"!c_ev{tt}_3",), inc="vs", ev=f"c_rs{tt}")
        P.op("vector", "tensor_scalar", s_st4[:, sb4 + 5: sb4 + 6],
             s_st4[:, sb4 + 4: sb4 + 5], -1.0 / D, None,
             ALU.mult, wait=(f"!c_rs{tt}",), inc="vs", ev=f"c_nm{tt}")
        P.op("scalar", "activation", s_sq[:], s_r[bi][:], AF.Square,
             bias=s_st4[:, sb4 + 5: sb4 + 6], accum_out=s_st4[:, sb4 + 6: sb4 + 7],
             wait=(f"c_nm{tt}",), inc="ss", ev=f"c_vsm{tt}")
        P.op("scalar", "activation", s_st4[:, sb4 + 7: sb4 + 8],
             s_st4[:, sb4 + 6: sb4 + 7], AF.Sqrt,
             bias=c_eps[:, 0:1], scale=1.0 / D, wait=(f"!c_vsm{tt}",),
             inc="ss", ev=f"c_std{tt}")
        P.op("vector", "reciprocal", s_st4[:, sb4 + 6: sb4 + 7],
             s_st4[:, sb4 + 7: sb4 + 8], wait=(f"c_std{tt}",), inc="vs", ev=f"c_rc{tt}")
        P.op("vector", "tensor_scalar", s_st4[:, sb4 + 4: sb4 + 5],
             s_st4[:, sb4 + 5: sb4 + 6], -1.0, None, ALU.mult)
        wv = [f"!c_rc{tt}"]
        if tt >= 2:
            wv.append(f"c_sp{tt-2}")
            wv.append(f"c_te{tt-2}_{FC-1}")
        P.op("vector", "tensor_scalar", s_x2[bi][:], s_r[bi][:],
             s_st4[:, sb4 + 4: sb4 + 5],
             s_st4[:, sb4 + 6: sb4 + 7], ALU.subtract, ALU.mult, wait=tuple(wv),
             inc="vs", ev=f"c_x2{tt}")
        P.op("sync", "dma_start", out=d_x2[tt * 128:(tt + 1) * 128, :], in_=s_x2[bi][:],
             wait=(f"c_x2{tt}",), inc=f"sp{bi}_ld", dma=True, ev=f"c_sp{tt}")
        for fc in range(FC):
            pj = fc % 2
            w = [f"c_x2{tt}"]
            if tt == 0 and fc == 0:
                w.append("ld_const")
            if tt * FC + fc >= 2:
                pidx = tt * FC + fc - 2
                w.append(f"c_te{pidx//FC}_{pidx%FC}")
            P.op("tensor", "transpose", p_t2[pj][:],
                 s_x2[bi][:, fc * 128:(fc + 1) * 128], c_idb[:],
                 wait=tuple(w), inc="ts", ev=f"c_tr{tt}_{fc}")
            P.op("vector", "tensor_copy",
                 s_x2T[:, fc * TOK + tt * 128: fc * TOK + (tt + 1) * 128],
                 p_t2[pj][:], wait=(f"c_tr{tt}_{fc}",), inc="vs", ev=f"c_te{tt}_{fc}")

    for tt in range(NT):
        pj = tt % 2
        w = ["c_rw_l", f"c_te{tt}_{FC-1}"]
        if tt >= 2:
            w.append(f"c_lg{tt-2}")
        for fc in range(FC):
            P.op("tensor", "matmul", p_rt[pj][:],
                 s_x2T[:, fc * TOK + tt * 128: fc * TOK + (tt + 1) * 128],
                 s_rwt[:, fc * NE:(fc + 1) * NE],
                 start=(fc == 0), stop=(fc == FC - 1),
                 wait=tuple(w) if fc == 0 else ())
        P.op("tensor", "sem_inc", inc="ts", ev=f"c_rt{tt}")
        lg = s_lgt[:, tt * NE:(tt + 1) * NE]
        P.op("vector", "tensor_copy", lg, p_rt[pj][:], wait=(f"c_rt{tt}",),
             inc="vs", ev=f"c_lg{tt}")
        rb4 = 4 * (tt % 2)
        P.op("vector", "tensor_add", s_lb[:], lg, c_bias[:],
             wait=("ld_const",) if tt == 0 else ())
        P.op("vector", "reduce_max", s_rst[:, rb4: rb4 + 1], s_lb[:],
             axis=mybir.AxisListType.X, inc="vs", ev=f"c_m1{tt}")
        P.op("vector", "tensor_scalar", s_i1[:], s_lb[:], s_rst[:, rb4: rb4 + 1], None,
             ALU.is_ge, wait=(f"!c_m1{tt}",))
        P.op("vector", "scalar_tensor_tensor", s_i1[:], s_i1[:], -BIG, s_lb[:],
             ALU.mult, ALU.add)
        P.op("vector", "reduce_max", s_rst[:, rb4 + 1: rb4 + 2], s_i1[:],
             axis=mybir.AxisListType.X, inc="vs", ev=f"c_m2{tt}")
        P.op("vector", "tensor_scalar", s_i1[:], s_lb[:], s_rst[:, rb4 + 1: rb4 + 2], None,
             ALU.is_ge, wait=(f"!c_m2{tt}",))
        P.op("vector", "reduce_max", s_rst[:, rb4 + 2: rb4 + 3], lg, axis=mybir.AxisListType.X,
             negate=True, inc="vs", ev=f"c_pm{tt}")
        P.op("scalar", "activation", s_ex[:], lg, AF.Exp, bias=s_rst[:, rb4 + 2: rb4 + 3],
             wait=(f"c_pm{tt}",), inc="ss", ev=f"c_ex{tt}")
        P.op("vector", "scalar_tensor_tensor", s_ex[:], s_ex[:], 1.0, s_i1[:],
             ALU.bypass, ALU.mult, accum_out=s_rst[:, rb4 + 3: rb4 + 4],
             wait=(f"c_ex{tt}",), inc="vs", ev=f"c_cs{tt}")
        P.op("vector", "reciprocal", s_rst[:, rb4 + 3: rb4 + 4], s_rst[:, rb4 + 3: rb4 + 4],
             wait=(f"!c_cs{tt}",), inc="vs", ev=f"c_rc2{tt}")
        P.op("vector", "tensor_scalar", s_cmb[:, tt * NE:(tt + 1) * NE], s_ex[:],
             s_rst[:, rb4 + 3: rb4 + 4], None, ALU.mult, wait=(f"!c_rc2{tt}",),
             inc="vs", ev=f"c_cb{tt}")
        P.op("vector", "tensor_scalar", s_indf[:, tt * NE:(tt + 1) * NE],
             s_cmb[:, tt * NE:(tt + 1) * NE], 0.0, None, ALU.is_gt,
             wait=(f"!c_cb{tt}",))
        P.op("vector", "tensor_copy", s_indb[:, tt * NE:(tt + 1) * NE],
             s_indf[:, tt * NE:(tt + 1) * NE], inc="vs", ev=f"c_ind{tt}")
        w = [f"c_ind{tt}"]
        if tt == 0:
            w.append("init_v")
        if tt >= 2:
            w.append(f"c_pge{tt-2}")
        for tp in range(tt + 1):
            P.op("tensor", "matmul", p_pg[tt % 2][:],
                 c_um[:] if tp == tt else c_one[:],
                 s_indb[:, tp * NE:(tp + 1) * NE],
                 start=(tp == 0), stop=(tp == tt),
                 wait=tuple(w) if tp == 0 else ())
        P.op("tensor", "sem_inc", inc="ts", ev=f"c_pgm{tt}")
        P.op("vector", "tensor_copy", s_pg[:, tt * NE:(tt + 1) * NE], p_pg[tt % 2][:],
             wait=(f"c_pgm{tt}",), inc="vs", ev=f"c_pge{tt}")

    if debug:
        P.op("gpsimd", "dma_start", out=d_dbg["x2T"], in_=s_x2T[:, 0:D], wait=(f"c_te7_{FC-1}", f"c_te{NT-1}_{FC-1}"), inc="dbg_ld", dma=True)
        P.op("sync", "dma_start", out=d_dbg["cmb"][:, 0:NT * NE], in_=s_cmb[:], wait=(f"c_cb{NT-1}",), inc="dbg_ld", dma=True)
        P.op("sync", "dma_start", out=d_dbg["lgt"][:, 0:NT * NE], in_=s_lgt[:], wait=(f"c_cb{NT-1}",), inc="dbg_ld", dma=True)
    P.barrier("endC")
    pfree(8)
    if stop == "C":
        return _finish_early("endC.vector")

    p_u = [pal(f"p_u{i}", [128, 512], F32) for i in range(2)]
    p_g = [pal(f"p_g{i}", [128, 512], F32) for i in range(2)]
    p_d = [pal(f"p_d{i}", [128, 512], F32) for i in range(2)]

    NPOS = NE + 2
    last_de = {}
    gate0 = ("endC.tensor", "endC.vector", "endC.scalar", "endC.sync",
             f"c_sp{NT-2}", f"c_sp{NT-1}")

    def emit_loads(pos):
        if pos < 2:
            half = pos
            wu_src = d_su[:, half * EDIM:(half + 1) * EDIM]
            wg_src = d_sg[:, half * EDIM:(half + 1) * EDIM]
            wd_src = d_sd[half * EDIM:(half + 1) * EDIM, :]
        else:
            e = pos - 2
            wu_src, wg_src, wd_src = d_wu[e], d_wgf[e], d_wd[e]
        for jh in range(2):
            for gg, src in ((0, wu_src), (1, wg_src)):
                k = pos * 4 + jh * 2 + gg
                sl = k % 3
                w = []
                if k >= 3:
                    ppos, rem = divmod(k - 3, 4)
                    pjh, pgg = divmod(rem, 2)
                    w.append(f"d_{'uh' if pgg == 0 else 'gh'}{ppos}_{pjh}")
                elif k == 0:
                    w += list(gate0)
                P.op("gpsimd", "dma_start",
                     out=s_wring[sl][:].rearrange("p (c j) -> p c j", c=FC),
                     in_=src[:, jh * 512:(jh + 1) * 512].rearrange("(c p) j -> p c j", p=128),
                     wait=tuple(w), inc=f"wring{sl}_ld", dma=True, ev=f"d_wl{k}")
        for dh in range(2):
            w = []
            if pos >= 1:
                w.append(last_de[pos - 1])
            P.op("gpsimd", "dma_start",
                 out=s_wd2[dh][:].rearrange("p (c f) -> p c f", c=4),
                 in_=wd_src[dh * 512:(dh + 1) * 512, :].rearrange("(c p) f -> p c f", p=128),
                 wait=tuple(w), inc=f"wd{dh}_ld", dma=True, ev=f"d_wd{pos}_{dh}")
        if pos == 0:
            P.op("gpsimd", "dma_start",
                 out=s_x2tok[:].rearrange("p (t d) -> p t d", t=NT),
                 in_=d_x2.rearrange("(t p) d -> p t d", p=128),
                 inc="x2tok_ld", dma=True, ev="d_x2tok")

    def shared_compute(pos):
        for jh in range(2):
            ku = pos * 4 + jh * 2
            kg = ku + 1
            for st in range(2):
                for jl in range(4):
                    pj = jl % 2
                    idx = (jh * 2 + st) * 4 + jl
                    w = [f"d_wl{ku}"]
                    if idx >= 2:
                        w.append(f"d_si{pos}_{idx-2}")
                    elif pos >= 1:
                        w.append(f"d_si{pos-1}_{14 + idx}")
                    for fc in range(FC):
                        P.op("tensor", "matmul", p_u[pj][:],
                             s_wring[ku % 3][:, fc * 512 + jl * 128: fc * 512 + (jl + 1) * 128],
                             s_x2T[:, fc * TOK + st * 512: fc * TOK + (st + 1) * 512],
                             start=(fc == 0), stop=(fc == FC - 1),
                             wait=tuple(w) if fc == 0 else ())
                    uev = f"d_uh{pos}_{jh}" if (st, jl) == (1, 3) else f"d_u{pos}_{jh}_{st}_{jl}"
                    P.op("tensor", "sem_inc", inc="ts", ev=uev)
                    w = [f"d_wl{kg}"]
                    if idx >= 2:
                        w.append(f"d_h{pos}_{idx-2}")
                    elif pos >= 1:
                        w.append(f"d_h{pos-1}_{14 + idx}")
                    for fc in range(FC):
                        P.op("tensor", "matmul", p_g[pj][:],
                             s_wring[kg % 3][:, fc * 512 + jl * 128: fc * 512 + (jl + 1) * 128],
                             s_x2T[:, fc * TOK + st * 512: fc * TOK + (st + 1) * 512],
                             start=(fc == 0), stop=(fc == FC - 1),
                             wait=tuple(w) if fc == 0 else ())
                    gev = f"d_gh{pos}_{jh}" if (st, jl) == (1, 3) else f"d_g{pos}_{jh}_{st}_{jl}"
                    P.op("tensor", "sem_inc", inc="ts", ev=gev)
                    w = [uev]
                    if idx >= 2:
                        w.append(f"d_h{pos}_{idx-2}")
                    elif pos >= 1:
                        w.append(f"d_h{pos-1}_{14 + idx}")
                    P.op("scalar", "activation", s_su2[pj][:], p_u[pj][:], AF.Silu,
                         wait=tuple(w), inc="ss", ev=f"d_si{pos}_{idx}")
                    jc = jh * 4 + jl
                    P.op("vector", "tensor_mul",
                         s_Hsh[:, jc * TOK + st * 512: jc * TOK + (st + 1) * 512],
                         s_su2[pj][:], p_g[pj][:],
                         wait=(f"d_si{pos}_{idx}", gev),
                         inc="vs", ev=f"d_h{pos}_{idx}")
        for dh in range(2):
            for tsub in range(NT):
                for fch in range(4):
                    idx = tsub * 4 + fch
                    pj = idx % 2
                    w = [f"d_h{pos}_15", f"d_wd{pos}_{dh}"]
                    if idx >= 2:
                        w.append(f"d_dev{pos}_{dh}_{idx-2}")
                    elif dh == 1:
                        w.append(f"d_dev{pos}_0_{30 + idx}")
                    elif pos >= 1:
                        w.append(f"d_dev{pos-1}_1_{30 + idx}")
                    for jl in range(4):
                        jc = dh * 4 + jl
                        P.op("tensor", "matmul", p_d[pj][:],
                             s_Hsh[:, jc * TOK + tsub * 128: jc * TOK + (tsub + 1) * 128],
                             s_wd2[dh][:, jl * D + fch * 512: jl * D + (fch + 1) * 512],
                             start=(jl == 0), stop=(jl == 3),
                             wait=tuple(w) if jl == 0 else ())
                    ev = f"d_de{pos}_{dh}_{idx}"
                    P.op("tensor", "sem_inc", inc="ts", ev=ev)
                    if dh == 1 and idx == 31:
                        last_de[pos] = ev
                    acc = s_acc[:, tsub * D + fch * 512: tsub * D + (fch + 1) * 512]
                    if pos == 0 and dh == 0:
                        P.op("vector", "tensor_scalar", acc, p_d[pj][:], 1.0, None,
                             ALU.mult, wait=(ev,), inc="vs", ev=f"d_dev{pos}_{dh}_{idx}")
                    else:
                        P.op("vector", "scalar_tensor_tensor", acc, p_d[pj][:], 1.0, acc,
                             ALU.mult, ALU.add, wait=(ev,),
                             inc="vs", ev=f"d_dev{pos}_{dh}_{idx}")

    def sel_build(pos):
        e = pos - 2
        for tt in range(NT):
            w = []
            if tt == 0:
                w.append(last_de[1] if pos == 2 else f"d_gmm{pos-1}_{15}")
            P.op("vector", "tensor_scalar", s_sel[:, tt * CAP:(tt + 1) * CAP],
                 c_iot[:], s_pg[:, tt * NE + e: tt * NE + e + 1],
                 s_indf[:, tt * NE + e: tt * NE + e + 1], ALU.is_equal, ALU.mult,
                 wait=tuple(w))
            w = []
            if tt == 0 and pos > 2:
                w.append(f"d_stm{pos-1}_{NT-1}")
            P.op("vector", "tensor_scalar", s_selc8[:, tt * CAP:(tt + 1) * CAP],
                 c_iot[:], s_pg[:, tt * NE + e: tt * NE + e + 1],
                 s_cmb[:, tt * NE + e: tt * NE + e + 1], ALU.is_equal, ALU.mult,
                 wait=tuple(w), inc="vs", ev=f"d_slc{pos}_{tt}")

    def sparse_compute(pos):
        e = pos - 2
        for tt in range(NT):
            for cs in range(2):
                w = [f"d_slc{pos}_{tt}"]
                if tt == 0 and cs == 0:
                    w.append(f"d_dev1_1_31" if pos == 2 else f"d_ste{pos-1}")
                if tt >= 2 and cs == 0:
                    w.append(f"d_stv{pos}_{tt-2}")
                P.op("tensor", "transpose",
                     p_stb[tt % 2][:, cs * 128:(cs + 1) * 128],
                     s_selc8[:, tt * CAP + cs * 128: tt * CAP + (cs + 1) * 128],
                     c_idb[:],
                     wait=tuple(w), inc="ts" if cs == 1 else None,
                     ev=f"d_stm{pos}_{tt}" if cs == 1 else None)
            P.op("scalar", "copy",
                 s_selT[:, tt * 2 * 128: (tt + 1) * 2 * 128],
                 p_stb[tt % 2][:],
                 wait=(f"d_stm{pos}_{tt}",), inc="ss",
                 ev=f"d_ste{pos}" if tt == NT - 1 else f"d_stv{pos}_{tt}")
        for fc in range(FC):
            pgx = p_su if fc % 2 == 0 else p_sg
            w = []
            if fc == 0:
                w += [f"d_slc{pos}_{NT-1}", "d_x2tok"]
                if pos > 2:
                    w += [f"d_gev{pos-1}_{15}", f"d_si{pos-1}_{7}", f"d_h{pos-1}_{7}"]
            if fc >= 2:
                w.append(f"d_gev{pos}_{fc-2}")
            for tt in range(NT):
                P.op("tensor", "matmul", pgx[:],
                     s_x2tok[:, tt * D + fc * 128: tt * D + (fc + 1) * 128],
                     s_sel[:, tt * CAP:(tt + 1) * CAP],
                     start=(tt == 0), stop=(tt == NT - 1),
                     wait=tuple(w) if tt == 0 else ())
            P.op("tensor", "sem_inc", inc="ts", ev=f"d_gmm{pos}_{fc}")
            P.op("scalar", "copy", s_xg[:, fc * CAP:(fc + 1) * CAP], pgx[:],
                 wait=(f"d_gmm{pos}_{fc}",), inc="ss", ev=f"d_gev{pos}_{fc}")
        for jh in range(2):
            ku = pos * 4 + jh * 2
            kg = ku + 1
            for jl in range(4):
                jc = jh * 4 + jl
                pj = jc % 2
                w = [f"d_wl{ku}"]
                w.append(f"d_gev{pos}_{FC-1}" if jc == 0 else f"d_si{pos}_{jc-1}")
                for fc in range(FC):
                    P.op("tensor", "matmul", p_su[:],
                         s_wring[ku % 3][:, fc * 512 + jl * 128: fc * 512 + (jl + 1) * 128],
                         s_xg[:, fc * CAP:(fc + 1) * CAP],
                         start=(fc == 0), stop=(fc == FC - 1),
                         wait=tuple(w) if fc == 0 else ())
                uev = f"d_uh{pos}_{jh}" if jl == 3 else f"d_u{pos}_{jh}_{jl}"
                P.op("tensor", "sem_inc", inc="ts", ev=uev)
                w = [f"d_wl{kg}"]
                w.append(f"d_gev{pos}_{FC-1}" if jc == 0 else f"d_h{pos}_{jc-1}")
                for fc in range(FC):
                    P.op("tensor", "matmul", p_sg[:],
                         s_wring[kg % 3][:, fc * 512 + jl * 128: fc * 512 + (jl + 1) * 128],
                         s_xg[:, fc * CAP:(fc + 1) * CAP],
                         start=(fc == 0), stop=(fc == FC - 1),
                         wait=tuple(w) if fc == 0 else ())
                gev = f"d_gh{pos}_{jh}" if jl == 3 else f"d_g{pos}_{jh}_{jl}"
                P.op("tensor", "sem_inc", inc="ts", ev=gev)
                w = [uev]
                if jc >= 2:
                    w.append(f"d_h{pos}_{jc-2}")
                elif pos > 2:
                    w.append(f"d_h{pos-1}_{6 + jc}")
                P.op("scalar", "activation", s_su2s[pj][:], p_su[:], AF.Silu,
                     wait=tuple(w), inc="ss", ev=f"d_si{pos}_{jc}")
                P.op("vector", "tensor_mul", s_hh[:, jc * CAP:(jc + 1) * CAP],
                     s_su2s[pj][:], p_sg[:],
                     wait=(f"d_si{pos}_{jc}", gev), inc="vs", ev=f"d_h{pos}_{jc}")

        def de_mm(fch, cs):
            idx = fch * 2 + cs
            pj = idx % 2
            w = []
            if idx == 0:
                w += [f"d_h{pos}_{7}", f"d_wd{pos}_0", f"d_wd{pos}_1"]
            if idx >= 2:
                w.append(f"d_dv{pos}_{idx-2}")
            for dh in range(2):
                for jl in range(4):
                    jc = dh * 4 + jl
                    P.op("tensor", "matmul", p_sd[pj][:],
                         s_hh[:, jc * CAP + cs * 128: jc * CAP + cs * 128 + 128],
                         s_wd2[dh][:, jl * D + fch * 512: jl * D + (fch + 1) * 512],
                         start=(dh == 0 and jl == 0), stop=(dh == 1 and jl == 3),
                         wait=tuple(w) if (dh == 0 and jl == 0) else ())
            ev = f"d_dm{pos}_{idx}"
            P.op("tensor", "sem_inc", inc="ts", ev=ev)
            if idx == 7:
                last_de[pos] = ev
            w = [ev]
            if fch >= 2:
                w.append(f"d_scm{pos}_{fch-2}_{NT-1}")
            elif pos > 2:
                w.append(f"d_scm{pos-1}_{fch+2}_{NT-1}")
            P.op("scalar", "copy", s_deg2[fch % 2][:, cs * 512:(cs + 1) * 512],
                 p_sd[pj][:], wait=tuple(w), inc="ss", ev=f"d_dv{pos}_{idx}")

        def sc_mm(fch):
            for tt in range(NT):
                g = fch * NT + tt
                pj = g % 2
                w = []
                if tt == 0:
                    w.append(f"d_dv{pos}_{fch * 2 + 1}")
                    if g == 0:
                        w.append(f"d_ste{pos}")
                        if pos > 2:
                            w.append(f"d_sv{pos-1}_3_{NT-1}")
                if g >= 2:
                    fp, tp = divmod(g - 2, NT)
                    w.append(f"d_sv{pos}_{fp}_{tp}")
                for cs in range(2):
                    P.op("tensor", "matmul", p_ss[pj][:],
                         s_selT[:, tt * 256 + cs * 128: tt * 256 + (cs + 1) * 128],
                         s_deg2[fch % 2][:, cs * 512:(cs + 1) * 512],
                         start=(cs == 0), stop=(cs == 1),
                         wait=tuple(w) if cs == 0 else ())
                P.op("tensor", "sem_inc", inc="ts", ev=f"d_scm{pos}_{fch}_{tt}")
                acc = s_acc[:, tt * D + fch * 512: tt * D + (fch + 1) * 512]
                P.op("vector", "scalar_tensor_tensor", acc, p_ss[pj][:], 1.0, acc,
                     ALU.mult, ALU.add, wait=(f"d_scm{pos}_{fch}_{tt}",),
                     inc="vs", ev=f"d_sv{pos}_{fch}_{tt}")

        if pos + 1 < NPOS:
            sel_build(pos + 1)
        de_mm(0, 0); de_mm(0, 1); de_mm(1, 0); de_mm(1, 1)
        sc_mm(0)
        de_mm(2, 0); de_mm(2, 1)
        sc_mm(1)
        de_mm(3, 0); de_mm(3, 1)
        sc_mm(2)
        sc_mm(3)

    for pos in range(2):
        emit_loads(pos)
        shared_compute(pos)

    pfree(6)
    p_su = pal("p_su", [128, CAP], F32)
    p_sg = pal("p_sg", [128, CAP], F32)
    p_sd = [pal(f"p_sd{i}", [128, 512], F32) for i in range(2)]
    p_ss = [pal(f"p_ss{i}", [128, 512], F32) for i in range(2)]
    p_stb = [pal(f"p_stb{i}", [128, 256], BF16) for i in range(2)]

    sel_build(2)
    for pos in range(2, NPOS):
        emit_loads(pos)
        sparse_compute(pos)

    P.barrier("endD", engines=("sync", "tensor", "vector", "scalar"))
    pfree(8)
    if stop == "D":
        return _finish_early("endD.vector")

    for tt in range(NT):
        bi = tt % 2
        eb = 8 * (tt % 2)
        w = ["endD.vector", "endD.tensor", f"c_sp{tt}"]
        if tt >= 2:
            w.append(f"e_r2{tt-2}")
        P.op("sync", "dma_start", out=s_x2r[bi][:], in_=d_x2[tt * 128:(tt + 1) * 128, :],
             wait=tuple(w), inc=f"x2r{bi}_ld", dma=True, ev=f"e_x2{tt}")
        wv = [f"e_x2{tt}"]
        if tt >= 2:
            wv.append(f"e_ob{tt-2}")
        P.op("vector", "scalar_tensor_tensor", s_r2[bi][:],
             s_acc[:, tt * D:(tt + 1) * D], 1.0, s_x2r[bi][:], ALU.mult, ALU.add,
             accum_out=s_st2[:, eb: eb + 1], wait=tuple(wv), inc="vs", ev=f"e_r2{tt}")
        P.op("vector", "tensor_scalar", s_st2[:, eb + 1: eb + 2], s_st2[:, eb: eb + 1],
             -1.0 / D, None, ALU.mult, wait=(f"!e_r2{tt}",), inc="vs", ev=f"e_nm{tt}")
        P.op("scalar", "activation", s_sq2[:], s_r2[bi][:], AF.Square,
             bias=s_st2[:, eb + 1: eb + 2], accum_out=s_st2[:, eb + 2: eb + 3],
             wait=(f"e_nm{tt}", "endD.vector"), inc="ss", ev=f"e_vsm{tt}")
        P.op("scalar", "activation", s_st2[:, eb + 3: eb + 4], s_st2[:, eb + 2: eb + 3],
             AF.Sqrt, bias=c_eps[:, 0:1], scale=1.0 / D, wait=(f"!e_vsm{tt}",),
             inc="ss", ev=f"e_std{tt}")
        P.op("vector", "reciprocal", s_st2[:, eb + 2: eb + 3], s_st2[:, eb + 3: eb + 4],
             wait=(f"e_std{tt}",), inc="vs", ev=f"e_rc{tt}")
        P.op("vector", "tensor_scalar", s_st2[:, eb + 4: eb + 5], s_st2[:, eb + 1: eb + 2],
             -1.0, None, ALU.mult)
        P.op("vector", "tensor_scalar", s_ob[bi][:], s_r2[bi][:], s_st2[:, eb + 4: eb + 5],
             s_st2[:, eb + 2: eb + 3], ALU.subtract, ALU.mult,
             wait=(f"!e_rc{tt}",) + ((f"e_st{tt-2}",) if tt >= 2 else ()),
             inc="vs", ev=f"e_ob{tt}")
        P.op("sync", "dma_start", out=d_out[tt * 128:(tt + 1) * 128, :], in_=s_ob[bi][:],
             wait=(f"e_ob{tt}",), inc=f"st{bi}_st", dma=True, ev=f"e_st{tt}")

    P.op("sync", "wait_only", wait=(f"e_st{NT-2}", f"e_st{NT-1}"))

    with nc.Block() as block:
        P.emit(block)
    pfree(len(pstack))
    return nc


_NC_CACHE = None


def _get_nc():
    global _NC_CACHE
    if _NC_CACHE is None:
        _NC_CACHE = build_program()
    return _NC_CACHE


def make_in_maps(L0_attn_x, L0_attn_Wq, L0_attn_Wk, L0_attn_Wv, L0_attn_Wg, L0_attn_Wo,
                 L0_ffn_router_w, L0_ffn_router_bias, L0_ffn_Wu, L0_ffn_Wg, L0_ffn_Wd,
                 L0_ffn_Su, L0_ffn_Sg, L0_ffn_Sd):
    x = np.ascontiguousarray(np.asarray(L0_attn_x, dtype=np.float32)).reshape(B * S, D)
    wq = np.ascontiguousarray(np.asarray(L0_attn_Wq, dtype=np.float32))
    wk = np.ascontiguousarray(np.asarray(L0_attn_Wk, dtype=np.float32))
    wv = np.ascontiguousarray(np.asarray(L0_attn_Wv, dtype=np.float32))
    wg = np.ascontiguousarray(np.asarray(L0_attn_Wg, dtype=np.float32))
    wo = np.ascontiguousarray(np.asarray(L0_attn_Wo, dtype=np.float32))
    rw = np.ascontiguousarray(np.asarray(L0_ffn_router_w, dtype=np.float32))
    rb = np.ascontiguousarray(np.asarray(L0_ffn_router_bias, dtype=np.float32))
    wu = np.ascontiguousarray(np.asarray(L0_ffn_Wu, dtype=np.float32))
    wgf = np.ascontiguousarray(np.asarray(L0_ffn_Wg, dtype=np.float32))
    wd = np.ascontiguousarray(np.asarray(L0_ffn_Wd, dtype=np.float32))
    su = np.ascontiguousarray(np.asarray(L0_ffn_Su, dtype=np.float32).reshape(D, 2 * EDIM))
    sg = np.ascontiguousarray(np.asarray(L0_ffn_Sg, dtype=np.float32).reshape(D, 2 * EDIM))
    sd = np.ascontiguousarray(np.asarray(L0_ffn_Sd, dtype=np.float32).reshape(2 * EDIM, D))

    rwt = np.ascontiguousarray(
        np.transpose(rw.reshape(NE, FC, 128), (2, 1, 0)).reshape(128, FC * NE))
    biasb = np.ascontiguousarray(np.broadcast_to(rb, (128, NE)))
    identb = np.eye(128).astype(ml_dtypes.bfloat16)
    utri = np.triu(np.ones((128, 128))).astype(ml_dtypes.bfloat16)
    iotac = np.ascontiguousarray(np.broadcast_to(
        np.arange(1, CAP + 1, dtype=np.float32), (128, CAP))).astype(ml_dtypes.bfloat16)

    in_maps = []
    for core in range(NCORE):
        xs = np.ascontiguousarray(x[core * TOK:(core + 1) * TOK])
        flags = np.zeros((128, 4), dtype=np.float32)
        flags[:, :core % 4] = 1.0
        in_maps.append({
            "x_tok": xs,
            "x_feat": np.ascontiguousarray(xs.T),
            "wq": wq, "wk": wk, "wv": wv, "wgt": wg, "wo": wo,
            "rwt": rwt, "rbias": biasb,
            "wu": wu, "wgf": wgf, "wd": wd,
            "su": su, "sg": sg, "sd": sd,
            "identb": identb, "iotac": iotac, "utri": utri,
            "flags": flags,
            "flags1m": np.ascontiguousarray(1.0 - flags),
        })
    return in_maps


def kernel(**inputs):
    in_maps = make_in_maps(**inputs)
    nc = _get_nc()
    res = run_bass_kernel_spmd(nc, in_maps, core_ids=list(range(NCORE)))
    out = np.concatenate([np.asarray(r["out"]) for r in res.results], axis=0)
    return out.reshape(B, S, D)
